# version 13
# baseline (speedup 1.0000x reference)
import math
from contextlib import ExitStack

import numpy as np
import ml_dtypes

import concourse.bass as bass
import concourse.mybir as mybir
from concourse.bass_utils import run_bass_kernel_spmd

F32 = mybir.dt.float32
BF16 = mybir.dt.bfloat16
I32 = mybir.dt.int32
AF = mybir.ActivationFunctionType
ALU = mybir.AluOpType
BF = ml_dtypes.bfloat16
AX = mybir.AxisListType.X

NCORES = 8
T = 8192
DM = 4096
NKV = 8
NQH = 32
DH = 128
BLK = 256
TPC = 1024
SG = [16, 32, 48, 64]
CSHIFT = -24.0
MINV = [0, 16, 32, 48]
CTRL_COLS = [(g, j) for g in range(4) for j in range(2, SG[g]) if j - 2 >= MINV[g]]
NCTRL = len(CTRL_COLS)
CTRL_IX = {gj: c for c, gj in enumerate(CTRL_COLS)}


def core_blocks(i):
    return [i, 15 - i, 16 + i, 31 - i]


def owner_of_block(b):
    if b <= 7:
        return b, 0
    if b <= 15:
        return 15 - b, 1
    if b <= 23:
        return b - 16, 2
    return 31 - b, 3


def slot_cols(g, j):
    if j < 2:
        return 8192 + g * 256 + j * 128
    m = j - 2
    b, sub = m >> 1, m & 1
    r, lb = owner_of_block(b)
    return r * 1024 + lb * 256 + sub * 128


def rope_cos_sin(positions):
    idx = np.arange(0, DH, 2, dtype=np.float64) / DH
    freq = 1.0 / (2.0 * math.pi) * (500000.0 ** (-idx))
    factor, lo, hi = 32.0, 1.0, 4.0
    L0 = 8192
    freq_low = lo / L0
    freq_high = hi / L0
    freq_scaled = np.where(freq < freq_low, freq / factor, freq)
    smooth = np.clip((L0 * freq - lo) / (hi - lo), 0.0, 1.0)
    freq_smooth = (1.0 - smooth) * (freq / factor) + smooth * freq
    is_mid = (freq >= freq_low) & (freq <= freq_high)
    freq = np.where(is_mid, freq_smooth, freq_scaled)
    phase = 2.0 * math.pi * positions.astype(np.float64)[:, None] * freq[None, :]
    emb = np.concatenate([phase, phase], axis=-1)
    return np.cos(emb).T.astype(np.float32), np.sin(emb).T.astype(np.float32)


DIAG_BATCHES = [(g, 0) for g in range(4)]
CTRL_BATCHES = []
for g in range(4):
    for bi in range(1, SG[g] // 4):
        if any((4 * bi + idx) - 2 >= MINV[g] for idx in range(4)):
            CTRL_BATCHES.append((g, bi))
CTRLB_IX = {gb: c for c, gb in enumerate(CTRL_BATCHES)}
NCTRLB = len(CTRL_BATCHES)


def slot_keep(blocks, g, j):
    if j < 2:
        return 1.0
    return 1.0 if (j - 2) < 2 * blocks[g] else 0.0


def host_tables(core):
    blocks = core_blocks(core)
    pos = np.concatenate([np.arange(b * BLK, (b + 1) * BLK) for b in blocks])
    cos, sin = rope_cos_sin(pos)
    s128 = np.float32(1.0 / math.sqrt(DH))
    ctrlb = np.ones((128, 4 * NCTRLB), np.float32)
    for (g, bi), c in CTRLB_IX.items():
        for idx in range(4):
            ctrlb[:, 4 * c + idx] = slot_keep(blocks, g, 4 * bi + idx)
    p = np.arange(128)[:, None]
    f = np.arange(256)[None, :]
    st0 = (f >= p).astype(np.float32)
    st1 = (f >= 128 + p).astype(np.float32)
    dmask = np.ones((128, 4 * 1024), np.float32)
    for g in range(4):
        dmask[:, g * 1024:g * 1024 + 256] = st0
        dmask[:, g * 1024 + 256:g * 1024 + 512] = st1
        dmask[:, g * 1024 + 512:g * 1024 + 768] = slot_keep(blocks, g, 2)
        dmask[:, g * 1024 + 768:g * 1024 + 1024] = slot_keep(blocks, g, 3)
    return dict(cosq=np.ascontiguousarray(cos * s128),
                sinq=np.ascontiguousarray(sin * s128),
                cosk=cos, sink=sin, ctrlb=ctrlb,
                dmask=dmask.astype(BF)), pos


def host_consts():
    cst = np.zeros((128, 8), np.float32)
    cst[:, 0] = CSHIFT
    cst[:, 1] = 1.0
    cst[:, 2] = -0.5
    cst[:, 3] = 1e-5
    cst[:, 4] = 1.0 / DM
    cst[:, 5] = 0.0
    cst[:, 6] = -1.0
    rot = np.zeros((128, 128), np.float32)
    rot[np.arange(64) + 64, np.arange(64)] = -1.0
    rot[np.arange(64), np.arange(64) + 64] = 1.0
    idm = np.eye(128, dtype=np.float32)
    return cst, rot.astype(BF), idm


DBG = False
DBG2 = False
PHASE_LIM = 99


def build_nc():
    nc = bass.Bass("TRN2", target_bir_lowering=False)

    x_in = nc.dram_tensor("x", [TPC, DM], F32, kind="ExternalInput").ap()
    wq = nc.dram_tensor("wq", [DM, DM], F32, kind="ExternalInput").ap()
    wk = nc.dram_tensor("wk", [DM, NKV * DH], F32, kind="ExternalInput").ap()
    wv = nc.dram_tensor("wv", [DM, NKV * DH], F32, kind="ExternalInput").ap()
    wo = nc.dram_tensor("wo", [DM, DM], F32, kind="ExternalInput").ap()
    qwt = nc.dram_tensor("qwt", [DM, DM], I32, kind="ExternalInput").ap()
    bias_bc = nc.dram_tensor("bias_bc", [128, DM], F32, kind="ExternalInput").ap()
    sw_bc = nc.dram_tensor("sw_bc", [128, 1], F32, kind="ExternalInput").ap()
    cosq = nc.dram_tensor("cosq", [128, TPC], F32, kind="ExternalInput").ap()
    sinq = nc.dram_tensor("sinq", [128, TPC], F32, kind="ExternalInput").ap()
    cosk = nc.dram_tensor("cosk", [128, TPC], F32, kind="ExternalInput").ap()
    sink = nc.dram_tensor("sink", [128, TPC], F32, kind="ExternalInput").ap()
    cst_in = nc.dram_tensor("cst", [128, 8], F32, kind="ExternalInput").ap()
    ctrlb_in = nc.dram_tensor("ctrlb", [128, 4 * NCTRLB], F32, kind="ExternalInput").ap()
    dmask_in = nc.dram_tensor("dmask", [128, 4 * 1024], BF16, kind="ExternalInput").ap()
    rot_in = nc.dram_tensor("rot", [128, 128], BF16, kind="ExternalInput").ap()
    idm_in = nc.dram_tensor("idm", [128, 128], F32, kind="ExternalInput").ap()
    out = nc.dram_tensor("out", [TPC, DM], F32, kind="ExternalOutput").ap()

    cc_in = nc.dram_tensor("cc_in", [2 * TPC, 1024], BF16)
    cc_out = nc.dram_tensor("cc_out", [NCORES * 2 * TPC, 1024], BF16)
    q_scr = nc.dram_tensor("q_scr", [NQH * DH, TPC], BF16)
    xn_scr = nc.dram_tensor("xn_scr", [128, 32 * TPC], BF16)
    s_scr = nc.dram_tensor("s_scr", [TPC, DM], BF16)
    dbg = {}
    if DBG:
        dbg["xn"] = nc.dram_tensor("dbg_xn", [128, 32 * TPC], BF16, kind="ExternalOutput").ap()
        dbg["cc"] = nc.dram_tensor("dbg_cc", [NCORES * 2 * TPC, 1024], BF16, kind="ExternalOutput").ap()
        dbg["q"] = nc.dram_tensor("dbg_q", [NQH * DH, TPC], BF16, kind="ExternalOutput").ap()
        dbg["ctx"] = nc.dram_tensor("dbg_ctx", [128, 32 * TPC], BF16, kind="ExternalOutput").ap()
        dbg["s"] = nc.dram_tensor("dbg_s", [TPC, DM], BF16, kind="ExternalOutput").ap()
        if DBG2:
            dbg["xnf"] = nc.dram_tensor("dbg_xnf", [128, DM], F32, kind="ExternalOutput").ap()
            dbg["stat"] = nc.dram_tensor("dbg_stat", [128, 8], F32, kind="ExternalOutput").ap()
            dbg["x2"] = nc.dram_tensor("dbg_x2", [128, DM], F32, kind="ExternalOutput").ap()

    BASE = 16384 + 1024
    A, Bb, C, D, E, F = (BASE, BASE + 65536, BASE + 131072, BASE + 167936,
                         BASE + 184320, BASE + 200704)

    def sb(name, shape, dtype, off):
        return nc.alloc_sbuf_tensor_at(name, shape, dtype, offset=off)

    xn_sb = sb("xn_sb", [128, 32 * TPC], BF16, A)
    K1_sb = sb("K1_sb", [128, 9216], BF16, A)
    V1_sb = sb("V1_sb", [128, 9216], BF16, A + 18432)
    l2_sb = sb("l2_sb", [128, TPC], F32, A + 36864)
    rl_sb = sb("rl_sb", [128, TPC], F32, A + 40960)

    ctx_sb = sb("ctx_sb", [128, 32 * TPC], BF16, Bb)
    cosq_sb = sb("cosq_sb", [128, TPC], F32, Bb)
    sinq_sb = sb("sinq_sb", [128, TPC], F32, Bb + 4096)
    cosk_sb = sb("cosk_sb", [128, TPC], F32, Bb + 8192)
    sink_sb = sb("sink_sb", [128, TPC], F32, Bb + 12288)
    kp_sb = sb("kp_sb", [128, 2 * 512], BF16, Bb + 16384)
    t1_sb = sb("t1_sb", [128, 512], F32, Bb + 18432)
    t2_sb = sb("t2_sb", [128, 512], F32, Bb + 20480)
    kp2_sb = sb("kp2_sb", [128, 2 * 512], BF16, Bb + 22528)
    bias_sb = sb("bias_sb", [128, DM], F32, Bb)
    t1f_sb = sb("t1f_sb", [128, 2 * 512], F32, Bb + 16384)
    gt_sb = sb("gt_sb", [128, 2 * 512], BF16, Bb + 20480)
    stile_sb = sb("stile_sb", [128, 2 * 512], BF16, Bb + 22528)
    xtile_sb = sb("xtile_sb", [128, 2 * 512], F32, Bb + 24576)
    otile_sb = sb("otile_sb", [128, 2 * 512], F32, Bb + 28672)

    xt_sb = sb("xt_sb", [128, 2 * DM], F32, C)
    vslab_sb = sb("vslab_sb", [128, 16384], BF16, C)
    K0_sb = sb("K0_sb", [128, 9216], BF16, C)
    V0_sb = sb("V0_sb", [128, 9216], BF16, C + 18432)
    slab0_sb = sb("slab0_sb", [128, 16384], BF16, C)

    x2_sb = sb("x2_sb", [128, DM], F32, D)
    qslabD_sb = sb("qslabD_sb", [128, 8192], BF16, D)
    Q0_sb = sb("Q0_sb", [128, 4096], BF16, D)
    Q1_sb = sb("Q1_sb", [128, 4096], BF16, D + 8192)
    slab1_sb = sb("slab1_sb", [128, 16384], BF16, D)

    xnf_sb = sb("xnf_sb", [128, DM], F32, E)
    wslab_sb = sb("wslab_sb", [128, 2 * 4096], BF16, E)

    cst_sb = sb("cst_sb", [128, 8], F32, F)
    ctrlb_sb = sb("ctrlb_sb", [128, 4 * NCTRLB], F32, F + 32)
    rot_sb = sb("rot_sb", [128, 128], BF16, F + 288)
    idm_sb = sb("idm_sb", [128, 128], F32, F + 544)
    ones_sb = sb("ones_sb", [128, 128], BF16, F + 1056)
    onesr_sb = sb("onesr_sb", [1, 128], F32, F + 1312)
    swb_sb = sb("swb_sb", [128, 1], F32, F + 1824)
    pt_sb = sb("pt_sb", [128, 2 * 1024], BF16, F + 1856)
    sbf_sb = sb("sbf_sb", [128, 2 * 512], BF16, F + 5952)
    stat_sb = sb("stat_sb", [128, 8], F32, F + 8000)
    dmask_sb = sb("dmask_sb", [128, 4 * 1024], BF16, A + 49152)

    St0 = nc.alloc_psum_tensor("St0", [128, 1024], F32)
    St1 = nc.alloc_psum_tensor("St1", [128, 1024], F32)
    ctxA = nc.alloc_psum_tensor("ctxA", [128, 512], F32)
    ctxB = nc.alloc_psum_tensor("ctxB", [128, 512], F32)
    lA = nc.alloc_psum_tensor("lA", [128, 512], F32)
    lB = nc.alloc_psum_tensor("lB", [128, 512], F32)
    StP = [St0, St1]
    KVs = [(K0_sb, V0_sb, Q0_sb), (K1_sb, V1_sb, Q1_sb)]

    cAP = {"shift": cst_sb[:, 0:1], "one": cst_sb[:, 1:2], "mhalf": cst_sb[:, 2:3],
           "eps": cst_sb[:, 3:4], "rdm": cst_sb[:, 4:5], "zero": cst_sb[:, 5:6],
           "mone": cst_sb[:, 6:7]}

    ENGS = ("tensor", "scalar", "vector", "gpsimd", "sync")
    ops = {e: [] for e in ENGS}
    stack = ExitStack()
    sems = {}
    cnt = {}
    for name in ("pre", "da", "db", "dc", "ga", "gb", "gc", "gd", "ge", "gf",
                 "gg", "gh", "sa", "sb", "sc", "sd",
                 "cc", "pe", "act", "dve", "gp", "bar"):
        sems[name] = stack.enter_context(nc.semaphore(name))
        cnt[name] = 0

    AUTO = {"scalar": "act", "vector": "dve", "gpsimd": "gp"}

    def emit(eng, fn, dma=False):
        if eng in AUTO and not dma:
            nm = AUTO[eng]
            idx = cnt[nm]
            sem = sems[nm]

            def wrapped(e, fn=fn, idx=idx, sem=sem):
                if idx > 0:
                    e.wait_ge(sem, idx)
                fn(e).then_inc(sem, 1)
            ops[eng].append(wrapped)
            cnt[nm] += 1
            return cnt[nm]
        ops[eng].append(fn)
        return None

    def winc(name, n=1):
        cnt[name] += n
        return cnt[name]

    def wait(eng, name, val):
        if val and val > 0:
            emit(eng, lambda e, s=sems[name], v=val: e.wait_ge(s, v), dma=True)

    def barrier(engines=ENGS):
        for nm in ("pre", "da", "db", "dc", "sa", "sb", "sc", "sd"):
            wait("sync", nm, cnt[nm])
        for nm in ("ga", "gb"):
            wait("gpsimd", nm, cnt[nm])
        wait("scalar", "act", cnt["act"])
        wait("vector", "dve", cnt["dve"])
        wait("gpsimd", "gp", cnt["gp"])
        wait("tensor", "pe", cnt["pe"])
        prev = cnt["bar"]
        for en in engines:
            emit(en, lambda e, s=sems["bar"], p=prev: e.wait_ge(s, p), dma=True)
            emit(en, lambda e, s=sems["bar"]: e.sem_inc(s, 1), dma=True)
        tgt = winc("bar", len(engines))
        for en in engines:
            wait(en, "bar", tgt)

    for dst, src in ((cst_sb, cst_in), (ctrlb_sb, ctrlb_in),
                     (rot_sb, rot_in), (idm_sb, idm_in), (swb_sb, sw_bc),
                     (cosq_sb, cosq), (sinq_sb, sinq), (cosk_sb, cosk), (sink_sb, sink)):
        emit("sync", lambda e, d=dst, s=src: e.dma_start(out=d[:, :], in_=s[:, :])
             .then_inc(sems["pre"], 16))
        winc("pre", 16)
    emit("gpsimd", lambda e: e.memset(ones_sb[:, :], 1.0))
    g_ones = emit("gpsimd", lambda e: e.memset(onesr_sb[:, :], 1.0))
    n_pre = cnt["pre"]
    for en in ENGS:
        wait(en, "pre", n_pre)
        wait(en, "gp", g_ones)


    if PHASE_LIM >= 1:
        d_xnf = [None] * 8
        d_evac = []
        pe_tr = {}

        def emit_stats(tb):
            par = tb % 2
            xt = xt_sb[:, par * DM:(par + 1) * DM]
            lane = "da" if par == 0 else "db"
            xl = xload[tb]
            wait("vector", lane, xl)
            wait("scalar", lane, xl)
            emit("vector", lambda e, xt=xt: e.reduce_sum(stat_sb[:, 0:1], xt, axis=AX))
            a_sq = emit("scalar", lambda e, xt=xt: e.square(x2_sb[:, :], xt))
            yield
            wait("vector", "act", a_sq)
            emit("vector", lambda e: e.reduce_sum(stat_sb[:, 1:2], x2_sb[:, :], axis=AX))
            yield
            emit("vector", lambda e: e.tensor_scalar_mul(stat_sb[:, 2:3], stat_sb[:, 0:1], cAP["rdm"]))
            emit("vector", lambda e: e.tensor_scalar_mul(stat_sb[:, 3:4], stat_sb[:, 1:2], cAP["rdm"]))
            emit("vector", lambda e: e.tensor_mul(stat_sb[:, 4:5], stat_sb[:, 2:3], stat_sb[:, 2:3]))
            d_var = emit("vector", lambda e: e.tensor_sub(stat_sb[:, 5:6], stat_sb[:, 3:4], stat_sb[:, 4:5]))
            wait("scalar", "dve", d_var)
            emit("scalar", lambda e: e.activation(stat_sb[:, 6:7], stat_sb[:, 5:6], AF.Ln,
                                                  bias=cAP["eps"], scale=cAP["one"]))
            a_rstd = emit("scalar", lambda e: e.activation(stat_sb[:, 7:8], stat_sb[:, 6:7], AF.Exp,
                                                           bias=cAP["zero"], scale=cAP["mhalf"]))
            yield
            wait("vector", "act", a_rstd)
            if tb >= 1:
                wait("vector", "pe", pe_tr[(tb - 1, 7)])
            d_xnf[tb] = emit("vector", lambda e, xt=xt: e.tensor_scalar(
                xnf_sb[:, :], xt, stat_sb[:, 2:3], stat_sb[:, 7:8], ALU.subtract, ALU.mult))

        def emit_tr_batch(tb, b4):
            Sp = StP[b4 % 2]
            if b4 == 0:
                wait("tensor", "dve", d_xnf[tb])
            gi = tb * 8 + b4
            if gi >= 2:
                wait("tensor", "dve", d_evac[gi - 2])
            for j in range(4):
                kc = b4 * 4 + j
                last = (j == 3)
                emit("tensor", lambda e, kc=kc, Sp=Sp, j=j, last=last: (
                    e.transpose(Sp[:, j * 128:(j + 1) * 128],
                                xnf_sb[:, kc * 128:(kc + 1) * 128], idm_sb[:, :])
                    .then_inc(sems["pe"], 1) if last else
                    e.transpose(Sp[:, j * 128:(j + 1) * 128],
                                xnf_sb[:, kc * 128:(kc + 1) * 128], idm_sb[:, :])))
            pe_tr[(tb, b4)] = winc("pe")
            wait("vector", "pe", pe_tr[(tb, b4)])
            d_evac.append(emit("vector", lambda e, tb=tb, b4=b4, Sp=Sp: e.tensor_copy(
                xn_sb.ap().rearrange("p (c n) -> p c n", n=TPC)
                     [:, 4 * b4:4 * b4 + 4, tb * 128:(tb + 1) * 128],
                Sp[:, 0:512].rearrange("p (c n) -> p c n", n=128))))

        xload = [None] * 8

        def load_x(tb):
            par = tb % 2
            lane = "da" if par == 0 else "db"
            if tb >= 2:
                wait("sync", "dve", d_xnf[tb - 2])
            emit("sync", lambda e, tb=tb, par=par, lane=lane: e.dma_start(
                out=xt_sb[:, par * DM:(par + 1) * DM],
                in_=x_in[tb * 128:(tb + 1) * 128, :]).then_inc(sems[lane], 16))
            xload[tb] = winc(lane, 16)

        load_x(0)
        load_x(1)
        gen = emit_stats(0)
        for _ in gen:
            pass
        for tb in range(8):
            if tb + 1 < 8:
                load_x(tb + 1)
                gen = emit_stats(tb + 1)
            else:
                gen = iter(())
            emit_tr_batch(tb, 0)
            next(gen, None)
            emit_tr_batch(tb, 1)
            emit_tr_batch(tb, 2)
            next(gen, None)
            emit_tr_batch(tb, 3)
            emit_tr_batch(tb, 4)
            next(gen, None)
            emit_tr_batch(tb, 5)
            emit_tr_batch(tb, 6)
            emit_tr_batch(tb, 7)
            for _ in gen:
                pass
        barrier()
        if DBG:
            emit("sync", lambda e: e.dma_start(out=dbg["xn"][:, :], in_=xn_sb[:, :])
                 .then_inc(sems["pre"], 16))
            winc("pre", 16)
            if "xnf" in dbg and PHASE_LIM == 1:
                emit("sync", lambda e: e.dma_start(out=dbg["xnf"][:, :], in_=xnf_sb[:, :])
                     .then_inc(sems["pre"], 16))
                winc("pre", 16)
                emit("sync", lambda e: e.dma_start(out=dbg["stat"][:, :], in_=stat_sb[:, :])
                     .then_inc(sems["pre"], 16))
                winc("pre", 16)
                emit("sync", lambda e: e.dma_start(out=dbg["x2"][:, :], in_=x2_sb[:, :])
                     .then_inc(sems["pre"], 16))
                winc("pre", 16)

    kvload = [None] * 8
    av_last_kv = [None] * 8
    kv_issued = set()
    prefetch = {}

    if PHASE_LIM >= 2:
        gld_k = [None] * 8
        pe_mm = {}
        pe_rot = {}
        a_evac = {}
        d_t1 = {}
        d_t2 = {}
        d_kp2 = {}
        st_k = {}
        pe_last_of_dk = {}
        for dk in range(8):
            par = dk % 2
            if dk >= 2:
                wait("gpsimd", "pe", pe_last_of_dk[dk - 2])
            glane = "ga" if par == 0 else "gb"
            emit("gpsimd", lambda e, dk=dk, par=par, glane=glane: e.dma_start(
                out=wslab_sb[:, par * 4096:(par + 1) * 4096].rearrange("p (c n) -> p c n", n=128),
                in_=wk.rearrange("(c p) n -> p c n", p=128)[:, :, dk * 128:(dk + 1) * 128])
                .then_inc(sems[glane], 16), dma=True)
            gld_k[dk] = (glane, winc(glane, 16))
            for tb2 in range(2):
                it = dk * 2 + tb2
                half = it % 2
                wait("tensor", *gld_k[dk])
                if it >= 2:
                    wait("tensor", "act", a_evac[it - 2])
                for kc in range(32):
                    emit("tensor", lambda e, kc=kc, half=half, par=par, tb2=tb2: (
                        lambda i: i.then_inc(sems["pe"], 1) if kc == 31 else i)(
                        e.matmul(
                            St0[:, half * 512:(half + 1) * 512],
                            wslab_sb[:, par * 4096 + kc * 128:par * 4096 + (kc + 1) * 128],
                            xn_sb[:, kc * TPC + tb2 * 512:kc * TPC + (tb2 + 1) * 512],
                            start=(kc == 0), stop=(kc == 31))))
                pe_mm[it] = winc("pe")
                wait("scalar", "pe", pe_mm[it])
                if it >= 2:
                    wait("scalar", "pe", pe_rot[it - 2])
                    wait("scalar", "dve", d_t1[it - 2])
                a_evac[it] = emit("scalar", lambda e, half=half: e.copy(
                    kp_sb[:, half * 512:(half + 1) * 512], St0[:, half * 512:(half + 1) * 512]))
                wait("tensor", "act", a_evac[it])
                if it >= 2:
                    wait("tensor", "dve", d_t2[it - 2])
                emit("tensor", lambda e, half=half: e.matmul(
                    St1[:, half * 512:(half + 1) * 512], rot_sb[:, :],
                    kp_sb[:, half * 512:(half + 1) * 512], start=True, stop=True)
                    .then_inc(sems["pe"], 1))
                pe_rot[it] = winc("pe")
                pe_last_of_dk[dk] = pe_rot[it]
                wait("vector", "act", a_evac[it])
                d_t1[it] = emit("vector", lambda e, half=half, tb2=tb2: e.tensor_mul(
                    t1_sb[:, :], kp_sb[:, half * 512:(half + 1) * 512],
                    cosk_sb[:, tb2 * 512:(tb2 + 1) * 512]))
                wait("vector", "pe", pe_rot[it])
                d_t2[it] = emit("vector", lambda e, half=half, tb2=tb2: e.tensor_mul(
                    t2_sb[:, :], St1[:, half * 512:(half + 1) * 512],
                    sink_sb[:, tb2 * 512:(tb2 + 1) * 512]))
                slane = "sa" if half == 0 else "sb"
                if it >= 2:
                    wait("vector", *st_k[it - 2])
                d_kp2[it] = emit("vector", lambda e, half=half: e.tensor_add(
                    kp2_sb[:, half * 512:(half + 1) * 512], t1_sb[:, :], t2_sb[:, :]))
                wait("sync", "dve", d_kp2[it])
                emit("sync", lambda e, dk=dk, tb2=tb2, half=half, slane=slane: e.dma_start(
                    out=cc_in.ap()[dk * 128:(dk + 1) * 128, tb2 * 512:(tb2 + 1) * 512],
                    in_=kp2_sb[:, half * 512:(half + 1) * 512]).then_inc(sems[slane], 16))
                st_k[it] = (slane, winc(slane, 16))

        gld_v = [None] * 2
        pe_vmm = {}
        a_vevac = {}
        st_v = {}
        for ns in range(2):
            if ns >= 1:
                wait("gpsimd", "pe", pe_vmm[7])
            else:
                wait("gpsimd", "pe", pe_last_of_dk[6])
            emit("gpsimd", lambda e, ns=ns: e.dma_start(
                out=vslab_sb.ap().rearrange("p (c n) -> p c n", n=512),
                in_=wv.rearrange("(c p) n -> p c n", p=128)[:, :, ns * 512:(ns + 1) * 512])
                .then_inc(sems["gc"], 16), dma=True)
            gld_v[ns] = winc("gc", 16)
            for tb in range(8):
                itv = ns * 8 + tb
                half = itv % 2
                wait("tensor", "gc", gld_v[ns])
                if itv >= 2:
                    wait("tensor", "act", a_vevac[itv - 2])
                else:
                    wait("tensor", "act", a_evac[14 + itv])
                for kc in range(32):
                    emit("tensor", lambda e, kc=kc, half=half, tb=tb: (
                        lambda i: i.then_inc(sems["pe"], 1) if kc == 31 else i)(
                        e.matmul(
                            St0[:, half * 512:(half + 1) * 512],
                            xn_sb[:, kc * TPC + tb * 128:kc * TPC + tb * 128 + 128],
                            vslab_sb[:, kc * 512:(kc + 1) * 512],
                            start=(kc == 0), stop=(kc == 31))))
                pe_vmm[itv] = winc("pe")
                slane = "sa" if half == 0 else "sb"
                wait("scalar", "pe", pe_vmm[itv])
                wait("scalar", *(st_v[itv - 2] if itv >= 2 else st_k[14 + half]))
                a_vevac[itv] = emit("scalar", lambda e, half=half: e.copy(
                    kp_sb[:, half * 512:(half + 1) * 512], St0[:, half * 512:(half + 1) * 512]))
                wait("sync", "act", a_vevac[itv])
                emit("sync", lambda e, tb=tb, ns=ns, half=half, slane=slane: e.dma_start(
                    out=cc_in.ap()[1024 + tb * 128:1024 + (tb + 1) * 128, ns * 512:(ns + 1) * 512],
                    in_=kp_sb[:, half * 512:(half + 1) * 512]).then_inc(sems[slane], 16))
                st_v[itv] = (slane, winc(slane, 16))
        prefetch["cc_gate"] = (cnt["sa"], cnt["sb"])

    def issue_kv_load(kvh):
        par = kvh % 2
        Ksb, Vsb, Qsb = KVs[par]
        lane = "da" if par == 0 else "db"
        if kvh < 2:
            wait("sync", "pre", cnt["pre"])
            wait("sync", "dc", cnt["dc"])
            wait("sync", "cc", 1)
            wait("sync", "pe", pe_last_of_qh[7])
            wait("sync", *st_q[2 * (4 * (kvh + 1)) - 1])
        else:
            wait("sync", "pe", av_last_kv[kvh - 2])
        for r in range(NCORES):
            emit("sync", lambda e, r=r, kvh=kvh, Ksb=Ksb, lane=lane: e.dma_start(
                out=Ksb[:, r * 1024:(r + 1) * 1024],
                in_=cc_out.ap()[(2048 * r + kvh * 128):(2048 * r + (kvh + 1) * 128), :])
                .then_inc(sems[lane], 16))
            winc(lane, 16)
        emit("sync", lambda e, kvh=kvh, Ksb=Ksb, lane=lane: e.dma_start(
            out=Ksb[:, 8192:9216],
            in_=cc_in.ap()[kvh * 128:(kvh + 1) * 128, :]).then_inc(sems[lane], 16))
        winc(lane, 16)
        for r in range(NCORES):
            emit("sync", lambda e, r=r, kvh=kvh, Vsb=Vsb, lane=lane: e.dma_start(
                out=Vsb[:, r * 1024:(r + 1) * 1024].rearrange("p (c n) -> p c n", n=128),
                in_=cc_out.ap().rearrange("(a p) n -> p a n", p=128)
                    [:, 16 * r + 8:16 * r + 16, kvh * 128:(kvh + 1) * 128])
                .then_inc(sems[lane], 16))
            winc(lane, 16)
        emit("sync", lambda e, kvh=kvh, Vsb=Vsb, lane=lane: e.dma_start(
            out=Vsb[:, 8192:9216].rearrange("p (c n) -> p c n", n=128),
            in_=cc_in.ap().rearrange("(a p) n -> p a n", p=128)
                [:, 8:16, kvh * 128:(kvh + 1) * 128]).then_inc(sems[lane], 16))
        winc(lane, 16)
        emit("sync", lambda e, kvh=kvh, Qsb=Qsb, lane=lane: e.dma_start(
            out=Qsb.ap().rearrange("p (c n) -> p c n", n=TPC),
            in_=q_scr.ap().rearrange("(c p) n -> p c n", p=128)[:, 4 * kvh:4 * kvh + 4, :])
            .then_inc(sems[lane], 16))
        kvload[kvh] = ("da" if par == 0 else "db", winc(lane, 16))
        kv_issued.add(kvh)

    if PHASE_LIM >= 3:
        emit("sync", lambda e: e.dma_start(out=xn_scr.ap()[:, :], in_=xn_sb[:, :])
             .then_inc(sems["dc"], 16))
        winc("dc", 16)
        gld_q = [None] * 32
        pe_qmm = {}
        pe_qrot = {}
        a_qevac = {}
        dq_t1 = {}
        dq_t2 = {}
        dq_kp2 = {}
        st_q = {}
        pe_last_of_qh = {}
        QLANES = ["ga", "gb", "gc", "gd", "ge", "gf", "gg", "gh"]

        def qslab_ap(slot):
            if slot < 2:
                return wslab_sb[:, slot * 4096:(slot + 1) * 4096]
            if slot < 4:
                return qslabD_sb[:, (slot - 2) * 4096:(slot - 1) * 4096]
            return vslab_sb[:, (slot - 4) * 4096:(slot - 3) * 4096]

        for qh in range(32):
            slot = qh % 8
            glane = QLANES[slot]
            if qh >= 8:
                wait("gpsimd", "pe", pe_last_of_qh[qh - 8])
            elif slot < 2:
                wait("gpsimd", "pe", pe_last_of_dk[6 + slot])
            elif slot >= 4 or slot in (2, 3):
                wait("gpsimd", "pe", pe_vmm[15])
            emit("gpsimd", lambda e, qh=qh, slot=slot, glane=glane: e.dma_start(
                out=qslab_ap(slot).rearrange("p (c n) -> p c n", n=128),
                in_=wq.rearrange("(c p) n -> p c n", p=128)[:, :, qh * 128:(qh + 1) * 128])
                .then_inc(sems[glane], 16), dma=True)
            gld_q[qh] = (glane, winc(glane, 16))
            if qh == 7:
                wait("gpsimd", "sa", prefetch["cc_gate"][0])
                wait("gpsimd", "sb", prefetch["cc_gate"][1])
                emit("gpsimd", lambda e: e.collective_compute(
                    "AllGather", mybir.AluOpType.bypass,
                    replica_groups=[list(range(NCORES))],
                    ins=[cc_in.ap()], outs=[cc_out.ap()]).then_inc(sems["cc"]), dma=True)
                cnt["cc"] = 1
            for tb2 in range(2):
                it = qh * 2 + tb2
                half = it % 2
                wait("tensor", *gld_q[qh])
                if it >= 2:
                    wait("tensor", "act", a_qevac[it - 2])
                else:
                    wait("tensor", "act", a_vevac[14 + it])
                slot = qh % 8
                for kc in range(32):
                    emit("tensor", lambda e, kc=kc, half=half, slot=slot, tb2=tb2: (
                        lambda i: i.then_inc(sems["pe"], 1) if kc == 31 else i)(
                        e.matmul(
                            St0[:, half * 512:(half + 1) * 512],
                            qslab_ap(slot)[:, kc * 128:(kc + 1) * 128],
                            xn_sb[:, kc * TPC + tb2 * 512:kc * TPC + (tb2 + 1) * 512],
                            start=(kc == 0), stop=(kc == 31))))
                pe_qmm[it] = winc("pe")
                wait("scalar", "pe", pe_qmm[it])
                if it >= 2:
                    wait("scalar", "pe", pe_qrot[it - 2])
                    wait("scalar", "dve", dq_t1[it - 2])
                else:
                    wait("scalar", *st_v[14 + it])
                a_qevac[it] = emit("scalar", lambda e, half=half: e.copy(
                    kp_sb[:, half * 512:(half + 1) * 512], St0[:, half * 512:(half + 1) * 512]))
                wait("tensor", "act", a_qevac[it])
                if it >= 2:
                    wait("tensor", "dve", dq_t2[it - 2])
                else:
                    wait("tensor", "dve", d_t2[14 + it])
                emit("tensor", lambda e, half=half: e.matmul(
                    St1[:, half * 512:(half + 1) * 512], rot_sb[:, :],
                    kp_sb[:, half * 512:(half + 1) * 512], start=True, stop=True)
                    .then_inc(sems["pe"], 1))
                pe_qrot[it] = winc("pe")
                pe_last_of_qh[qh] = pe_qrot[it]
                wait("vector", "act", a_qevac[it])
                dq_t1[it] = emit("vector", lambda e, half=half, tb2=tb2: e.tensor_mul(
                    t1_sb[:, :], kp_sb[:, half * 512:(half + 1) * 512],
                    cosq_sb[:, tb2 * 512:(tb2 + 1) * 512]))
                wait("vector", "pe", pe_qrot[it])
                dq_t2[it] = emit("vector", lambda e, half=half, tb2=tb2: e.tensor_mul(
                    t2_sb[:, :], St1[:, half * 512:(half + 1) * 512],
                    sinq_sb[:, tb2 * 512:(tb2 + 1) * 512]))
                slane = "sc" if half == 0 else "sd"
                if it >= 2:
                    wait("vector", *st_q[it - 2])
                else:
                    wait("vector", *st_k[14 + it])
                dq_kp2[it] = emit("vector", lambda e, half=half: e.tensor_add(
                    kp2_sb[:, half * 512:(half + 1) * 512], t1_sb[:, :], t2_sb[:, :]))
                wait("sync", "dve", dq_kp2[it])
                emit("sync", lambda e, qh=qh, tb2=tb2, half=half, slane=slane: e.dma_start(
                    out=q_scr.ap()[qh * 128:(qh + 1) * 128, tb2 * 512:(tb2 + 1) * 512],
                    in_=kp2_sb[:, half * 512:(half + 1) * 512]).then_inc(sems[slane], 16))
                st_q[it] = (slane, winc(slane, 16))
        if PHASE_LIM >= 4:
            issue_kv_load(0)
            issue_kv_load(1)
            emit("sync", lambda e: e.dma_start(out=dmask_sb[:, :], in_=dmask_in[:, :])
                 .then_inc(sems["dc"], 16))
            prefetch["dmask"] = winc("dc", 16)
        barrier()
        if DBG:
            emit("sync", lambda e: e.dma_start(out=dbg["q"][:, :], in_=q_scr.ap()[:, :])
                 .then_inc(sems["pre"], 16))
            winc("pre", 16)

    if PHASE_LIM >= 4:
        batches = []
        for g in range(4):
            for bi in range(SG[g] // 4):
                batches.append((g, bi, [(g, 4 * bi + jj) for jj in range(4)]))
        assert len(batches) == 40

        sched = []
        for kvh in range(8):
            for qhl in range(4):
                for bid, (g, bi, slots) in enumerate(batches):
                    sched.append((kvh, qhl, bid, g, bi, slots))
        NB = len(sched)

        st_done = {}
        exp_done = {}
        gate_av = {}
        av_done = {}
        evacA_done = {}
        evacB_done = {}

        def emit_st(i):
            kvh, qhl, bid, g, bi, slots = sched[i]
            if kvh not in kv_issued:
                issue_kv_load(kvh)
            if bid == 0 and qhl == 0:
                wait("tensor", *kvload[kvh])
            par = kvh % 2
            Ksb, Vsb, Qsb = KVs[par]
            Sp = StP[i % 2]
            if i >= 2:
                wait("tensor", "act", exp_done[i - 2])
            for idx, (g_, j) in enumerate(slots):
                koff = slot_cols(g_, j)
                emit("tensor", lambda e, idx=idx, koff=koff, g_=g_, qhl=qhl, Sp=Sp,
                     Ksb=Ksb, Qsb=Qsb, last=(idx == 3): (
                         lambda x: x.then_inc(sems["pe"], 1) if last else x)(
                         e.matmul(
                             Sp[:, idx * 256:(idx + 1) * 256],
                             Ksb[:, koff:koff + 128],
                             Qsb[:, qhl * TPC + g_ * 256:qhl * TPC + (g_ + 1) * 256],
                             start=True, stop=True)))
            st_done[i] = winc("pe")

        wait("vector", "dc", prefetch["dmask"])
        emit_st(0)
        emit_st(1)
        for i in range(NB):
            kvh, qhl, bid, g, bi, slots = sched[i]
            qh = 4 * kvh + qhl
            par = kvh % 2
            Ksb, Vsb, Qsb = KVs[par]
            pb = i % 2
            Sp = StP[pb]
            wait("scalar", "pe", st_done[i])
            if i >= 2:
                wait("scalar", "pe", av_done[i - 2])
            exp_done[i] = emit("scalar", lambda e, pb=pb, Sp=Sp: e.activation(
                pt_sb[:, pb * 1024:(pb + 1) * 1024], Sp[:, :], AF.Exp,
                bias=cAP["shift"], scale=cAP["one"]))
            q0 = pb * 1024
            if bi == 0:
                wait("vector", "act", exp_done[i])
                ev = emit("vector", lambda e, q0=q0, g=g: e.tensor_mul(
                    pt_sb[:, q0:q0 + 1024], pt_sb[:, q0:q0 + 1024],
                    dmask_sb[:, g * 1024:(g + 1) * 1024]))
                gate_av[i] = [("dve", ev)]
            elif (g, bi) in CTRLB_IX:
                cb = CTRLB_IX[(g, bi)]
                base = ctrlb_sb.ap()
                in1 = bass.AP(tensor=base.tensor, offset=base.offset + 4 * cb,
                              ap=[[base.ap[0][0], 128], [1, 4], [0, 256]])
                wait("vector", "act", exp_done[i])
                ev = emit("vector", lambda e, q0=q0, in1=in1: e.tensor_tensor(
                    pt_sb.ap().rearrange("p (c n) -> p c n", n=256)[:, q0 // 256:q0 // 256 + 4, :],
                    pt_sb.ap().rearrange("p (c n) -> p c n", n=256)[:, q0 // 256:q0 // 256 + 4, :],
                    in1, ALU.mult))
                gate_av[i] = [("dve", ev)]
            else:
                gate_av[i] = [("act", exp_done[i])]
            if i + 2 < NB:
                emit_st(i + 2)
            for gt in gate_av[i]:
                wait("tensor", *gt)
            if bid == 0 and qh > 0:
                wait("tensor", "dve", evacA_done[qh - 1])
            if bid == 12 and qh > 0:
                wait("tensor", "dve", evacB_done[qh - 1])
            ctxp = ctxA if g < 2 else ctxB
            c0 = (g % 2) * 256
            lp = lA if g < 2 else lB
            l0 = (g % 2) * 256
            first = (bi == 0)
            last = (bi == SG[g] // 4 - 1)
            for idx, (g_, j) in enumerate(slots):
                voff = slot_cols(g_, j)
                emit("tensor", lambda e, idx=idx, voff=voff, ctxp=ctxp, pb=pb, c0=c0,
                     Vsb=Vsb, fl=(j == 0), ll=(j == SG[g] - 1): e.matmul(
                         ctxp[:, c0:c0 + 256], Vsb[:, voff:voff + 128],
                         pt_sb[:, pb * 1024 + idx * 256:pb * 1024 + (idx + 1) * 256],
                         start=fl, stop=ll))
            for idx in range(4):
                emit("tensor", lambda e, idx=idx, lp=lp, pb=pb, l0=l0,
                     fl=(first and idx == 0), ll=(last and idx == 3),
                     lastb=(idx == 3): (
                         lambda x: x.then_inc(sems["pe"], 1) if lastb else x)(
                         e.matmul(
                             lp[:, l0:l0 + 256], ones_sb[:, :],
                             pt_sb[:, pb * 1024 + idx * 256:pb * 1024 + (idx + 1) * 256],
                             start=fl, stop=ll)))
            av_done[i] = winc("pe")
            if bid == 39:
                av_last_kv[kvh] = av_done[i]
            if bid == 11:
                wait("scalar", "pe", av_done[i])
                if qh > 0:
                    wait("scalar", "dve", evacA_done[qh - 1])
                emit("scalar", lambda e: e.activation(
                    l2_sb[:, 0:512], lA[:, :], AF.Ln, bias=cAP["zero"], scale=cAP["one"]))
                a_rl = emit("scalar", lambda e: e.activation(
                    rl_sb[:, 0:512], l2_sb[:, 0:512], AF.Exp, bias=cAP["zero"], scale=cAP["mone"]))
                wait("vector", "act", a_rl)
                evacA_done[qh] = emit("vector", lambda e, qh=qh: e.tensor_mul(
                    ctx_sb[:, qh * TPC:qh * TPC + 512], ctxA[:, :], rl_sb[:, 0:512]))
            if bid == 39:
                wait("scalar", "pe", av_done[i])
                if qh > 0:
                    wait("scalar", "dve", evacB_done[qh - 1])
                emit("scalar", lambda e: e.activation(
                    l2_sb[:, 512:1024], lB[:, :], AF.Ln, bias=cAP["zero"], scale=cAP["one"]))
                b_rl = emit("scalar", lambda e: e.activation(
                    rl_sb[:, 512:1024], l2_sb[:, 512:1024], AF.Exp, bias=cAP["zero"], scale=cAP["mone"]))
                wait("vector", "act", b_rl)
                evacB_done[qh] = emit("vector", lambda e, qh=qh: e.tensor_mul(
                    ctx_sb[:, qh * TPC + 512:(qh + 1) * TPC], ctxB[:, :], rl_sb[:, 512:1024]))
        if PHASE_LIM >= 5:
            wait("gpsimd", "pe", av_last_kv[6])
            emit("gpsimd", lambda e: e.dma_start(
                out=slab0_sb.ap().rearrange("p (c n) -> p c n", n=512),
                in_=wo.rearrange("(c p) n -> p c n", p=128)[:, :, 0:512])
                .then_inc(sems["ga"], 16), dma=True)
            prefetch["wo0"] = ("ga", winc("ga", 16))
        barrier()
        if DBG:
            emit("sync", lambda e: e.dma_start(out=dbg["ctx"][:, :], in_=ctx_sb[:, :])
                 .then_inc(sems["pre"], 16))
            winc("pre", 16)
            emit("sync", lambda e: e.dma_start(out=dbg["cc"][:, :], in_=cc_out.ap()[:, :])
                 .then_inc(sems["pre"], 16))
            winc("pre", 16)

    if PHASE_LIM >= 5:
        emit("sync", lambda e: e.dma_start(out=xn_sb[:, :], in_=xn_scr.ap()[:, :])
             .then_inc(sems["dc"], 16))
        xn_reload = winc("dc", 16)
        slabs = [slab0_sb, slab1_sb]
        gld_wo = [None] * 8
        gld_wo[0] = prefetch["wo0"]
        emit("gpsimd", lambda e: e.dma_start(
            out=slab1_sb.ap().rearrange("p (c n) -> p c n", n=512),
            in_=wo.rearrange("(c p) n -> p c n", p=128)[:, :, 512:1024])
            .then_inc(sems["gb"], 16), dma=True)
        gld_wo[1] = ("gb", winc("gb", 16))
        wo_tile = {}
        s_evac = {}
        s_store = {}
        for ns in range(8):
            parn = ns % 2
            glane = "ga" if parn == 0 else "gb"
            if ns >= 2:
                wait("gpsimd", "pe", wo_tile[(ns - 2) * 8 + 7])
                emit("gpsimd", lambda e, ns=ns, parn=parn, glane=glane: e.dma_start(
                    out=slabs[parn].ap().rearrange("p (c n) -> p c n", n=512),
                    in_=wo.rearrange("(c p) n -> p c n", p=128)[:, :, ns * 512:(ns + 1) * 512])
                    .then_inc(sems[glane], 16), dma=True)
                gld_wo[ns] = (glane, winc(glane, 16))
            for tb in range(8):
                it = ns * 8 + tb
                half = it % 2
                if tb == 0:
                    wait("tensor", *gld_wo[ns])
                if it >= 2:
                    wait("tensor", "act", s_evac[it - 2])
                for cd in range(32):
                    emit("tensor", lambda e, cd=cd, half=half, tb=tb, parn=parn: (
                        lambda i: i.then_inc(sems["pe"], 1) if cd == 31 else i)(
                        e.matmul(
                            St0[:, half * 512:(half + 1) * 512],
                            ctx_sb[:, cd * TPC + tb * 128:cd * TPC + tb * 128 + 128],
                            slabs[parn][:, cd * 512:(cd + 1) * 512],
                            start=(cd == 0), stop=(cd == 31))))
                wo_tile[it] = winc("pe")
                slane = "sa" if half == 0 else "sb"
                wait("scalar", "pe", wo_tile[it])
                if it >= 2:
                    wait("scalar", *s_store[it - 2])
                s_evac[it] = emit("scalar", lambda e, half=half: e.copy(
                    sbf_sb[:, half * 512:(half + 1) * 512], St0[:, half * 512:(half + 1) * 512]))
                wait("sync", "act", s_evac[it])
                emit("sync", lambda e, tb=tb, ns=ns, half=half, slane=slane: e.dma_start(
                    out=s_scr.ap()[tb * 128:(tb + 1) * 128, ns * 512:(ns + 1) * 512],
                    in_=sbf_sb[:, half * 512:(half + 1) * 512]).then_inc(sems[slane], 16))
                s_store[it] = (slane, winc(slane, 16))
        if PHASE_LIM >= 6:
            wait("gpsimd", "pe", wo_tile[6 * 8 + 7])
            emit("gpsimd", lambda e: e.dma_start(
                out=slab0_sb.ap().rearrange("p (c n) -> p c n", n=512),
                in_=qwt.rearrange("(c p) n -> p c n", p=128)[:, :, 0:512])
                .then_inc(sems["ga"], 16), dma=True)
            prefetch["ffn0"] = ("ga", winc("ga", 16))
        barrier()
        if DBG:
            emit("sync", lambda e: e.dma_start(out=dbg["s"][:, :], in_=s_scr.ap()[:, :])
                 .then_inc(sems["pre"], 16))
            winc("pre", 16)

    if PHASE_LIM >= 6:
        emit("sync", lambda e: e.dma_start(out=bias_sb[:, :], in_=bias_bc[:, :])
             .then_inc(sems["dc"], 16))
        bias_ld = winc("dc", 16)
        slabs = [slab0_sb, slab1_sb]
        gld_ffn = [None] * 8
        gld_ffn[0] = prefetch["ffn0"]
        emit("gpsimd", lambda e: e.dma_start(
            out=slab1_sb.ap().rearrange("p (c n) -> p c n", n=512),
            in_=qwt.rearrange("(c p) n -> p c n", p=128)[:, :, 512:1024])
            .then_inc(sems["gb"], 16), dma=True)
        gld_ffn[1] = ("gb", winc("gb", 16))
        y_tile = {}
        stt_done = {}
        silu_done = {}
        comb_done = {}
        o_store = {}
        tload = {}
        for ns in range(8):
            parn = ns % 2
            glane = "ga" if parn == 0 else "gb"
            if ns >= 2:
                wait("gpsimd", "pe", y_tile[(ns - 2) * 8 + 7])
                emit("gpsimd", lambda e, ns=ns, parn=parn, glane=glane: e.dma_start(
                    out=slabs[parn].ap().rearrange("p (c n) -> p c n", n=512),
                    in_=qwt.rearrange("(c p) n -> p c n", p=128)[:, :, ns * 512:(ns + 1) * 512])
                    .then_inc(sems[glane], 16), dma=True)
                gld_ffn[ns] = (glane, winc(glane, 16))
            for tb in range(8):
                it = ns * 8 + tb
                half = it % 2
                if it >= 2:
                    wait("sync", "dve", comb_done[it - 2])
                lane = "da" if half == 0 else "db"
                emit("sync", lambda e, tb=tb, ns=ns, half=half, lane=lane: e.dma_start(
                    out=stile_sb[:, half * 512:(half + 1) * 512],
                    in_=s_scr.ap()[tb * 128:(tb + 1) * 128, ns * 512:(ns + 1) * 512])
                    .then_inc(sems[lane], 16))
                winc(lane, 16)
                emit("sync", lambda e, tb=tb, ns=ns, half=half, lane=lane: e.dma_start(
                    out=xtile_sb[:, half * 512:(half + 1) * 512],
                    in_=x_in[tb * 128:(tb + 1) * 128, ns * 512:(ns + 1) * 512])
                    .then_inc(sems[lane], 16))
                tload[it] = (lane, winc(lane, 16))
                if tb == 0:
                    wait("tensor", *gld_ffn[ns])
                    if ns == 0:
                        wait("tensor", "dc", xn_reload)
                if it >= 2:
                    wait("tensor", "dve", stt_done[it - 2])
                for kc in range(32):
                    emit("tensor", lambda e, kc=kc, half=half, tb=tb, parn=parn: (
                        lambda i: i.then_inc(sems["pe"], 1) if kc == 31 else i)(
                        e.matmul(
                            St0[:, half * 512:(half + 1) * 512],
                            xn_sb[:, kc * TPC + tb * 128:kc * TPC + tb * 128 + 128],
                            slabs[parn][:, kc * 512:(kc + 1) * 512],
                            start=(kc == 0), stop=(kc == 31))))
                y_tile[it] = winc("pe")
                wait("vector", "pe", y_tile[it])
                if it == 0:
                    wait("vector", "dc", bias_ld)
                    
                if it >= 2:
                    wait("vector", "act", silu_done[it - 2])
                stt_done[it] = emit("vector", lambda e, half=half, ns=ns: e.scalar_tensor_tensor(
                    t1f_sb[:, half * 512:(half + 1) * 512], St0[:, half * 512:(half + 1) * 512],
                    swb_sb[:, 0:1], bias_sb[:, ns * 512:(ns + 1) * 512], ALU.mult, ALU.add))
                wait("scalar", "dve", stt_done[it])
                if it >= 2:
                    wait("scalar", "dve", comb_done[it - 2])
                silu_done[it] = emit("scalar", lambda e, half=half: e.activation(
                    gt_sb[:, half * 512:(half + 1) * 512], t1f_sb[:, half * 512:(half + 1) * 512],
                    AF.Silu, bias=cAP["zero"], scale=cAP["one"]))
                wait("vector", "act", silu_done[it])
                wait("vector", *tload[it])
                emit("vector", lambda e, half=half: e.tensor_mul(
                    t1f_sb[:, half * 512:(half + 1) * 512], gt_sb[:, half * 512:(half + 1) * 512],
                    stile_sb[:, half * 512:(half + 1) * 512]))
                if it >= 2:
                    wait("vector", *o_store[it - 2])
                comb_done[it] = emit("vector", lambda e, half=half: e.tensor_add(
                    otile_sb[:, half * 512:(half + 1) * 512],
                    t1f_sb[:, half * 512:(half + 1) * 512],
                    xtile_sb[:, half * 512:(half + 1) * 512]))
                slane = "sa" if half == 0 else "sb"
                wait("sync", "dve", comb_done[it])
                emit("sync", lambda e, tb=tb, ns=ns, half=half, slane=slane: e.dma_start(
                    out=out[tb * 128:(tb + 1) * 128, ns * 512:(ns + 1) * 512],
                    in_=otile_sb[:, half * 512:(half + 1) * 512]).then_inc(sems[slane], 16))
                o_store[it] = (slane, winc(slane, 16))
    for nm in ("sa", "sb", "sc", "sd", "pre", "dc"):
        wait("sync", nm, cnt[nm])

    with nc.Block() as block:
        @block.tensor
        def _(e):
            for fn in ops["tensor"]:
                fn(e)

        @block.scalar
        def _(e):
            for fn in ops["scalar"]:
                fn(e)

        @block.vector
        def _(e):
            for fn in ops["vector"]:
                fn(e)

        @block.gpsimd
        def _(e):
            for fn in ops["gpsimd"]:
                fn(e)

        @block.sync
        def _(e):
            for fn in ops["sync"]:
                fn(e)

    stack.close()
    return nc


_NC_CACHE = None


def get_nc():
    global _NC_CACHE
    if _NC_CACHE is None:
        _NC_CACHE = build_nc()
    return _NC_CACHE


def make_in_maps(inputs):
    x = np.asarray(inputs["L0_attn_resid"])
    wq = np.ascontiguousarray(np.asarray(inputs["L0_attn_Wq"], dtype=np.float32))
    wk = np.ascontiguousarray(np.asarray(inputs["L0_attn_Wk"], dtype=np.float32))
    wv = np.ascontiguousarray(np.asarray(inputs["L0_attn_Wv"], dtype=np.float32))
    wo = np.ascontiguousarray(np.asarray(inputs["L0_attn_Wo"], dtype=np.float32))
    qw = np.asarray(inputs["L0_ffn_qw"])
    sw = np.asarray(inputs["L0_ffn_sw"], dtype=np.float32)
    bias = np.asarray(inputs["L0_ffn_bias"], dtype=np.float32)

    qwt = np.ascontiguousarray(qw.T.astype(np.int32))
    bias_bc = np.ascontiguousarray(np.broadcast_to(bias[None, :], (128, DM)).astype(np.float32))
    sw_bc = np.full((128, 1), float(sw), np.float32)
    cst, rot, idm = host_consts()

    xf = np.asarray(x, dtype=np.float32).reshape(T, DM)
    in_maps = []
    orders = []
    for core in range(NCORES):
        tabs, pos = host_tables(core)
        orders.append(pos)
        shard = np.ascontiguousarray(xf[pos])
        in_maps.append(dict(
            x=shard, wq=wq, wk=wk, wv=wv, wo=wo, qwt=qwt,
            bias_bc=bias_bc, sw_bc=sw_bc,
            cosq=tabs["cosq"], sinq=tabs["sinq"], cosk=tabs["cosk"], sink=tabs["sink"],
            cst=cst, ctrlb=np.ascontiguousarray(tabs["ctrlb"]),
            dmask=np.ascontiguousarray(tabs["dmask"]), rot=rot, idm=idm,
        ))
    return in_maps, orders


def kernel(**inputs):
    nc = get_nc()
    in_maps, orders = make_in_maps(inputs)
    res = run_bass_kernel_spmd(nc, in_maps, core_ids=list(range(NCORES)))
    full = np.empty((T, DM), np.float32)
    for core in range(NCORES):
        full[orders[core]] = np.asarray(res.results[core]["out"], dtype=np.float32)
    return full.reshape(1, T, DM)


# version 14
# speedup vs baseline: 1.0171x; 1.0171x over previous
import math
from contextlib import ExitStack

import numpy as np
import ml_dtypes

import concourse.bass as bass
import concourse.mybir as mybir
from concourse.bass_utils import run_bass_kernel_spmd

F32 = mybir.dt.float32
BF16 = mybir.dt.bfloat16
I32 = mybir.dt.int32
AF = mybir.ActivationFunctionType
ALU = mybir.AluOpType
BF = ml_dtypes.bfloat16
AX = mybir.AxisListType.X

NCORES = 8
T = 8192
DM = 4096
NKV = 8
NQH = 32
DH = 128
BLK = 256
TPC = 1024
SG = [16, 32, 48, 64]
CSHIFT = -24.0
MINV = [0, 16, 32, 48]
CTRL_COLS = [(g, j) for g in range(4) for j in range(2, SG[g]) if j - 2 >= MINV[g]]
NCTRL = len(CTRL_COLS)
CTRL_IX = {gj: c for c, gj in enumerate(CTRL_COLS)}


def core_blocks(i):
    return [i, 15 - i, 16 + i, 31 - i]


def owner_of_block(b):
    if b <= 7:
        return b, 0
    if b <= 15:
        return 15 - b, 1
    if b <= 23:
        return b - 16, 2
    return 31 - b, 3


def slot_cols(g, j):
    if j < 2:
        return 8192 + g * 256 + j * 128
    m = j - 2
    b, sub = m >> 1, m & 1
    r, lb = owner_of_block(b)
    return r * 1024 + lb * 256 + sub * 128


def rope_cos_sin(positions):
    idx = np.arange(0, DH, 2, dtype=np.float64) / DH
    freq = 1.0 / (2.0 * math.pi) * (500000.0 ** (-idx))
    factor, lo, hi = 32.0, 1.0, 4.0
    L0 = 8192
    freq_low = lo / L0
    freq_high = hi / L0
    freq_scaled = np.where(freq < freq_low, freq / factor, freq)
    smooth = np.clip((L0 * freq - lo) / (hi - lo), 0.0, 1.0)
    freq_smooth = (1.0 - smooth) * (freq / factor) + smooth * freq
    is_mid = (freq >= freq_low) & (freq <= freq_high)
    freq = np.where(is_mid, freq_smooth, freq_scaled)
    phase = 2.0 * math.pi * positions.astype(np.float64)[:, None] * freq[None, :]
    emb = np.concatenate([phase, phase], axis=-1)
    return np.cos(emb).T.astype(np.float32), np.sin(emb).T.astype(np.float32)


DIAG_BATCHES = [(g, 0) for g in range(4)]
CTRL_BATCHES = []
for g in range(4):
    for bi in range(1, SG[g] // 4):
        if any((4 * bi + idx) - 2 >= MINV[g] for idx in range(4)):
            CTRL_BATCHES.append((g, bi))
CTRLB_IX = {gb: c for c, gb in enumerate(CTRL_BATCHES)}
NCTRLB = len(CTRL_BATCHES)


def slot_keep(blocks, g, j):
    if j < 2:
        return 1.0
    return 1.0 if (j - 2) < 2 * blocks[g] else 0.0


def host_tables(core):
    blocks = core_blocks(core)
    pos = np.concatenate([np.arange(b * BLK, (b + 1) * BLK) for b in blocks])
    cos, sin = rope_cos_sin(pos)
    s128 = np.float32(1.0 / math.sqrt(DH))
    ctrlb = np.ones((128, 4 * NCTRLB), np.float32)
    for (g, bi), c in CTRLB_IX.items():
        for idx in range(4):
            ctrlb[:, 4 * c + idx] = slot_keep(blocks, g, 4 * bi + idx)
    p = np.arange(128)[:, None]
    f = np.arange(256)[None, :]
    st0 = (f >= p).astype(np.float32)
    st1 = (f >= 128 + p).astype(np.float32)
    dmask = np.ones((128, 4 * 1024), np.float32)
    for g in range(4):
        dmask[:, g * 1024:g * 1024 + 256] = st0
        dmask[:, g * 1024 + 256:g * 1024 + 512] = st1
        dmask[:, g * 1024 + 512:g * 1024 + 768] = slot_keep(blocks, g, 2)
        dmask[:, g * 1024 + 768:g * 1024 + 1024] = slot_keep(blocks, g, 3)
    return dict(cosq=np.ascontiguousarray(cos * s128),
                sinq=np.ascontiguousarray(sin * s128),
                cosk=cos, sink=sin, ctrlb=ctrlb,
                dmask=dmask.astype(BF)), pos


def host_consts():
    cst = np.zeros((128, 8), np.float32)
    cst[:, 0] = CSHIFT
    cst[:, 1] = 1.0
    cst[:, 2] = -0.5
    cst[:, 3] = 1e-5
    cst[:, 4] = 1.0 / DM
    cst[:, 5] = 0.0
    cst[:, 6] = -1.0
    rot = np.zeros((128, 128), np.float32)
    rot[np.arange(64) + 64, np.arange(64)] = -1.0
    rot[np.arange(64), np.arange(64) + 64] = 1.0
    idm = np.eye(128, dtype=np.float32)
    return cst, rot.astype(BF), idm


DBG = False
DBG2 = False
PHASE_LIM = 99


def build_nc():
    nc = bass.Bass("TRN2", target_bir_lowering=False)

    x_in = nc.dram_tensor("x", [TPC, DM], F32, kind="ExternalInput").ap()
    wq = nc.dram_tensor("wq", [DM, DM], F32, kind="ExternalInput").ap()
    wk = nc.dram_tensor("wk", [DM, NKV * DH], F32, kind="ExternalInput").ap()
    wv = nc.dram_tensor("wv", [DM, NKV * DH], F32, kind="ExternalInput").ap()
    wo = nc.dram_tensor("wo", [DM, DM], F32, kind="ExternalInput").ap()
    qwt = nc.dram_tensor("qwt", [DM, DM], I32, kind="ExternalInput").ap()
    bias_bc = nc.dram_tensor("bias_bc", [128, DM], F32, kind="ExternalInput").ap()
    sw_bc = nc.dram_tensor("sw_bc", [128, 1], F32, kind="ExternalInput").ap()
    cosq = nc.dram_tensor("cosq", [128, TPC], F32, kind="ExternalInput").ap()
    sinq = nc.dram_tensor("sinq", [128, TPC], F32, kind="ExternalInput").ap()
    cosk = nc.dram_tensor("cosk", [128, TPC], F32, kind="ExternalInput").ap()
    sink = nc.dram_tensor("sink", [128, TPC], F32, kind="ExternalInput").ap()
    cst_in = nc.dram_tensor("cst", [128, 8], F32, kind="ExternalInput").ap()
    ctrlb_in = nc.dram_tensor("ctrlb", [128, 4 * NCTRLB], F32, kind="ExternalInput").ap()
    dmask_in = nc.dram_tensor("dmask", [128, 4 * 1024], BF16, kind="ExternalInput").ap()
    rot_in = nc.dram_tensor("rot", [128, 128], BF16, kind="ExternalInput").ap()
    idm_in = nc.dram_tensor("idm", [128, 128], F32, kind="ExternalInput").ap()
    out = nc.dram_tensor("out", [TPC, DM], F32, kind="ExternalOutput").ap()

    cc_in = nc.dram_tensor("cc_in", [2 * TPC, 1024], BF16)
    cc_out = nc.dram_tensor("cc_out", [NCORES * 2 * TPC, 1024], BF16)
    q_scr = nc.dram_tensor("q_scr", [NQH * DH, TPC], BF16)
    xn_scr = nc.dram_tensor("xn_scr", [128, 32 * TPC], BF16)
    s_scr = nc.dram_tensor("s_scr", [TPC, DM], BF16)
    dbg = {}
    if DBG:
        dbg["xn"] = nc.dram_tensor("dbg_xn", [128, 32 * TPC], BF16, kind="ExternalOutput").ap()
        dbg["cc"] = nc.dram_tensor("dbg_cc", [NCORES * 2 * TPC, 1024], BF16, kind="ExternalOutput").ap()
        dbg["q"] = nc.dram_tensor("dbg_q", [NQH * DH, TPC], BF16, kind="ExternalOutput").ap()
        dbg["ctx"] = nc.dram_tensor("dbg_ctx", [128, 32 * TPC], BF16, kind="ExternalOutput").ap()
        dbg["s"] = nc.dram_tensor("dbg_s", [TPC, DM], BF16, kind="ExternalOutput").ap()
        if DBG2:
            dbg["xnf"] = nc.dram_tensor("dbg_xnf", [128, DM], F32, kind="ExternalOutput").ap()
            dbg["stat"] = nc.dram_tensor("dbg_stat", [128, 8], F32, kind="ExternalOutput").ap()
            dbg["x2"] = nc.dram_tensor("dbg_x2", [128, DM], F32, kind="ExternalOutput").ap()

    BASE = 16384 + 1024
    A, Bb, C, D, E, F = (BASE, BASE + 65536, BASE + 131072, BASE + 167936,
                         BASE + 184320, BASE + 200704)

    def sb(name, shape, dtype, off):
        return nc.alloc_sbuf_tensor_at(name, shape, dtype, offset=off)

    xn_sb = sb("xn_sb", [128, 32 * TPC], BF16, A)
    K1_sb = sb("K1_sb", [128, 9216], BF16, A)
    V1_sb = sb("V1_sb", [128, 9216], BF16, A + 18432)
    l2_sb = sb("l2_sb", [128, TPC], F32, A + 36864)
    rl_sb = sb("rl_sb", [128, TPC], F32, A + 40960)

    ctx_sb = sb("ctx_sb", [128, 32 * TPC], BF16, Bb)
    cosq_sb = sb("cosq_sb", [128, TPC], F32, Bb)
    sinq_sb = sb("sinq_sb", [128, TPC], F32, Bb + 4096)
    cosk_sb = sb("cosk_sb", [128, TPC], F32, Bb + 8192)
    sink_sb = sb("sink_sb", [128, TPC], F32, Bb + 12288)
    kp_sb = sb("kp_sb", [128, 2 * 512], BF16, Bb + 16384)
    t1_sb = sb("t1_sb", [128, 512], F32, Bb + 18432)
    t2_sb = sb("t2_sb", [128, 512], F32, Bb + 20480)
    kp2_sb = sb("kp2_sb", [128, 2 * 512], BF16, Bb + 22528)
    bias_sb = sb("bias_sb", [128, DM], F32, Bb)
    t1f_sb = sb("t1f_sb", [128, 2 * 512], F32, Bb + 16384)
    gt_sb = sb("gt_sb", [128, 2 * 512], BF16, Bb + 20480)
    stile_sb = sb("stile_sb", [128, 2 * 512], BF16, Bb + 22528)
    xtile_sb = sb("xtile_sb", [128, 2 * 512], F32, Bb + 24576)
    otile_sb = sb("otile_sb", [128, 2 * 512], F32, Bb + 28672)

    xt_sb = sb("xt_sb", [128, 2 * DM], F32, C)
    vslab_sb = sb("vslab_sb", [128, 16384], BF16, C)
    K0_sb = sb("K0_sb", [128, 9216], BF16, C)
    V0_sb = sb("V0_sb", [128, 9216], BF16, C + 18432)
    slab0_sb = sb("slab0_sb", [128, 16384], BF16, C)

    x2_sb = sb("x2_sb", [128, DM], F32, D)
    qslabD_sb = sb("qslabD_sb", [128, 8192], BF16, D)
    Q0_sb = sb("Q0_sb", [128, 4096], BF16, D)
    Q1_sb = sb("Q1_sb", [128, 4096], BF16, D + 8192)
    slab1_sb = sb("slab1_sb", [128, 16384], BF16, D)

    xnf_sb = sb("xnf_sb", [128, DM], F32, E)
    wslab_sb = sb("wslab_sb", [128, 2 * 4096], BF16, E)

    cst_sb = sb("cst_sb", [128, 8], F32, F)
    ctrlb_sb = sb("ctrlb_sb", [128, 4 * NCTRLB], F32, F + 32)
    rot_sb = sb("rot_sb", [128, 128], BF16, F + 288)
    idm_sb = sb("idm_sb", [128, 128], F32, F + 544)
    ones_sb = sb("ones_sb", [128, 128], BF16, F + 1056)
    onesr_sb = sb("onesr_sb", [1, 128], F32, F + 1312)
    swb_sb = sb("swb_sb", [128, 1], F32, F + 1824)
    pt_sb = sb("pt_sb", [128, 2 * 1024], BF16, F + 1856)
    sbf_sb = sb("sbf_sb", [128, 2 * 512], BF16, F + 5952)
    stat_sb = sb("stat_sb", [128, 8], F32, F + 8000)
    dmask_sb = sb("dmask_sb", [128, 4 * 1024], BF16, A + 49152)

    St0 = nc.alloc_psum_tensor("St0", [128, 1024], F32)
    St1 = nc.alloc_psum_tensor("St1", [128, 1024], F32)
    ctxA = nc.alloc_psum_tensor("ctxA", [128, 512], F32)
    ctxB = nc.alloc_psum_tensor("ctxB", [128, 512], F32)
    lA = nc.alloc_psum_tensor("lA", [128, 512], F32)
    lB = nc.alloc_psum_tensor("lB", [128, 512], F32)
    StP = [St0, St1]
    KVs = [(K0_sb, V0_sb, Q0_sb), (K1_sb, V1_sb, Q1_sb)]

    cAP = {"shift": cst_sb[:, 0:1], "one": cst_sb[:, 1:2], "mhalf": cst_sb[:, 2:3],
           "eps": cst_sb[:, 3:4], "rdm": cst_sb[:, 4:5], "zero": cst_sb[:, 5:6],
           "mone": cst_sb[:, 6:7]}

    ENGS = ("tensor", "scalar", "vector", "gpsimd", "sync")
    ops = {e: [] for e in ENGS}
    stack = ExitStack()
    sems = {}
    cnt = {}
    for name in ("pre", "da", "db", "dc", "ga", "gb", "gc", "gd", "ge", "gf",
                 "gg", "gh", "sa", "sb", "sc", "sd",
                 "cc", "pe", "act", "dve", "gp", "bar"):
        sems[name] = stack.enter_context(nc.semaphore(name))
        cnt[name] = 0

    AUTO = {"scalar": "act", "vector": "dve", "gpsimd": "gp"}

    def emit(eng, fn, dma=False):
        if eng in AUTO and not dma:
            nm = AUTO[eng]
            idx = cnt[nm]
            sem = sems[nm]

            def wrapped(e, fn=fn, idx=idx, sem=sem):
                if idx > 0:
                    e.wait_ge(sem, idx)
                fn(e).then_inc(sem, 1)
            ops[eng].append(wrapped)
            cnt[nm] += 1
            return cnt[nm]
        ops[eng].append(fn)
        return None

    def winc(name, n=1):
        cnt[name] += n
        return cnt[name]

    def wait(eng, name, val):
        if val and val > 0:
            emit(eng, lambda e, s=sems[name], v=val: e.wait_ge(s, v), dma=True)

    def barrier(engines=ENGS):
        for nm in ("pre", "da", "db", "dc", "sa", "sb", "sc", "sd"):
            wait("sync", nm, cnt[nm])
        for nm in ("ga", "gb"):
            wait("gpsimd", nm, cnt[nm])
        wait("scalar", "act", cnt["act"])
        wait("vector", "dve", cnt["dve"])
        wait("gpsimd", "gp", cnt["gp"])
        wait("tensor", "pe", cnt["pe"])
        prev = cnt["bar"]
        for en in engines:
            emit(en, lambda e, s=sems["bar"], p=prev: e.wait_ge(s, p), dma=True)
            emit(en, lambda e, s=sems["bar"]: e.sem_inc(s, 1), dma=True)
        tgt = winc("bar", len(engines))
        for en in engines:
            wait(en, "bar", tgt)

    for dst, src in ((cst_sb, cst_in), (ctrlb_sb, ctrlb_in),
                     (rot_sb, rot_in), (idm_sb, idm_in), (swb_sb, sw_bc),
                     (cosq_sb, cosq), (sinq_sb, sinq), (cosk_sb, cosk), (sink_sb, sink)):
        emit("sync", lambda e, d=dst, s=src: e.dma_start(out=d[:, :], in_=s[:, :])
             .then_inc(sems["pre"], 16))
        winc("pre", 16)
    emit("gpsimd", lambda e: e.memset(ones_sb[:, :], 1.0))
    g_ones = emit("gpsimd", lambda e: e.memset(onesr_sb[:, :], 1.0))
    n_pre = cnt["pre"]
    for en in ENGS:
        wait(en, "pre", n_pre)
        wait(en, "gp", g_ones)


    if PHASE_LIM >= 1:
        d_xnf = [None] * 8
        d_evac = []
        pe_tr = {}

        def emit_stats(tb):
            par = tb % 2
            xt = xt_sb[:, par * DM:(par + 1) * DM]
            lane = "da" if par == 0 else "db"
            xl = xload[tb]
            wait("vector", lane, xl)
            wait("scalar", lane, xl)
            emit("vector", lambda e, xt=xt: e.reduce_sum(stat_sb[:, 0:1], xt, axis=AX))
            a_sq = emit("scalar", lambda e, xt=xt: e.square(x2_sb[:, :], xt))
            yield
            wait("vector", "act", a_sq)
            emit("vector", lambda e: e.reduce_sum(stat_sb[:, 1:2], x2_sb[:, :], axis=AX))
            yield
            emit("vector", lambda e: e.tensor_scalar_mul(stat_sb[:, 2:3], stat_sb[:, 0:1], cAP["rdm"]))
            emit("vector", lambda e: e.tensor_scalar_mul(stat_sb[:, 3:4], stat_sb[:, 1:2], cAP["rdm"]))
            emit("vector", lambda e: e.tensor_mul(stat_sb[:, 4:5], stat_sb[:, 2:3], stat_sb[:, 2:3]))
            d_var = emit("vector", lambda e: e.tensor_sub(stat_sb[:, 5:6], stat_sb[:, 3:4], stat_sb[:, 4:5]))
            wait("scalar", "dve", d_var)
            emit("scalar", lambda e: e.activation(stat_sb[:, 6:7], stat_sb[:, 5:6], AF.Ln,
                                                  bias=cAP["eps"], scale=cAP["one"]))
            a_rstd = emit("scalar", lambda e: e.activation(stat_sb[:, 7:8], stat_sb[:, 6:7], AF.Exp,
                                                           bias=cAP["zero"], scale=cAP["mhalf"]))
            yield
            wait("vector", "act", a_rstd)
            if tb >= 1:
                wait("vector", "pe", pe_tr[(tb - 1, 7)])
            d_xnf[tb] = emit("vector", lambda e, xt=xt: e.tensor_scalar(
                xnf_sb[:, :], xt, stat_sb[:, 2:3], stat_sb[:, 7:8], ALU.subtract, ALU.mult))

        def emit_tr_batch(tb, b4):
            Sp = StP[b4 % 2]
            if b4 == 0:
                wait("tensor", "dve", d_xnf[tb])
            gi = tb * 8 + b4
            if gi >= 2:
                wait("tensor", "dve", d_evac[gi - 2])
            for j in range(4):
                kc = b4 * 4 + j
                last = (j == 3)
                emit("tensor", lambda e, kc=kc, Sp=Sp, j=j, last=last: (
                    e.transpose(Sp[:, j * 128:(j + 1) * 128],
                                xnf_sb[:, kc * 128:(kc + 1) * 128], idm_sb[:, :])
                    .then_inc(sems["pe"], 1) if last else
                    e.transpose(Sp[:, j * 128:(j + 1) * 128],
                                xnf_sb[:, kc * 128:(kc + 1) * 128], idm_sb[:, :])))
            pe_tr[(tb, b4)] = winc("pe")
            wait("vector", "pe", pe_tr[(tb, b4)])
            d_evac.append(emit("vector", lambda e, tb=tb, b4=b4, Sp=Sp: e.tensor_copy(
                xn_sb.ap().rearrange("p (c n) -> p c n", n=TPC)
                     [:, 4 * b4:4 * b4 + 4, tb * 128:(tb + 1) * 128],
                Sp[:, 0:512].rearrange("p (c n) -> p c n", n=128))))

        xload = [None] * 8

        def load_x(tb):
            par = tb % 2
            lane = "da" if par == 0 else "db"
            if tb >= 2:
                wait("sync", "dve", d_xnf[tb - 2])
            emit("sync", lambda e, tb=tb, par=par, lane=lane: e.dma_start(
                out=xt_sb[:, par * DM:(par + 1) * DM],
                in_=x_in[tb * 128:(tb + 1) * 128, :]).then_inc(sems[lane], 16))
            xload[tb] = winc(lane, 16)

        load_x(0)
        load_x(1)
        gen = emit_stats(0)
        for _ in gen:
            pass
        for tb in range(8):
            if tb + 2 < 8:
                load_x(tb + 2)
            if tb + 1 < 8:
                gen = emit_stats(tb + 1)
            else:
                gen = iter(())
            emit_tr_batch(tb, 0)
            next(gen, None)
            emit_tr_batch(tb, 1)
            emit_tr_batch(tb, 2)
            next(gen, None)
            emit_tr_batch(tb, 3)
            emit_tr_batch(tb, 4)
            next(gen, None)
            emit_tr_batch(tb, 5)
            emit_tr_batch(tb, 6)
            emit_tr_batch(tb, 7)
            for _ in gen:
                pass
        barrier()
        if DBG:
            emit("sync", lambda e: e.dma_start(out=dbg["xn"][:, :], in_=xn_sb[:, :])
                 .then_inc(sems["pre"], 16))
            winc("pre", 16)
            if "xnf" in dbg and PHASE_LIM == 1:
                emit("sync", lambda e: e.dma_start(out=dbg["xnf"][:, :], in_=xnf_sb[:, :])
                     .then_inc(sems["pre"], 16))
                winc("pre", 16)
                emit("sync", lambda e: e.dma_start(out=dbg["stat"][:, :], in_=stat_sb[:, :])
                     .then_inc(sems["pre"], 16))
                winc("pre", 16)
                emit("sync", lambda e: e.dma_start(out=dbg["x2"][:, :], in_=x2_sb[:, :])
                     .then_inc(sems["pre"], 16))
                winc("pre", 16)

    kvload = [None] * 8
    av_last_kv = [None] * 8
    kv_issued = set()
    prefetch = {}

    if PHASE_LIM >= 2:
        gld_k = [None] * 8
        pe_mm = {}
        pe_rot = {}
        a_evac = {}
        d_t1 = {}
        d_t2 = {}
        d_kp2 = {}
        st_k = {}
        pe_last_of_dk = {}
        for dk in range(8):
            par = dk % 2
            if dk >= 2:
                wait("gpsimd", "pe", pe_last_of_dk[dk - 2])
            glane = "ga" if par == 0 else "gb"
            emit("gpsimd", lambda e, dk=dk, par=par, glane=glane: e.dma_start(
                out=wslab_sb[:, par * 4096:(par + 1) * 4096].rearrange("p (c n) -> p c n", n=128),
                in_=wk.rearrange("(c p) n -> p c n", p=128)[:, :, dk * 128:(dk + 1) * 128])
                .then_inc(sems[glane], 16), dma=True)
            gld_k[dk] = (glane, winc(glane, 16))
            for tb2 in range(2):
                it = dk * 2 + tb2
                half = it % 2
                wait("tensor", *gld_k[dk])
                if it >= 2:
                    wait("tensor", "act", a_evac[it - 2])
                for kc in range(32):
                    emit("tensor", lambda e, kc=kc, half=half, par=par, tb2=tb2: (
                        lambda i: i.then_inc(sems["pe"], 1) if kc == 31 else i)(
                        e.matmul(
                            St0[:, half * 512:(half + 1) * 512],
                            wslab_sb[:, par * 4096 + kc * 128:par * 4096 + (kc + 1) * 128],
                            xn_sb[:, kc * TPC + tb2 * 512:kc * TPC + (tb2 + 1) * 512],
                            start=(kc == 0), stop=(kc == 31))))
                pe_mm[it] = winc("pe")
                wait("scalar", "pe", pe_mm[it])
                if it >= 2:
                    wait("scalar", "pe", pe_rot[it - 2])
                    wait("scalar", "dve", d_t1[it - 2])
                a_evac[it] = emit("scalar", lambda e, half=half: e.copy(
                    kp_sb[:, half * 512:(half + 1) * 512], St0[:, half * 512:(half + 1) * 512]))
                wait("tensor", "act", a_evac[it])
                if it >= 2:
                    wait("tensor", "dve", d_t2[it - 2])
                emit("tensor", lambda e, half=half: e.matmul(
                    St1[:, half * 512:(half + 1) * 512], rot_sb[:, :],
                    kp_sb[:, half * 512:(half + 1) * 512], start=True, stop=True)
                    .then_inc(sems["pe"], 1))
                pe_rot[it] = winc("pe")
                pe_last_of_dk[dk] = pe_rot[it]
                wait("vector", "act", a_evac[it])
                d_t1[it] = emit("vector", lambda e, half=half, tb2=tb2: e.tensor_mul(
                    t1_sb[:, :], kp_sb[:, half * 512:(half + 1) * 512],
                    cosk_sb[:, tb2 * 512:(tb2 + 1) * 512]))
                wait("vector", "pe", pe_rot[it])
                d_t2[it] = emit("vector", lambda e, half=half, tb2=tb2: e.tensor_mul(
                    t2_sb[:, :], St1[:, half * 512:(half + 1) * 512],
                    sink_sb[:, tb2 * 512:(tb2 + 1) * 512]))
                slane = "sa" if half == 0 else "sb"
                if it >= 2:
                    wait("vector", *st_k[it - 2])
                d_kp2[it] = emit("vector", lambda e, half=half: e.tensor_add(
                    kp2_sb[:, half * 512:(half + 1) * 512], t1_sb[:, :], t2_sb[:, :]))
                wait("sync", "dve", d_kp2[it])
                emit("sync", lambda e, dk=dk, tb2=tb2, half=half, slane=slane: e.dma_start(
                    out=cc_in.ap()[dk * 128:(dk + 1) * 128, tb2 * 512:(tb2 + 1) * 512],
                    in_=kp2_sb[:, half * 512:(half + 1) * 512]).then_inc(sems[slane], 16))
                st_k[it] = (slane, winc(slane, 16))

        gld_v = [None] * 2
        pe_vmm = {}
        a_vevac = {}
        st_v = {}
        for ns in range(2):
            if ns >= 1:
                wait("gpsimd", "pe", pe_vmm[7])
            else:
                wait("gpsimd", "pe", pe_last_of_dk[6])
            emit("gpsimd", lambda e, ns=ns: e.dma_start(
                out=vslab_sb.ap().rearrange("p (c n) -> p c n", n=512),
                in_=wv.rearrange("(c p) n -> p c n", p=128)[:, :, ns * 512:(ns + 1) * 512])
                .then_inc(sems["gc"], 16), dma=True)
            gld_v[ns] = winc("gc", 16)
            for tb in range(8):
                itv = ns * 8 + tb
                half = itv % 2
                wait("tensor", "gc", gld_v[ns])
                if itv >= 2:
                    wait("tensor", "act", a_vevac[itv - 2])
                else:
                    wait("tensor", "act", a_evac[14 + itv])
                for kc in range(32):
                    emit("tensor", lambda e, kc=kc, half=half, tb=tb: (
                        lambda i: i.then_inc(sems["pe"], 1) if kc == 31 else i)(
                        e.matmul(
                            St0[:, half * 512:(half + 1) * 512],
                            xn_sb[:, kc * TPC + tb * 128:kc * TPC + tb * 128 + 128],
                            vslab_sb[:, kc * 512:(kc + 1) * 512],
                            start=(kc == 0), stop=(kc == 31))))
                pe_vmm[itv] = winc("pe")
                slane = "sa" if half == 0 else "sb"
                wait("scalar", "pe", pe_vmm[itv])
                wait("scalar", *(st_v[itv - 2] if itv >= 2 else st_k[14 + half]))
                a_vevac[itv] = emit("scalar", lambda e, half=half: e.copy(
                    kp_sb[:, half * 512:(half + 1) * 512], St0[:, half * 512:(half + 1) * 512]))
                wait("sync", "act", a_vevac[itv])
                emit("sync", lambda e, tb=tb, ns=ns, half=half, slane=slane: e.dma_start(
                    out=cc_in.ap()[1024 + tb * 128:1024 + (tb + 1) * 128, ns * 512:(ns + 1) * 512],
                    in_=kp_sb[:, half * 512:(half + 1) * 512]).then_inc(sems[slane], 16))
                st_v[itv] = (slane, winc(slane, 16))
        prefetch["cc_gate"] = (cnt["sa"], cnt["sb"])

    def issue_kv_load(kvh):
        par = kvh % 2
        Ksb, Vsb, Qsb = KVs[par]
        lane = "da" if par == 0 else "db"
        if kvh < 2:
            wait("sync", "pre", cnt["pre"])
            wait("sync", "dc", cnt["dc"])
            wait("sync", "cc", 1)
            wait("sync", "pe", pe_last_of_qh[7])
            wait("sync", *st_q[2 * (4 * (kvh + 1)) - 1])
        else:
            wait("sync", "pe", av_last_kv[kvh - 2])
        for r in range(NCORES):
            emit("sync", lambda e, r=r, kvh=kvh, Ksb=Ksb, lane=lane: e.dma_start(
                out=Ksb[:, r * 1024:(r + 1) * 1024],
                in_=cc_out.ap()[(2048 * r + kvh * 128):(2048 * r + (kvh + 1) * 128), :])
                .then_inc(sems[lane], 16))
            winc(lane, 16)
        emit("sync", lambda e, kvh=kvh, Ksb=Ksb, lane=lane: e.dma_start(
            out=Ksb[:, 8192:9216],
            in_=cc_in.ap()[kvh * 128:(kvh + 1) * 128, :]).then_inc(sems[lane], 16))
        winc(lane, 16)
        for r in range(NCORES):
            emit("sync", lambda e, r=r, kvh=kvh, Vsb=Vsb, lane=lane: e.dma_start(
                out=Vsb[:, r * 1024:(r + 1) * 1024].rearrange("p (c n) -> p c n", n=128),
                in_=cc_out.ap().rearrange("(a p) n -> p a n", p=128)
                    [:, 16 * r + 8:16 * r + 16, kvh * 128:(kvh + 1) * 128])
                .then_inc(sems[lane], 16))
            winc(lane, 16)
        emit("sync", lambda e, kvh=kvh, Vsb=Vsb, lane=lane: e.dma_start(
            out=Vsb[:, 8192:9216].rearrange("p (c n) -> p c n", n=128),
            in_=cc_in.ap().rearrange("(a p) n -> p a n", p=128)
                [:, 8:16, kvh * 128:(kvh + 1) * 128]).then_inc(sems[lane], 16))
        winc(lane, 16)
        emit("sync", lambda e, kvh=kvh, Qsb=Qsb, lane=lane: e.dma_start(
            out=Qsb.ap().rearrange("p (c n) -> p c n", n=TPC),
            in_=q_scr.ap().rearrange("(c p) n -> p c n", p=128)[:, 4 * kvh:4 * kvh + 4, :])
            .then_inc(sems[lane], 16))
        kvload[kvh] = ("da" if par == 0 else "db", winc(lane, 16))
        kv_issued.add(kvh)

    if PHASE_LIM >= 3:
        emit("sync", lambda e: e.dma_start(out=xn_scr.ap()[:, :], in_=xn_sb[:, :])
             .then_inc(sems["dc"], 16))
        winc("dc", 16)
        gld_q = [None] * 32
        pe_qmm = {}
        pe_qrot = {}
        a_qevac = {}
        dq_t1 = {}
        dq_t2 = {}
        dq_kp2 = {}
        st_q = {}
        pe_last_of_qh = {}
        QLANES = ["ga", "gb", "gc", "gd", "ge", "gf", "gg", "gh"]

        def qslab_ap(slot):
            if slot < 2:
                return wslab_sb[:, slot * 4096:(slot + 1) * 4096]
            if slot < 4:
                return qslabD_sb[:, (slot - 2) * 4096:(slot - 1) * 4096]
            return vslab_sb[:, (slot - 4) * 4096:(slot - 3) * 4096]

        for qh in range(32):
            slot = qh % 8
            glane = QLANES[slot]
            if qh >= 8:
                wait("gpsimd", "pe", pe_last_of_qh[qh - 8])
            elif slot < 2:
                wait("gpsimd", "pe", pe_last_of_dk[6 + slot])
            elif slot >= 4 or slot in (2, 3):
                wait("gpsimd", "pe", pe_vmm[15])
            emit("gpsimd", lambda e, qh=qh, slot=slot, glane=glane: e.dma_start(
                out=qslab_ap(slot).rearrange("p (c n) -> p c n", n=128),
                in_=wq.rearrange("(c p) n -> p c n", p=128)[:, :, qh * 128:(qh + 1) * 128])
                .then_inc(sems[glane], 16), dma=True)
            gld_q[qh] = (glane, winc(glane, 16))
            if qh == 7:
                wait("gpsimd", "sa", prefetch["cc_gate"][0])
                wait("gpsimd", "sb", prefetch["cc_gate"][1])
                emit("gpsimd", lambda e: e.collective_compute(
                    "AllGather", mybir.AluOpType.bypass,
                    replica_groups=[list(range(NCORES))],
                    ins=[cc_in.ap()], outs=[cc_out.ap()]).then_inc(sems["cc"]), dma=True)
                cnt["cc"] = 1
            for tb2 in range(2):
                it = qh * 2 + tb2
                half = it % 2
                wait("tensor", *gld_q[qh])
                if it >= 2:
                    wait("tensor", "act", a_qevac[it - 2])
                else:
                    wait("tensor", "act", a_vevac[14 + it])
                slot = qh % 8
                for kc in range(32):
                    emit("tensor", lambda e, kc=kc, half=half, slot=slot, tb2=tb2: (
                        lambda i: i.then_inc(sems["pe"], 1) if kc == 31 else i)(
                        e.matmul(
                            St0[:, half * 512:(half + 1) * 512],
                            qslab_ap(slot)[:, kc * 128:(kc + 1) * 128],
                            xn_sb[:, kc * TPC + tb2 * 512:kc * TPC + (tb2 + 1) * 512],
                            start=(kc == 0), stop=(kc == 31))))
                pe_qmm[it] = winc("pe")
                wait("scalar", "pe", pe_qmm[it])
                if it >= 2:
                    wait("scalar", "pe", pe_qrot[it - 2])
                    wait("scalar", "dve", dq_t1[it - 2])
                else:
                    wait("scalar", *st_v[14 + it])
                a_qevac[it] = emit("scalar", lambda e, half=half: e.copy(
                    kp_sb[:, half * 512:(half + 1) * 512], St0[:, half * 512:(half + 1) * 512]))
                wait("tensor", "act", a_qevac[it])
                if it >= 2:
                    wait("tensor", "dve", dq_t2[it - 2])
                else:
                    wait("tensor", "dve", d_t2[14 + it])
                emit("tensor", lambda e, half=half: e.matmul(
                    St1[:, half * 512:(half + 1) * 512], rot_sb[:, :],
                    kp_sb[:, half * 512:(half + 1) * 512], start=True, stop=True)
                    .then_inc(sems["pe"], 1))
                pe_qrot[it] = winc("pe")
                pe_last_of_qh[qh] = pe_qrot[it]
                wait("vector", "act", a_qevac[it])
                dq_t1[it] = emit("vector", lambda e, half=half, tb2=tb2: e.tensor_mul(
                    t1_sb[:, :], kp_sb[:, half * 512:(half + 1) * 512],
                    cosq_sb[:, tb2 * 512:(tb2 + 1) * 512]))
                wait("vector", "pe", pe_qrot[it])
                dq_t2[it] = emit("vector", lambda e, half=half, tb2=tb2: e.tensor_mul(
                    t2_sb[:, :], St1[:, half * 512:(half + 1) * 512],
                    sinq_sb[:, tb2 * 512:(tb2 + 1) * 512]))
                slane = "sc" if half == 0 else "sd"
                if it >= 2:
                    wait("vector", *st_q[it - 2])
                else:
                    wait("vector", *st_k[14 + it])
                dq_kp2[it] = emit("vector", lambda e, half=half: e.tensor_add(
                    kp2_sb[:, half * 512:(half + 1) * 512], t1_sb[:, :], t2_sb[:, :]))
                wait("sync", "dve", dq_kp2[it])
                emit("sync", lambda e, qh=qh, tb2=tb2, half=half, slane=slane: e.dma_start(
                    out=q_scr.ap()[qh * 128:(qh + 1) * 128, tb2 * 512:(tb2 + 1) * 512],
                    in_=kp2_sb[:, half * 512:(half + 1) * 512]).then_inc(sems[slane], 16))
                st_q[it] = (slane, winc(slane, 16))
        if PHASE_LIM >= 4:
            issue_kv_load(0)
            issue_kv_load(1)
            emit("sync", lambda e: e.dma_start(out=dmask_sb[:, :], in_=dmask_in[:, :])
                 .then_inc(sems["dc"], 16))
            prefetch["dmask"] = winc("dc", 16)
        barrier()
        if DBG:
            emit("sync", lambda e: e.dma_start(out=dbg["q"][:, :], in_=q_scr.ap()[:, :])
                 .then_inc(sems["pre"], 16))
            winc("pre", 16)

    if PHASE_LIM >= 4:
        batches = []
        for g in range(4):
            for bi in range(SG[g] // 4):
                batches.append((g, bi, [(g, 4 * bi + jj) for jj in range(4)]))
        assert len(batches) == 40

        sched = []
        for kvh in range(8):
            for qhl in range(4):
                for bid, (g, bi, slots) in enumerate(batches):
                    sched.append((kvh, qhl, bid, g, bi, slots))
        NB = len(sched)

        st_done = {}
        exp_done = {}
        gate_av = {}
        av_done = {}
        evacA_done = {}
        evacB_done = {}

        def emit_st(i):
            kvh, qhl, bid, g, bi, slots = sched[i]
            if kvh not in kv_issued:
                issue_kv_load(kvh)
            if bid == 0 and qhl == 0:
                wait("tensor", *kvload[kvh])
            par = kvh % 2
            Ksb, Vsb, Qsb = KVs[par]
            Sp = StP[i % 2]
            if i >= 2:
                wait("tensor", "act", exp_done[i - 2])
            for idx, (g_, j) in enumerate(slots):
                koff = slot_cols(g_, j)
                emit("tensor", lambda e, idx=idx, koff=koff, g_=g_, qhl=qhl, Sp=Sp,
                     Ksb=Ksb, Qsb=Qsb, last=(idx == 3): (
                         lambda x: x.then_inc(sems["pe"], 1) if last else x)(
                         e.matmul(
                             Sp[:, idx * 256:(idx + 1) * 256],
                             Ksb[:, koff:koff + 128],
                             Qsb[:, qhl * TPC + g_ * 256:qhl * TPC + (g_ + 1) * 256],
                             start=True, stop=True)))
            st_done[i] = winc("pe")

        wait("vector", "dc", prefetch["dmask"])
        emit_st(0)
        emit_st(1)
        for i in range(NB):
            kvh, qhl, bid, g, bi, slots = sched[i]
            qh = 4 * kvh + qhl
            par = kvh % 2
            Ksb, Vsb, Qsb = KVs[par]
            pb = i % 2
            Sp = StP[pb]
            wait("scalar", "pe", st_done[i])
            if i >= 2:
                wait("scalar", "pe", av_done[i - 2])
            exp_done[i] = emit("scalar", lambda e, pb=pb, Sp=Sp: e.activation(
                pt_sb[:, pb * 1024:(pb + 1) * 1024], Sp[:, :], AF.Exp,
                bias=cAP["shift"], scale=cAP["one"]))
            q0 = pb * 1024
            if bi == 0:
                wait("vector", "act", exp_done[i])
                ev = emit("vector", lambda e, q0=q0, g=g: e.tensor_mul(
                    pt_sb[:, q0:q0 + 1024], pt_sb[:, q0:q0 + 1024],
                    dmask_sb[:, g * 1024:(g + 1) * 1024]))
                gate_av[i] = [("dve", ev)]
            elif (g, bi) in CTRLB_IX:
                cb = CTRLB_IX[(g, bi)]
                base = ctrlb_sb.ap()
                in1 = bass.AP(tensor=base.tensor, offset=base.offset + 4 * cb,
                              ap=[[base.ap[0][0], 128], [1, 4], [0, 256]])
                wait("vector", "act", exp_done[i])
                ev = emit("vector", lambda e, q0=q0, in1=in1: e.tensor_tensor(
                    pt_sb.ap().rearrange("p (c n) -> p c n", n=256)[:, q0 // 256:q0 // 256 + 4, :],
                    pt_sb.ap().rearrange("p (c n) -> p c n", n=256)[:, q0 // 256:q0 // 256 + 4, :],
                    in1, ALU.mult))
                gate_av[i] = [("dve", ev)]
            else:
                gate_av[i] = [("act", exp_done[i])]
            if i + 2 < NB:
                emit_st(i + 2)
            for gt in gate_av[i]:
                wait("tensor", *gt)
            if bid == 0 and qh > 0:
                wait("tensor", "dve", evacA_done[qh - 1])
            if bid == 12 and qh > 0:
                wait("tensor", "dve", evacB_done[qh - 1])
            ctxp = ctxA if g < 2 else ctxB
            c0 = (g % 2) * 256
            lp = lA if g < 2 else lB
            l0 = (g % 2) * 256
            first = (bi == 0)
            last = (bi == SG[g] // 4 - 1)
            for idx, (g_, j) in enumerate(slots):
                voff = slot_cols(g_, j)
                emit("tensor", lambda e, idx=idx, voff=voff, ctxp=ctxp, pb=pb, c0=c0,
                     Vsb=Vsb, fl=(j == 0), ll=(j == SG[g] - 1): e.matmul(
                         ctxp[:, c0:c0 + 256], Vsb[:, voff:voff + 128],
                         pt_sb[:, pb * 1024 + idx * 256:pb * 1024 + (idx + 1) * 256],
                         start=fl, stop=ll))
            for idx in range(4):
                emit("tensor", lambda e, idx=idx, lp=lp, pb=pb, l0=l0,
                     fl=(first and idx == 0), ll=(last and idx == 3),
                     lastb=(idx == 3): (
                         lambda x: x.then_inc(sems["pe"], 1) if lastb else x)(
                         e.matmul(
                             lp[:, l0:l0 + 256], ones_sb[:, :],
                             pt_sb[:, pb * 1024 + idx * 256:pb * 1024 + (idx + 1) * 256],
                             start=fl, stop=ll)))
            av_done[i] = winc("pe")
            if bid == 39:
                av_last_kv[kvh] = av_done[i]
            if bid == 11:
                wait("scalar", "pe", av_done[i])
                if qh > 0:
                    wait("scalar", "dve", evacA_done[qh - 1])
                emit("scalar", lambda e: e.activation(
                    l2_sb[:, 0:512], lA[:, :], AF.Ln, bias=cAP["zero"], scale=cAP["one"]))
                a_rl = emit("scalar", lambda e: e.activation(
                    rl_sb[:, 0:512], l2_sb[:, 0:512], AF.Exp, bias=cAP["zero"], scale=cAP["mone"]))
                wait("vector", "act", a_rl)
                evacA_done[qh] = emit("vector", lambda e, qh=qh: e.tensor_mul(
                    ctx_sb[:, qh * TPC:qh * TPC + 512], ctxA[:, :], rl_sb[:, 0:512]))
            if bid == 39:
                wait("scalar", "pe", av_done[i])
                if qh > 0:
                    wait("scalar", "dve", evacB_done[qh - 1])
                emit("scalar", lambda e: e.activation(
                    l2_sb[:, 512:1024], lB[:, :], AF.Ln, bias=cAP["zero"], scale=cAP["one"]))
                b_rl = emit("scalar", lambda e: e.activation(
                    rl_sb[:, 512:1024], l2_sb[:, 512:1024], AF.Exp, bias=cAP["zero"], scale=cAP["mone"]))
                wait("vector", "act", b_rl)
                evacB_done[qh] = emit("vector", lambda e, qh=qh: e.tensor_mul(
                    ctx_sb[:, qh * TPC + 512:(qh + 1) * TPC], ctxB[:, :], rl_sb[:, 512:1024]))
        if PHASE_LIM >= 5:
            wait("gpsimd", "pe", av_last_kv[6])
            emit("gpsimd", lambda e: e.dma_start(
                out=slab0_sb.ap().rearrange("p (c n) -> p c n", n=512),
                in_=wo.rearrange("(c p) n -> p c n", p=128)[:, :, 0:512])
                .then_inc(sems["ga"], 16), dma=True)
            prefetch["wo0"] = ("ga", winc("ga", 16))
        barrier()
        if DBG:
            emit("sync", lambda e: e.dma_start(out=dbg["ctx"][:, :], in_=ctx_sb[:, :])
                 .then_inc(sems["pre"], 16))
            winc("pre", 16)
            emit("sync", lambda e: e.dma_start(out=dbg["cc"][:, :], in_=cc_out.ap()[:, :])
                 .then_inc(sems["pre"], 16))
            winc("pre", 16)

    if PHASE_LIM >= 5:
        emit("sync", lambda e: e.dma_start(out=xn_sb[:, :], in_=xn_scr.ap()[:, :])
             .then_inc(sems["dc"], 16))
        xn_reload = winc("dc", 16)
        slabs = [slab0_sb, slab1_sb]
        gld_wo = [None] * 8
        gld_wo[0] = prefetch["wo0"]
        emit("gpsimd", lambda e: e.dma_start(
            out=slab1_sb.ap().rearrange("p (c n) -> p c n", n=512),
            in_=wo.rearrange("(c p) n -> p c n", p=128)[:, :, 512:1024])
            .then_inc(sems["gb"], 16), dma=True)
        gld_wo[1] = ("gb", winc("gb", 16))
        wo_tile = {}
        s_evac = {}
        s_store = {}
        for ns in range(8):
            parn = ns % 2
            glane = "ga" if parn == 0 else "gb"
            if ns >= 2:
                wait("gpsimd", "pe", wo_tile[(ns - 2) * 8 + 7])
                emit("gpsimd", lambda e, ns=ns, parn=parn, glane=glane: e.dma_start(
                    out=slabs[parn].ap().rearrange("p (c n) -> p c n", n=512),
                    in_=wo.rearrange("(c p) n -> p c n", p=128)[:, :, ns * 512:(ns + 1) * 512])
                    .then_inc(sems[glane], 16), dma=True)
                gld_wo[ns] = (glane, winc(glane, 16))
            for tb in range(8):
                it = ns * 8 + tb
                half = it % 2
                if tb == 0:
                    wait("tensor", *gld_wo[ns])
                if it >= 2:
                    wait("tensor", "act", s_evac[it - 2])
                for cd in range(32):
                    emit("tensor", lambda e, cd=cd, half=half, tb=tb, parn=parn: (
                        lambda i: i.then_inc(sems["pe"], 1) if cd == 31 else i)(
                        e.matmul(
                            St0[:, half * 512:(half + 1) * 512],
                            ctx_sb[:, cd * TPC + tb * 128:cd * TPC + tb * 128 + 128],
                            slabs[parn][:, cd * 512:(cd + 1) * 512],
                            start=(cd == 0), stop=(cd == 31))))
                wo_tile[it] = winc("pe")
                slane = "sa" if half == 0 else "sb"
                wait("scalar", "pe", wo_tile[it])
                if it >= 2:
                    wait("scalar", *s_store[it - 2])
                s_evac[it] = emit("scalar", lambda e, half=half: e.copy(
                    sbf_sb[:, half * 512:(half + 1) * 512], St0[:, half * 512:(half + 1) * 512]))
                wait("sync", "act", s_evac[it])
                emit("sync", lambda e, tb=tb, ns=ns, half=half, slane=slane: e.dma_start(
                    out=s_scr.ap()[tb * 128:(tb + 1) * 128, ns * 512:(ns + 1) * 512],
                    in_=sbf_sb[:, half * 512:(half + 1) * 512]).then_inc(sems[slane], 16))
                s_store[it] = (slane, winc(slane, 16))
        if PHASE_LIM >= 6:
            wait("gpsimd", "pe", wo_tile[6 * 8 + 7])
            emit("gpsimd", lambda e: e.dma_start(
                out=slab0_sb.ap().rearrange("p (c n) -> p c n", n=512),
                in_=qwt.rearrange("(c p) n -> p c n", p=128)[:, :, 0:512])
                .then_inc(sems["ga"], 16), dma=True)
            prefetch["ffn0"] = ("ga", winc("ga", 16))
        barrier()
        if DBG:
            emit("sync", lambda e: e.dma_start(out=dbg["s"][:, :], in_=s_scr.ap()[:, :])
                 .then_inc(sems["pre"], 16))
            winc("pre", 16)

    if PHASE_LIM >= 6:
        emit("sync", lambda e: e.dma_start(out=bias_sb[:, :], in_=bias_bc[:, :])
             .then_inc(sems["dc"], 16))
        bias_ld = winc("dc", 16)
        slabs = [slab0_sb, slab1_sb]
        gld_ffn = [None] * 8
        gld_ffn[0] = prefetch["ffn0"]
        emit("gpsimd", lambda e: e.dma_start(
            out=slab1_sb.ap().rearrange("p (c n) -> p c n", n=512),
            in_=qwt.rearrange("(c p) n -> p c n", p=128)[:, :, 512:1024])
            .then_inc(sems["gb"], 16), dma=True)
        gld_ffn[1] = ("gb", winc("gb", 16))
        y_tile = {}
        stt_done = {}
        silu_done = {}
        comb_done = {}
        o_store = {}
        tload = {}
        for ns in range(8):
            parn = ns % 2
            glane = "ga" if parn == 0 else "gb"
            if ns >= 2:
                wait("gpsimd", "pe", y_tile[(ns - 2) * 8 + 7])
                emit("gpsimd", lambda e, ns=ns, parn=parn, glane=glane: e.dma_start(
                    out=slabs[parn].ap().rearrange("p (c n) -> p c n", n=512),
                    in_=qwt.rearrange("(c p) n -> p c n", p=128)[:, :, ns * 512:(ns + 1) * 512])
                    .then_inc(sems[glane], 16), dma=True)
                gld_ffn[ns] = (glane, winc(glane, 16))
            for tb in range(8):
                it = ns * 8 + tb
                half = it % 2
                if it >= 2:
                    wait("sync", "dve", comb_done[it - 2])
                lane = "da" if half == 0 else "db"
                emit("sync", lambda e, tb=tb, ns=ns, half=half, lane=lane: e.dma_start(
                    out=stile_sb[:, half * 512:(half + 1) * 512],
                    in_=s_scr.ap()[tb * 128:(tb + 1) * 128, ns * 512:(ns + 1) * 512])
                    .then_inc(sems[lane], 16))
                winc(lane, 16)
                emit("sync", lambda e, tb=tb, ns=ns, half=half, lane=lane: e.dma_start(
                    out=xtile_sb[:, half * 512:(half + 1) * 512],
                    in_=x_in[tb * 128:(tb + 1) * 128, ns * 512:(ns + 1) * 512])
                    .then_inc(sems[lane], 16))
                tload[it] = (lane, winc(lane, 16))
                if tb == 0:
                    wait("tensor", *gld_ffn[ns])
                    if ns == 0:
                        wait("tensor", "dc", xn_reload)
                if it >= 2:
                    wait("tensor", "dve", stt_done[it - 2])
                for kc in range(32):
                    emit("tensor", lambda e, kc=kc, half=half, tb=tb, parn=parn: (
                        lambda i: i.then_inc(sems["pe"], 1) if kc == 31 else i)(
                        e.matmul(
                            St0[:, half * 512:(half + 1) * 512],
                            xn_sb[:, kc * TPC + tb * 128:kc * TPC + tb * 128 + 128],
                            slabs[parn][:, kc * 512:(kc + 1) * 512],
                            start=(kc == 0), stop=(kc == 31))))
                y_tile[it] = winc("pe")
                wait("vector", "pe", y_tile[it])
                if it == 0:
                    wait("vector", "dc", bias_ld)
                    
                if it >= 2:
                    wait("vector", "act", silu_done[it - 2])
                stt_done[it] = emit("vector", lambda e, half=half, ns=ns: e.scalar_tensor_tensor(
                    t1f_sb[:, half * 512:(half + 1) * 512], St0[:, half * 512:(half + 1) * 512],
                    swb_sb[:, 0:1], bias_sb[:, ns * 512:(ns + 1) * 512], ALU.mult, ALU.add))
                wait("scalar", "dve", stt_done[it])
                if it >= 2:
                    wait("scalar", "dve", comb_done[it - 2])
                silu_done[it] = emit("scalar", lambda e, half=half: e.activation(
                    gt_sb[:, half * 512:(half + 1) * 512], t1f_sb[:, half * 512:(half + 1) * 512],
                    AF.Silu, bias=cAP["zero"], scale=cAP["one"]))
                wait("vector", "act", silu_done[it])
                wait("vector", *tload[it])
                emit("vector", lambda e, half=half: e.tensor_mul(
                    t1f_sb[:, half * 512:(half + 1) * 512], gt_sb[:, half * 512:(half + 1) * 512],
                    stile_sb[:, half * 512:(half + 1) * 512]))
                if it >= 2:
                    wait("vector", *o_store[it - 2])
                comb_done[it] = emit("vector", lambda e, half=half: e.tensor_add(
                    otile_sb[:, half * 512:(half + 1) * 512],
                    t1f_sb[:, half * 512:(half + 1) * 512],
                    xtile_sb[:, half * 512:(half + 1) * 512]))
                slane = "sa" if half == 0 else "sb"
                wait("sync", "dve", comb_done[it])
                emit("sync", lambda e, tb=tb, ns=ns, half=half, slane=slane: e.dma_start(
                    out=out[tb * 128:(tb + 1) * 128, ns * 512:(ns + 1) * 512],
                    in_=otile_sb[:, half * 512:(half + 1) * 512]).then_inc(sems[slane], 16))
                o_store[it] = (slane, winc(slane, 16))
    for nm in ("sa", "sb", "sc", "sd", "pre", "dc"):
        wait("sync", nm, cnt[nm])

    with nc.Block() as block:
        @block.tensor
        def _(e):
            for fn in ops["tensor"]:
                fn(e)

        @block.scalar
        def _(e):
            for fn in ops["scalar"]:
                fn(e)

        @block.vector
        def _(e):
            for fn in ops["vector"]:
                fn(e)

        @block.gpsimd
        def _(e):
            for fn in ops["gpsimd"]:
                fn(e)

        @block.sync
        def _(e):
            for fn in ops["sync"]:
                fn(e)

    stack.close()
    return nc


_NC_CACHE = None


def get_nc():
    global _NC_CACHE
    if _NC_CACHE is None:
        _NC_CACHE = build_nc()
    return _NC_CACHE


def make_in_maps(inputs):
    x = np.asarray(inputs["L0_attn_resid"])
    wq = np.ascontiguousarray(np.asarray(inputs["L0_attn_Wq"], dtype=np.float32))
    wk = np.ascontiguousarray(np.asarray(inputs["L0_attn_Wk"], dtype=np.float32))
    wv = np.ascontiguousarray(np.asarray(inputs["L0_attn_Wv"], dtype=np.float32))
    wo = np.ascontiguousarray(np.asarray(inputs["L0_attn_Wo"], dtype=np.float32))
    qw = np.asarray(inputs["L0_ffn_qw"])
    sw = np.asarray(inputs["L0_ffn_sw"], dtype=np.float32)
    bias = np.asarray(inputs["L0_ffn_bias"], dtype=np.float32)

    qwt = np.ascontiguousarray(qw.T.astype(np.int32))
    bias_bc = np.ascontiguousarray(np.broadcast_to(bias[None, :], (128, DM)).astype(np.float32))
    sw_bc = np.full((128, 1), float(sw), np.float32)
    cst, rot, idm = host_consts()

    xf = np.asarray(x, dtype=np.float32).reshape(T, DM)
    in_maps = []
    orders = []
    for core in range(NCORES):
        tabs, pos = host_tables(core)
        orders.append(pos)
        shard = np.ascontiguousarray(xf[pos])
        in_maps.append(dict(
            x=shard, wq=wq, wk=wk, wv=wv, wo=wo, qwt=qwt,
            bias_bc=bias_bc, sw_bc=sw_bc,
            cosq=tabs["cosq"], sinq=tabs["sinq"], cosk=tabs["cosk"], sink=tabs["sink"],
            cst=cst, ctrlb=np.ascontiguousarray(tabs["ctrlb"]),
            dmask=np.ascontiguousarray(tabs["dmask"]), rot=rot, idm=idm,
        ))
    return in_maps, orders


def kernel(**inputs):
    nc = get_nc()
    in_maps, orders = make_in_maps(inputs)
    res = run_bass_kernel_spmd(nc, in_maps, core_ids=list(range(NCORES)))
    full = np.empty((T, DM), np.float32)
    for core in range(NCORES):
        full[orders[core]] = np.asarray(res.results[core]["out"], dtype=np.float32)
    return full.reshape(1, T, DM)


# version 15
# speedup vs baseline: 1.0206x; 1.0035x over previous
import math
from contextlib import ExitStack

import numpy as np
import ml_dtypes

import concourse.bass as bass
import concourse.mybir as mybir
from concourse.bass_utils import run_bass_kernel_spmd

F32 = mybir.dt.float32
BF16 = mybir.dt.bfloat16
I32 = mybir.dt.int32
AF = mybir.ActivationFunctionType
ALU = mybir.AluOpType
BF = ml_dtypes.bfloat16
AX = mybir.AxisListType.X

NCORES = 8
T = 8192
DM = 4096
NKV = 8
NQH = 32
DH = 128
BLK = 256
TPC = 1024
SG = [16, 32, 48, 64]
CSHIFT = -24.0
MINV = [0, 16, 32, 48]
CTRL_COLS = [(g, j) for g in range(4) for j in range(2, SG[g]) if j - 2 >= MINV[g]]
NCTRL = len(CTRL_COLS)
CTRL_IX = {gj: c for c, gj in enumerate(CTRL_COLS)}


def core_blocks(i):
    return [i, 15 - i, 16 + i, 31 - i]


def owner_of_block(b):
    if b <= 7:
        return b, 0
    if b <= 15:
        return 15 - b, 1
    if b <= 23:
        return b - 16, 2
    return 31 - b, 3


def slot_cols(g, j):
    if j < 2:
        return 8192 + g * 256 + j * 128
    m = j - 2
    b, sub = m >> 1, m & 1
    r, lb = owner_of_block(b)
    return r * 1024 + lb * 256 + sub * 128


def rope_cos_sin(positions):
    idx = np.arange(0, DH, 2, dtype=np.float64) / DH
    freq = 1.0 / (2.0 * math.pi) * (500000.0 ** (-idx))
    factor, lo, hi = 32.0, 1.0, 4.0
    L0 = 8192
    freq_low = lo / L0
    freq_high = hi / L0
    freq_scaled = np.where(freq < freq_low, freq / factor, freq)
    smooth = np.clip((L0 * freq - lo) / (hi - lo), 0.0, 1.0)
    freq_smooth = (1.0 - smooth) * (freq / factor) + smooth * freq
    is_mid = (freq >= freq_low) & (freq <= freq_high)
    freq = np.where(is_mid, freq_smooth, freq_scaled)
    phase = 2.0 * math.pi * positions.astype(np.float64)[:, None] * freq[None, :]
    emb = np.concatenate([phase, phase], axis=-1)
    return np.cos(emb).T.astype(np.float32), np.sin(emb).T.astype(np.float32)


DIAG_BATCHES = [(g, 0) for g in range(4)]
CTRL_BATCHES = []
for g in range(4):
    for bi in range(1, SG[g] // 4):
        if any((4 * bi + idx) - 2 >= MINV[g] for idx in range(4)):
            CTRL_BATCHES.append((g, bi))
CTRLB_IX = {gb: c for c, gb in enumerate(CTRL_BATCHES)}
NCTRLB = len(CTRL_BATCHES)


def slot_keep(blocks, g, j):
    if j < 2:
        return 1.0
    return 1.0 if (j - 2) < 2 * blocks[g] else 0.0


def host_tables(core):
    blocks = core_blocks(core)
    pos = np.concatenate([np.arange(b * BLK, (b + 1) * BLK) for b in blocks])
    cos, sin = rope_cos_sin(pos)
    s128 = np.float32(1.0 / math.sqrt(DH))
    ctrlb = np.ones((128, 4 * NCTRLB), np.float32)
    for (g, bi), c in CTRLB_IX.items():
        for idx in range(4):
            ctrlb[:, 4 * c + idx] = slot_keep(blocks, g, 4 * bi + idx)
    p = np.arange(128)[:, None]
    f = np.arange(256)[None, :]
    st0 = (f >= p).astype(np.float32)
    st1 = (f >= 128 + p).astype(np.float32)
    dmask = np.ones((128, 4 * 1024), np.float32)
    for g in range(4):
        dmask[:, g * 1024:g * 1024 + 256] = st0
        dmask[:, g * 1024 + 256:g * 1024 + 512] = st1
        dmask[:, g * 1024 + 512:g * 1024 + 768] = slot_keep(blocks, g, 2)
        dmask[:, g * 1024 + 768:g * 1024 + 1024] = slot_keep(blocks, g, 3)
    return dict(cosq=np.ascontiguousarray(cos * s128),
                sinq=np.ascontiguousarray(sin * s128),
                cosk=cos, sink=sin, ctrlb=ctrlb,
                dmask=dmask.astype(BF)), pos


def host_consts():
    cst = np.zeros((128, 8), np.float32)
    cst[:, 0] = CSHIFT
    cst[:, 1] = 1.0
    cst[:, 2] = -0.5
    cst[:, 3] = 1e-5
    cst[:, 4] = 1.0 / DM
    cst[:, 5] = 0.0
    cst[:, 6] = -1.0
    rot = np.zeros((128, 128), np.float32)
    rot[np.arange(64) + 64, np.arange(64)] = -1.0
    rot[np.arange(64), np.arange(64) + 64] = 1.0
    idm = np.eye(128, dtype=np.float32)
    return cst, rot.astype(BF), idm


DBG = False
DBG2 = False
PHASE_LIM = 99


def build_nc():
    nc = bass.Bass("TRN2", target_bir_lowering=False)

    x_in = nc.dram_tensor("x", [TPC, DM], F32, kind="ExternalInput").ap()
    wq = nc.dram_tensor("wq", [DM, DM], F32, kind="ExternalInput").ap()
    wk = nc.dram_tensor("wk", [DM, NKV * DH], F32, kind="ExternalInput").ap()
    wv = nc.dram_tensor("wv", [DM, NKV * DH], F32, kind="ExternalInput").ap()
    wo = nc.dram_tensor("wo", [DM, DM], F32, kind="ExternalInput").ap()
    qwt = nc.dram_tensor("qwt", [DM, DM], I32, kind="ExternalInput").ap()
    bias_bc = nc.dram_tensor("bias_bc", [128, DM], F32, kind="ExternalInput").ap()
    sw_bc = nc.dram_tensor("sw_bc", [128, 1], F32, kind="ExternalInput").ap()
    cosq = nc.dram_tensor("cosq", [128, TPC], F32, kind="ExternalInput").ap()
    sinq = nc.dram_tensor("sinq", [128, TPC], F32, kind="ExternalInput").ap()
    cosk = nc.dram_tensor("cosk", [128, TPC], F32, kind="ExternalInput").ap()
    sink = nc.dram_tensor("sink", [128, TPC], F32, kind="ExternalInput").ap()
    cst_in = nc.dram_tensor("cst", [128, 8], F32, kind="ExternalInput").ap()
    ctrlb_in = nc.dram_tensor("ctrlb", [128, 4 * NCTRLB], F32, kind="ExternalInput").ap()
    dmask_in = nc.dram_tensor("dmask", [128, 4 * 1024], BF16, kind="ExternalInput").ap()
    rot_in = nc.dram_tensor("rot", [128, 128], BF16, kind="ExternalInput").ap()
    idm_in = nc.dram_tensor("idm", [128, 128], F32, kind="ExternalInput").ap()
    out = nc.dram_tensor("out", [TPC, DM], F32, kind="ExternalOutput").ap()

    cc_in = nc.dram_tensor("cc_in", [2 * TPC, 1024], BF16)
    cc_out = nc.dram_tensor("cc_out", [NCORES * 2 * TPC, 1024], BF16)
    q_scr = nc.dram_tensor("q_scr", [NQH * DH, TPC], BF16)
    xn_scr = nc.dram_tensor("xn_scr", [128, 32 * TPC], BF16)
    s_scr = nc.dram_tensor("s_scr", [TPC, DM], BF16)
    dbg = {}
    if DBG:
        dbg["xn"] = nc.dram_tensor("dbg_xn", [128, 32 * TPC], BF16, kind="ExternalOutput").ap()
        dbg["cc"] = nc.dram_tensor("dbg_cc", [NCORES * 2 * TPC, 1024], BF16, kind="ExternalOutput").ap()
        dbg["q"] = nc.dram_tensor("dbg_q", [NQH * DH, TPC], BF16, kind="ExternalOutput").ap()
        dbg["ctx"] = nc.dram_tensor("dbg_ctx", [128, 32 * TPC], BF16, kind="ExternalOutput").ap()
        dbg["s"] = nc.dram_tensor("dbg_s", [TPC, DM], BF16, kind="ExternalOutput").ap()
        if DBG2:
            dbg["xnf"] = nc.dram_tensor("dbg_xnf", [128, DM], F32, kind="ExternalOutput").ap()
            dbg["stat"] = nc.dram_tensor("dbg_stat", [128, 8], F32, kind="ExternalOutput").ap()
            dbg["x2"] = nc.dram_tensor("dbg_x2", [128, DM], F32, kind="ExternalOutput").ap()

    BASE = 16384 + 1024
    A, Bb, C, D, E, F = (BASE, BASE + 65536, BASE + 131072, BASE + 167936,
                         BASE + 184320, BASE + 200704)

    def sb(name, shape, dtype, off):
        return nc.alloc_sbuf_tensor_at(name, shape, dtype, offset=off)

    xn_sb = sb("xn_sb", [128, 32 * TPC], BF16, A)
    K1_sb = sb("K1_sb", [128, 9216], BF16, A)
    V1_sb = sb("V1_sb", [128, 9216], BF16, A + 18432)
    l2_sb = sb("l2_sb", [128, TPC], F32, A + 36864)
    rl_sb = sb("rl_sb", [128, TPC], F32, A + 40960)

    ctx_sb = sb("ctx_sb", [128, 32 * TPC], BF16, Bb)
    cosq_sb = sb("cosq_sb", [128, TPC], F32, Bb)
    sinq_sb = sb("sinq_sb", [128, TPC], F32, Bb + 4096)
    cosk_sb = sb("cosk_sb", [128, TPC], F32, Bb + 8192)
    sink_sb = sb("sink_sb", [128, TPC], F32, Bb + 12288)
    kp_sb = sb("kp_sb", [128, 2 * 512], BF16, Bb + 16384)
    t1_sb = sb("t1_sb", [128, 512], F32, Bb + 18432)
    t2_sb = sb("t2_sb", [128, 512], F32, Bb + 20480)
    kp2_sb = sb("kp2_sb", [128, 2 * 512], BF16, Bb + 22528)
    bias_sb = sb("bias_sb", [128, DM], F32, Bb)
    t1f_sb = sb("t1f_sb", [128, 2 * 512], F32, Bb + 16384)
    gt_sb = sb("gt_sb", [128, 2 * 512], BF16, Bb + 20480)
    stile_sb = sb("stile_sb", [128, 2 * 512], BF16, Bb + 22528)
    xtile_sb = sb("xtile_sb", [128, 2 * 512], F32, Bb + 24576)
    otile_sb = sb("otile_sb", [128, 2 * 512], F32, Bb + 28672)

    xt_sb = sb("xt_sb", [128, 2 * DM], F32, C)
    vslab_sb = sb("vslab_sb", [128, 16384], BF16, C)
    K0_sb = sb("K0_sb", [128, 9216], BF16, C)
    V0_sb = sb("V0_sb", [128, 9216], BF16, C + 18432)
    slab0_sb = sb("slab0_sb", [128, 16384], BF16, C)

    x2_sb = sb("x2_sb", [128, DM], F32, D)
    qslabD_sb = sb("qslabD_sb", [128, 8192], BF16, D)
    Q0_sb = sb("Q0_sb", [128, 4096], BF16, D)
    Q1_sb = sb("Q1_sb", [128, 4096], BF16, D + 8192)
    slab1_sb = sb("slab1_sb", [128, 16384], BF16, D)

    xnf_sb = sb("xnf_sb", [128, DM], F32, E)
    wslab_sb = sb("wslab_sb", [128, 2 * 4096], BF16, E)

    cst_sb = sb("cst_sb", [128, 8], F32, F)
    ctrlb_sb = sb("ctrlb_sb", [128, 4 * NCTRLB], F32, F + 32)
    rot_sb = sb("rot_sb", [128, 128], BF16, F + 288)
    idm_sb = sb("idm_sb", [128, 128], F32, F + 544)
    ones_sb = sb("ones_sb", [128, 128], BF16, F + 1056)
    onesr_sb = sb("onesr_sb", [1, 128], F32, F + 1312)
    swb_sb = sb("swb_sb", [128, 1], F32, F + 1824)
    pt_sb = sb("pt_sb", [128, 2 * 1024], BF16, F + 1856)
    sbf_sb = sb("sbf_sb", [128, 2 * 512], BF16, F + 5952)
    stat_sb = sb("stat_sb", [128, 8], F32, F + 8000)
    dmask_sb = sb("dmask_sb", [128, 4 * 1024], BF16, A + 49152)

    St0 = nc.alloc_psum_tensor("St0", [128, 1024], F32)
    St1 = nc.alloc_psum_tensor("St1", [128, 1024], F32)
    ctxA = nc.alloc_psum_tensor("ctxA", [128, 512], F32)
    ctxB = nc.alloc_psum_tensor("ctxB", [128, 512], F32)
    lA = nc.alloc_psum_tensor("lA", [128, 512], F32)
    lB = nc.alloc_psum_tensor("lB", [128, 512], F32)
    StP = [St0, St1]
    KVs = [(K0_sb, V0_sb, Q0_sb), (K1_sb, V1_sb, Q1_sb)]

    cAP = {"shift": cst_sb[:, 0:1], "one": cst_sb[:, 1:2], "mhalf": cst_sb[:, 2:3],
           "eps": cst_sb[:, 3:4], "rdm": cst_sb[:, 4:5], "zero": cst_sb[:, 5:6],
           "mone": cst_sb[:, 6:7]}

    ENGS = ("tensor", "scalar", "vector", "gpsimd", "sync")
    ops = {e: [] for e in ENGS}
    stack = ExitStack()
    sems = {}
    cnt = {}
    for name in ("pre", "da", "db", "dc", "ga", "gb", "gc", "gd", "ge", "gf",
                 "gg", "gh", "sa", "sb", "sc", "sd",
                 "cc", "pe", "act", "dve", "gp", "bar"):
        sems[name] = stack.enter_context(nc.semaphore(name))
        cnt[name] = 0

    AUTO = {"scalar": "act", "vector": "dve", "gpsimd": "gp"}

    def emit(eng, fn, dma=False):
        if eng in AUTO and not dma:
            nm = AUTO[eng]
            idx = cnt[nm]
            sem = sems[nm]

            def wrapped(e, fn=fn, idx=idx, sem=sem):
                if idx > 0:
                    e.wait_ge(sem, idx)
                fn(e).then_inc(sem, 1)
            ops[eng].append(wrapped)
            cnt[nm] += 1
            return cnt[nm]
        ops[eng].append(fn)
        return None

    def winc(name, n=1):
        cnt[name] += n
        return cnt[name]

    def wait(eng, name, val):
        if val and val > 0:
            emit(eng, lambda e, s=sems[name], v=val: e.wait_ge(s, v), dma=True)

    def barrier(engines=ENGS):
        for nm in ("pre", "da", "db", "dc", "sa", "sb", "sc", "sd"):
            wait("sync", nm, cnt[nm])
        for nm in ("ga", "gb"):
            wait("gpsimd", nm, cnt[nm])
        wait("scalar", "act", cnt["act"])
        wait("vector", "dve", cnt["dve"])
        wait("gpsimd", "gp", cnt["gp"])
        wait("tensor", "pe", cnt["pe"])
        prev = cnt["bar"]
        for en in engines:
            emit(en, lambda e, s=sems["bar"], p=prev: e.wait_ge(s, p), dma=True)
            emit(en, lambda e, s=sems["bar"]: e.sem_inc(s, 1), dma=True)
        tgt = winc("bar", len(engines))
        for en in engines:
            wait(en, "bar", tgt)

    for dst, src in ((cst_sb, cst_in), (ctrlb_sb, ctrlb_in),
                     (rot_sb, rot_in), (idm_sb, idm_in), (swb_sb, sw_bc),
                     (cosq_sb, cosq), (sinq_sb, sinq), (cosk_sb, cosk), (sink_sb, sink)):
        emit("sync", lambda e, d=dst, s=src: e.dma_start(out=d[:, :], in_=s[:, :])
             .then_inc(sems["pre"], 16))
        winc("pre", 16)
    emit("gpsimd", lambda e: e.memset(ones_sb[:, :], 1.0))
    g_ones = emit("gpsimd", lambda e: e.memset(onesr_sb[:, :], 1.0))
    n_pre = cnt["pre"]
    for en in ENGS:
        wait(en, "pre", n_pre)
        wait(en, "gp", g_ones)


    if PHASE_LIM >= 1:
        d_xnf = [None] * 8
        d_evac = []
        pe_tr = {}

        def emit_stats(tb):
            par = tb % 2
            xt = xt_sb[:, par * DM:(par + 1) * DM]
            lane = "da" if par == 0 else "db"
            xl = xload[tb]
            wait("vector", lane, xl)
            wait("scalar", lane, xl)
            emit("vector", lambda e, xt=xt: e.reduce_sum(stat_sb[:, 0:1], xt, axis=AX))
            a_sq = emit("scalar", lambda e, xt=xt: e.square(x2_sb[:, :], xt))
            yield
            wait("vector", "act", a_sq)
            emit("vector", lambda e: e.reduce_sum(stat_sb[:, 1:2], x2_sb[:, :], axis=AX))
            yield
            emit("vector", lambda e: e.tensor_scalar_mul(stat_sb[:, 2:3], stat_sb[:, 0:1], cAP["rdm"]))
            emit("vector", lambda e: e.tensor_scalar_mul(stat_sb[:, 3:4], stat_sb[:, 1:2], cAP["rdm"]))
            emit("vector", lambda e: e.tensor_mul(stat_sb[:, 4:5], stat_sb[:, 2:3], stat_sb[:, 2:3]))
            d_var = emit("vector", lambda e: e.tensor_sub(stat_sb[:, 5:6], stat_sb[:, 3:4], stat_sb[:, 4:5]))
            wait("scalar", "dve", d_var)
            emit("scalar", lambda e: e.activation(stat_sb[:, 6:7], stat_sb[:, 5:6], AF.Ln,
                                                  bias=cAP["eps"], scale=cAP["one"]))
            a_rstd = emit("scalar", lambda e: e.activation(stat_sb[:, 7:8], stat_sb[:, 6:7], AF.Exp,
                                                           bias=cAP["zero"], scale=cAP["mhalf"]))
            yield
            wait("vector", "act", a_rstd)
            if tb >= 1:
                wait("vector", "pe", pe_tr[(tb - 1, 7)])
            d_xnf[tb] = emit("vector", lambda e, xt=xt: e.tensor_scalar(
                xnf_sb[:, :], xt, stat_sb[:, 2:3], stat_sb[:, 7:8], ALU.subtract, ALU.mult))

        def emit_tr_batch(tb, b4):
            Sp = StP[b4 % 2]
            if b4 == 0:
                wait("tensor", "dve", d_xnf[tb])
            gi = tb * 8 + b4
            if gi >= 2:
                wait("tensor", "dve", d_evac[gi - 2])
            for j in range(4):
                kc = b4 * 4 + j
                last = (j == 3)
                emit("tensor", lambda e, kc=kc, Sp=Sp, j=j, last=last: (
                    e.transpose(Sp[:, j * 128:(j + 1) * 128],
                                xnf_sb[:, kc * 128:(kc + 1) * 128], idm_sb[:, :])
                    .then_inc(sems["pe"], 1) if last else
                    e.transpose(Sp[:, j * 128:(j + 1) * 128],
                                xnf_sb[:, kc * 128:(kc + 1) * 128], idm_sb[:, :])))
            pe_tr[(tb, b4)] = winc("pe")
            wait("vector", "pe", pe_tr[(tb, b4)])
            d_evac.append(emit("vector", lambda e, tb=tb, b4=b4, Sp=Sp: e.tensor_copy(
                xn_sb.ap().rearrange("p (c n) -> p c n", n=TPC)
                     [:, 4 * b4:4 * b4 + 4, tb * 128:(tb + 1) * 128],
                Sp[:, 0:512].rearrange("p (c n) -> p c n", n=128))))

        xload = [None] * 8

        def load_x(tb):
            par = tb % 2
            lane = "da" if par == 0 else "db"
            if tb >= 2:
                wait("sync", "dve", d_xnf[tb - 2])
            emit("sync", lambda e, tb=tb, par=par, lane=lane: e.dma_start(
                out=xt_sb[:, par * DM:(par + 1) * DM],
                in_=x_in[tb * 128:(tb + 1) * 128, :]).then_inc(sems[lane], 16))
            xload[tb] = winc(lane, 16)

        load_x(0)
        load_x(1)
        gen = emit_stats(0)
        for _ in gen:
            pass
        for tb in range(8):
            if tb + 2 < 8:
                load_x(tb + 2)
            if tb + 1 < 8:
                gen = emit_stats(tb + 1)
            else:
                gen = iter(())
            emit_tr_batch(tb, 0)
            next(gen, None)
            emit_tr_batch(tb, 1)
            emit_tr_batch(tb, 2)
            next(gen, None)
            emit_tr_batch(tb, 3)
            emit_tr_batch(tb, 4)
            next(gen, None)
            emit_tr_batch(tb, 5)
            emit_tr_batch(tb, 6)
            emit_tr_batch(tb, 7)
            for _ in gen:
                pass
        barrier()
        if DBG:
            emit("sync", lambda e: e.dma_start(out=dbg["xn"][:, :], in_=xn_sb[:, :])
                 .then_inc(sems["pre"], 16))
            winc("pre", 16)
            if "xnf" in dbg and PHASE_LIM == 1:
                emit("sync", lambda e: e.dma_start(out=dbg["xnf"][:, :], in_=xnf_sb[:, :])
                     .then_inc(sems["pre"], 16))
                winc("pre", 16)
                emit("sync", lambda e: e.dma_start(out=dbg["stat"][:, :], in_=stat_sb[:, :])
                     .then_inc(sems["pre"], 16))
                winc("pre", 16)
                emit("sync", lambda e: e.dma_start(out=dbg["x2"][:, :], in_=x2_sb[:, :])
                     .then_inc(sems["pre"], 16))
                winc("pre", 16)

    kvload = [None] * 8
    av_last_kv = [None] * 8
    kv_issued = set()
    prefetch = {}

    if PHASE_LIM >= 2:
        gld_k = [None] * 8
        pe_mm = {}
        pe_rot = {}
        a_evac = {}
        d_t1 = {}
        d_t2 = {}
        d_kp2 = {}
        st_k = {}
        pe_last_of_dk = {}
        for dk in range(8):
            par = dk % 2
            if dk >= 2:
                wait("gpsimd", "pe", pe_last_of_dk[dk - 2])
            glane = "ga" if par == 0 else "gb"
            emit("gpsimd", lambda e, dk=dk, par=par, glane=glane: e.dma_start(
                out=wslab_sb[:, par * 4096:(par + 1) * 4096].rearrange("p (c n) -> p c n", n=128),
                in_=wk.rearrange("(c p) n -> p c n", p=128)[:, :, dk * 128:(dk + 1) * 128])
                .then_inc(sems[glane], 16), dma=True)
            gld_k[dk] = (glane, winc(glane, 16))
            for tb2 in range(2):
                it = dk * 2 + tb2
                half = it % 2
                wait("tensor", *gld_k[dk])
                if it >= 2:
                    wait("tensor", "act", a_evac[it - 2])
                for kc in range(32):
                    emit("tensor", lambda e, kc=kc, half=half, par=par, tb2=tb2: (
                        lambda i: i.then_inc(sems["pe"], 1) if kc == 31 else i)(
                        e.matmul(
                            St0[:, half * 512:(half + 1) * 512],
                            wslab_sb[:, par * 4096 + kc * 128:par * 4096 + (kc + 1) * 128],
                            xn_sb[:, kc * TPC + tb2 * 512:kc * TPC + (tb2 + 1) * 512],
                            start=(kc == 0), stop=(kc == 31))))
                pe_mm[it] = winc("pe")
                wait("scalar", "pe", pe_mm[it])
                if it >= 2:
                    wait("scalar", "pe", pe_rot[it - 2])
                    wait("scalar", "dve", d_t1[it - 2])
                a_evac[it] = emit("scalar", lambda e, half=half: e.copy(
                    kp_sb[:, half * 512:(half + 1) * 512], St0[:, half * 512:(half + 1) * 512]))
                wait("tensor", "act", a_evac[it])
                if it >= 2:
                    wait("tensor", "dve", d_t2[it - 2])
                emit("tensor", lambda e, half=half: e.matmul(
                    St1[:, half * 512:(half + 1) * 512], rot_sb[:, :],
                    kp_sb[:, half * 512:(half + 1) * 512], start=True, stop=True)
                    .then_inc(sems["pe"], 1))
                pe_rot[it] = winc("pe")
                pe_last_of_dk[dk] = pe_rot[it]
                wait("vector", "act", a_evac[it])
                d_t1[it] = emit("vector", lambda e, half=half, tb2=tb2: e.tensor_mul(
                    t1_sb[:, :], kp_sb[:, half * 512:(half + 1) * 512],
                    cosk_sb[:, tb2 * 512:(tb2 + 1) * 512]))
                wait("vector", "pe", pe_rot[it])
                d_t2[it] = emit("vector", lambda e, half=half, tb2=tb2: e.tensor_mul(
                    t2_sb[:, :], St1[:, half * 512:(half + 1) * 512],
                    sink_sb[:, tb2 * 512:(tb2 + 1) * 512]))
                slane = "sa" if half == 0 else "sb"
                if it >= 2:
                    wait("vector", *st_k[it - 2])
                d_kp2[it] = emit("vector", lambda e, half=half: e.tensor_add(
                    kp2_sb[:, half * 512:(half + 1) * 512], t1_sb[:, :], t2_sb[:, :]))
                wait("sync", "dve", d_kp2[it])
                emit("sync", lambda e, dk=dk, tb2=tb2, half=half, slane=slane: e.dma_start(
                    out=cc_in.ap()[dk * 128:(dk + 1) * 128, tb2 * 512:(tb2 + 1) * 512],
                    in_=kp2_sb[:, half * 512:(half + 1) * 512]).then_inc(sems[slane], 16))
                st_k[it] = (slane, winc(slane, 16))

        gld_v = [None] * 2
        pe_vmm = {}
        a_vevac = {}
        st_v = {}
        for ns in range(2):
            if ns >= 1:
                wait("gpsimd", "pe", pe_vmm[7])
            else:
                wait("gpsimd", "pe", pe_last_of_dk[6])
            emit("gpsimd", lambda e, ns=ns: e.dma_start(
                out=vslab_sb.ap().rearrange("p (c n) -> p c n", n=512),
                in_=wv.rearrange("(c p) n -> p c n", p=128)[:, :, ns * 512:(ns + 1) * 512])
                .then_inc(sems["gc"], 16), dma=True)
            gld_v[ns] = winc("gc", 16)
            for tb in range(8):
                itv = ns * 8 + tb
                half = itv % 2
                wait("tensor", "gc", gld_v[ns])
                if itv >= 2:
                    wait("tensor", "act", a_vevac[itv - 2])
                else:
                    wait("tensor", "act", a_evac[14 + itv])
                for kc in range(32):
                    emit("tensor", lambda e, kc=kc, half=half, tb=tb: (
                        lambda i: i.then_inc(sems["pe"], 1) if kc == 31 else i)(
                        e.matmul(
                            St0[:, half * 512:(half + 1) * 512],
                            xn_sb[:, kc * TPC + tb * 128:kc * TPC + tb * 128 + 128],
                            vslab_sb[:, kc * 512:(kc + 1) * 512],
                            start=(kc == 0), stop=(kc == 31))))
                pe_vmm[itv] = winc("pe")
                slane = "sa" if half == 0 else "sb"
                wait("scalar", "pe", pe_vmm[itv])
                wait("scalar", *(st_v[itv - 2] if itv >= 2 else st_k[14 + half]))
                a_vevac[itv] = emit("scalar", lambda e, half=half: e.copy(
                    kp_sb[:, half * 512:(half + 1) * 512], St0[:, half * 512:(half + 1) * 512]))
                wait("sync", "act", a_vevac[itv])
                emit("sync", lambda e, tb=tb, ns=ns, half=half, slane=slane: e.dma_start(
                    out=cc_in.ap()[1024 + tb * 128:1024 + (tb + 1) * 128, ns * 512:(ns + 1) * 512],
                    in_=kp_sb[:, half * 512:(half + 1) * 512]).then_inc(sems[slane], 16))
                st_v[itv] = (slane, winc(slane, 16))
        prefetch["cc_gate"] = (cnt["sa"], cnt["sb"])

    def issue_kv_load(kvh):
        par = kvh % 2
        Ksb, Vsb, Qsb = KVs[par]
        lane = "da" if par == 0 else "db"
        if kvh < 2:
            wait("sync", "pre", cnt["pre"])
            wait("sync", "dc", cnt["dc"])
            wait("sync", "cc", 1)
            wait("sync", "pe", pe_last_of_qh[7])
            wait("sync", *st_q[2 * (4 * (kvh + 1)) - 1])
        else:
            wait("sync", "pe", av_last_kv[kvh - 2])
        for r in range(NCORES):
            emit("sync", lambda e, r=r, kvh=kvh, Ksb=Ksb, lane=lane: e.dma_start(
                out=Ksb[:, r * 1024:(r + 1) * 1024],
                in_=cc_out.ap()[(2048 * r + kvh * 128):(2048 * r + (kvh + 1) * 128), :])
                .then_inc(sems[lane], 16))
            winc(lane, 16)
        emit("sync", lambda e, kvh=kvh, Ksb=Ksb, lane=lane: e.dma_start(
            out=Ksb[:, 8192:9216],
            in_=cc_in.ap()[kvh * 128:(kvh + 1) * 128, :]).then_inc(sems[lane], 16))
        winc(lane, 16)
        for r in range(NCORES):
            emit("sync", lambda e, r=r, kvh=kvh, Vsb=Vsb, lane=lane: e.dma_start(
                out=Vsb[:, r * 1024:(r + 1) * 1024].rearrange("p (c n) -> p c n", n=128),
                in_=cc_out.ap().rearrange("(a p) n -> p a n", p=128)
                    [:, 16 * r + 8:16 * r + 16, kvh * 128:(kvh + 1) * 128])
                .then_inc(sems[lane], 16))
            winc(lane, 16)
        emit("sync", lambda e, kvh=kvh, Vsb=Vsb, lane=lane: e.dma_start(
            out=Vsb[:, 8192:9216].rearrange("p (c n) -> p c n", n=128),
            in_=cc_in.ap().rearrange("(a p) n -> p a n", p=128)
                [:, 8:16, kvh * 128:(kvh + 1) * 128]).then_inc(sems[lane], 16))
        winc(lane, 16)
        emit("sync", lambda e, kvh=kvh, Qsb=Qsb, lane=lane: e.dma_start(
            out=Qsb.ap().rearrange("p (c n) -> p c n", n=TPC),
            in_=q_scr.ap().rearrange("(c p) n -> p c n", p=128)[:, 4 * kvh:4 * kvh + 4, :])
            .then_inc(sems[lane], 16))
        kvload[kvh] = ("da" if par == 0 else "db", winc(lane, 16))
        kv_issued.add(kvh)

    if PHASE_LIM >= 3:
        emit("sync", lambda e: e.dma_start(out=xn_scr.ap()[:, :], in_=xn_sb[:, :])
             .then_inc(sems["dc"], 16))
        winc("dc", 16)
        gld_q = [None] * 32
        pe_qmm = {}
        pe_qrot = {}
        a_qevac = {}
        dq_t1 = {}
        dq_t2 = {}
        dq_kp2 = {}
        st_q = {}
        pe_last_of_qh = {}
        QLANES = ["ga", "gb", "gc", "gd", "ge", "gf", "gg", "gh"]

        def qslab_ap(slot):
            if slot < 2:
                return wslab_sb[:, slot * 4096:(slot + 1) * 4096]
            if slot < 4:
                return qslabD_sb[:, (slot - 2) * 4096:(slot - 1) * 4096]
            return vslab_sb[:, (slot - 4) * 4096:(slot - 3) * 4096]

        for qh in range(32):
            slot = qh % 8
            glane = QLANES[slot]
            if qh >= 8:
                wait("gpsimd", "pe", pe_last_of_qh[qh - 8])
            elif slot < 2:
                wait("gpsimd", "pe", pe_last_of_dk[6 + slot])
            elif slot >= 4 or slot in (2, 3):
                wait("gpsimd", "pe", pe_vmm[15])
            emit("gpsimd", lambda e, qh=qh, slot=slot, glane=glane: e.dma_start(
                out=qslab_ap(slot).rearrange("p (c n) -> p c n", n=128),
                in_=wq.rearrange("(c p) n -> p c n", p=128)[:, :, qh * 128:(qh + 1) * 128])
                .then_inc(sems[glane], 16), dma=True)
            gld_q[qh] = (glane, winc(glane, 16))
            if qh == 24:
                wait("gpsimd", "sa", prefetch["cc_gate"][0])
                wait("gpsimd", "sb", prefetch["cc_gate"][1])
                emit("gpsimd", lambda e: e.collective_compute(
                    "AllGather", mybir.AluOpType.bypass,
                    replica_groups=[list(range(NCORES))],
                    ins=[cc_in.ap()], outs=[cc_out.ap()]).then_inc(sems["cc"]), dma=True)
                cnt["cc"] = 1
            for tb2 in range(2):
                it = qh * 2 + tb2
                half = it % 2
                wait("tensor", *gld_q[qh])
                if it >= 2:
                    wait("tensor", "act", a_qevac[it - 2])
                else:
                    wait("tensor", "act", a_vevac[14 + it])
                slot = qh % 8
                for kc in range(32):
                    emit("tensor", lambda e, kc=kc, half=half, slot=slot, tb2=tb2: (
                        lambda i: i.then_inc(sems["pe"], 1) if kc == 31 else i)(
                        e.matmul(
                            St0[:, half * 512:(half + 1) * 512],
                            qslab_ap(slot)[:, kc * 128:(kc + 1) * 128],
                            xn_sb[:, kc * TPC + tb2 * 512:kc * TPC + (tb2 + 1) * 512],
                            start=(kc == 0), stop=(kc == 31))))
                pe_qmm[it] = winc("pe")
                wait("scalar", "pe", pe_qmm[it])
                if it >= 2:
                    wait("scalar", "pe", pe_qrot[it - 2])
                    wait("scalar", "dve", dq_t1[it - 2])
                else:
                    wait("scalar", *st_v[14 + it])
                a_qevac[it] = emit("scalar", lambda e, half=half: e.copy(
                    kp_sb[:, half * 512:(half + 1) * 512], St0[:, half * 512:(half + 1) * 512]))
                wait("tensor", "act", a_qevac[it])
                if it >= 2:
                    wait("tensor", "dve", dq_t2[it - 2])
                else:
                    wait("tensor", "dve", d_t2[14 + it])
                emit("tensor", lambda e, half=half: e.matmul(
                    St1[:, half * 512:(half + 1) * 512], rot_sb[:, :],
                    kp_sb[:, half * 512:(half + 1) * 512], start=True, stop=True)
                    .then_inc(sems["pe"], 1))
                pe_qrot[it] = winc("pe")
                pe_last_of_qh[qh] = pe_qrot[it]
                wait("vector", "act", a_qevac[it])
                dq_t1[it] = emit("vector", lambda e, half=half, tb2=tb2: e.tensor_mul(
                    t1_sb[:, :], kp_sb[:, half * 512:(half + 1) * 512],
                    cosq_sb[:, tb2 * 512:(tb2 + 1) * 512]))
                wait("vector", "pe", pe_qrot[it])
                dq_t2[it] = emit("vector", lambda e, half=half, tb2=tb2: e.tensor_mul(
                    t2_sb[:, :], St1[:, half * 512:(half + 1) * 512],
                    sinq_sb[:, tb2 * 512:(tb2 + 1) * 512]))
                slane = "sc" if half == 0 else "sd"
                if it >= 2:
                    wait("vector", *st_q[it - 2])
                else:
                    wait("vector", *st_k[14 + it])
                dq_kp2[it] = emit("vector", lambda e, half=half: e.tensor_add(
                    kp2_sb[:, half * 512:(half + 1) * 512], t1_sb[:, :], t2_sb[:, :]))
                wait("sync", "dve", dq_kp2[it])
                emit("sync", lambda e, qh=qh, tb2=tb2, half=half, slane=slane: e.dma_start(
                    out=q_scr.ap()[qh * 128:(qh + 1) * 128, tb2 * 512:(tb2 + 1) * 512],
                    in_=kp2_sb[:, half * 512:(half + 1) * 512]).then_inc(sems[slane], 16))
                st_q[it] = (slane, winc(slane, 16))
        if PHASE_LIM >= 4:
            issue_kv_load(0)
            issue_kv_load(1)
            emit("sync", lambda e: e.dma_start(out=dmask_sb[:, :], in_=dmask_in[:, :])
                 .then_inc(sems["dc"], 16))
            prefetch["dmask"] = winc("dc", 16)
        barrier()
        if DBG:
            emit("sync", lambda e: e.dma_start(out=dbg["q"][:, :], in_=q_scr.ap()[:, :])
                 .then_inc(sems["pre"], 16))
            winc("pre", 16)

    if PHASE_LIM >= 4:
        batches = []
        for g in range(4):
            for bi in range(SG[g] // 4):
                batches.append((g, bi, [(g, 4 * bi + jj) for jj in range(4)]))
        assert len(batches) == 40

        sched = []
        for kvh in range(8):
            for qhl in range(4):
                for bid, (g, bi, slots) in enumerate(batches):
                    sched.append((kvh, qhl, bid, g, bi, slots))
        NB = len(sched)

        st_done = {}
        exp_done = {}
        gate_av = {}
        av_done = {}
        evacA_done = {}
        evacB_done = {}

        def emit_st(i):
            kvh, qhl, bid, g, bi, slots = sched[i]
            if kvh not in kv_issued:
                issue_kv_load(kvh)
            if bid == 0 and qhl == 0:
                wait("tensor", *kvload[kvh])
            par = kvh % 2
            Ksb, Vsb, Qsb = KVs[par]
            Sp = StP[i % 2]
            if i >= 2:
                wait("tensor", "act", exp_done[i - 2])
            for idx, (g_, j) in enumerate(slots):
                koff = slot_cols(g_, j)
                emit("tensor", lambda e, idx=idx, koff=koff, g_=g_, qhl=qhl, Sp=Sp,
                     Ksb=Ksb, Qsb=Qsb, last=(idx == 3): (
                         lambda x: x.then_inc(sems["pe"], 1) if last else x)(
                         e.matmul(
                             Sp[:, idx * 256:(idx + 1) * 256],
                             Ksb[:, koff:koff + 128],
                             Qsb[:, qhl * TPC + g_ * 256:qhl * TPC + (g_ + 1) * 256],
                             start=True, stop=True)))
            st_done[i] = winc("pe")

        wait("vector", "dc", prefetch["dmask"])
        emit_st(0)
        emit_st(1)
        for i in range(NB):
            kvh, qhl, bid, g, bi, slots = sched[i]
            qh = 4 * kvh + qhl
            par = kvh % 2
            Ksb, Vsb, Qsb = KVs[par]
            pb = i % 2
            Sp = StP[pb]
            wait("scalar", "pe", st_done[i])
            if i >= 2:
                wait("scalar", "pe", av_done[i - 2])
            exp_done[i] = emit("scalar", lambda e, pb=pb, Sp=Sp: e.activation(
                pt_sb[:, pb * 1024:(pb + 1) * 1024], Sp[:, :], AF.Exp,
                bias=cAP["shift"], scale=cAP["one"]))
            q0 = pb * 1024
            if bi == 0:
                wait("vector", "act", exp_done[i])
                ev = emit("vector", lambda e, q0=q0, g=g: e.tensor_mul(
                    pt_sb[:, q0:q0 + 1024], pt_sb[:, q0:q0 + 1024],
                    dmask_sb[:, g * 1024:(g + 1) * 1024]))
                gate_av[i] = [("dve", ev)]
            elif (g, bi) in CTRLB_IX:
                cb = CTRLB_IX[(g, bi)]
                base = ctrlb_sb.ap()
                in1 = bass.AP(tensor=base.tensor, offset=base.offset + 4 * cb,
                              ap=[[base.ap[0][0], 128], [1, 4], [0, 256]])
                wait("vector", "act", exp_done[i])
                ev = emit("vector", lambda e, q0=q0, in1=in1: e.tensor_tensor(
                    pt_sb.ap().rearrange("p (c n) -> p c n", n=256)[:, q0 // 256:q0 // 256 + 4, :],
                    pt_sb.ap().rearrange("p (c n) -> p c n", n=256)[:, q0 // 256:q0 // 256 + 4, :],
                    in1, ALU.mult))
                gate_av[i] = [("dve", ev)]
            else:
                gate_av[i] = [("act", exp_done[i])]
            if i + 2 < NB:
                emit_st(i + 2)
            for gt in gate_av[i]:
                wait("tensor", *gt)
            if bid == 0 and qh > 0:
                wait("tensor", "dve", evacA_done[qh - 1])
            if bid == 12 and qh > 0:
                wait("tensor", "dve", evacB_done[qh - 1])
            ctxp = ctxA if g < 2 else ctxB
            c0 = (g % 2) * 256
            lp = lA if g < 2 else lB
            l0 = (g % 2) * 256
            first = (bi == 0)
            last = (bi == SG[g] // 4 - 1)
            for idx, (g_, j) in enumerate(slots):
                voff = slot_cols(g_, j)
                emit("tensor", lambda e, idx=idx, voff=voff, ctxp=ctxp, pb=pb, c0=c0,
                     Vsb=Vsb, fl=(j == 0), ll=(j == SG[g] - 1): e.matmul(
                         ctxp[:, c0:c0 + 256], Vsb[:, voff:voff + 128],
                         pt_sb[:, pb * 1024 + idx * 256:pb * 1024 + (idx + 1) * 256],
                         start=fl, stop=ll))
            for idx in range(4):
                emit("tensor", lambda e, idx=idx, lp=lp, pb=pb, l0=l0,
                     fl=(first and idx == 0), ll=(last and idx == 3),
                     lastb=(idx == 3): (
                         lambda x: x.then_inc(sems["pe"], 1) if lastb else x)(
                         e.matmul(
                             lp[:, l0:l0 + 256], ones_sb[:, :],
                             pt_sb[:, pb * 1024 + idx * 256:pb * 1024 + (idx + 1) * 256],
                             start=fl, stop=ll)))
            av_done[i] = winc("pe")
            if bid == 39:
                av_last_kv[kvh] = av_done[i]
            if bid == 11:
                wait("scalar", "pe", av_done[i])
                if qh > 0:
                    wait("scalar", "dve", evacA_done[qh - 1])
                emit("scalar", lambda e: e.activation(
                    l2_sb[:, 0:512], lA[:, :], AF.Ln, bias=cAP["zero"], scale=cAP["one"]))
                a_rl = emit("scalar", lambda e: e.activation(
                    rl_sb[:, 0:512], l2_sb[:, 0:512], AF.Exp, bias=cAP["zero"], scale=cAP["mone"]))
                wait("vector", "act", a_rl)
                evacA_done[qh] = emit("vector", lambda e, qh=qh: e.tensor_mul(
                    ctx_sb[:, qh * TPC:qh * TPC + 512], ctxA[:, :], rl_sb[:, 0:512]))
            if bid == 39:
                wait("scalar", "pe", av_done[i])
                if qh > 0:
                    wait("scalar", "dve", evacB_done[qh - 1])
                emit("scalar", lambda e: e.activation(
                    l2_sb[:, 512:1024], lB[:, :], AF.Ln, bias=cAP["zero"], scale=cAP["one"]))
                b_rl = emit("scalar", lambda e: e.activation(
                    rl_sb[:, 512:1024], l2_sb[:, 512:1024], AF.Exp, bias=cAP["zero"], scale=cAP["mone"]))
                wait("vector", "act", b_rl)
                evacB_done[qh] = emit("vector", lambda e, qh=qh: e.tensor_mul(
                    ctx_sb[:, qh * TPC + 512:(qh + 1) * TPC], ctxB[:, :], rl_sb[:, 512:1024]))
        if PHASE_LIM >= 5:
            wait("gpsimd", "pe", av_last_kv[6])
            emit("gpsimd", lambda e: e.dma_start(
                out=slab0_sb.ap().rearrange("p (c n) -> p c n", n=512),
                in_=wo.rearrange("(c p) n -> p c n", p=128)[:, :, 0:512])
                .then_inc(sems["ga"], 16), dma=True)
            prefetch["wo0"] = ("ga", winc("ga", 16))
        barrier()
        if DBG:
            emit("sync", lambda e: e.dma_start(out=dbg["ctx"][:, :], in_=ctx_sb[:, :])
                 .then_inc(sems["pre"], 16))
            winc("pre", 16)
            emit("sync", lambda e: e.dma_start(out=dbg["cc"][:, :], in_=cc_out.ap()[:, :])
                 .then_inc(sems["pre"], 16))
            winc("pre", 16)

    if PHASE_LIM >= 5:
        emit("sync", lambda e: e.dma_start(out=xn_sb[:, :], in_=xn_scr.ap()[:, :])
             .then_inc(sems["dc"], 16))
        xn_reload = winc("dc", 16)
        slabs = [slab0_sb, slab1_sb]
        gld_wo = [None] * 8
        gld_wo[0] = prefetch["wo0"]
        emit("gpsimd", lambda e: e.dma_start(
            out=slab1_sb.ap().rearrange("p (c n) -> p c n", n=512),
            in_=wo.rearrange("(c p) n -> p c n", p=128)[:, :, 512:1024])
            .then_inc(sems["gb"], 16), dma=True)
        gld_wo[1] = ("gb", winc("gb", 16))
        wo_tile = {}
        s_evac = {}
        s_store = {}
        for ns in range(8):
            parn = ns % 2
            glane = "ga" if parn == 0 else "gb"
            if ns >= 2:
                wait("gpsimd", "pe", wo_tile[(ns - 2) * 8 + 7])
                emit("gpsimd", lambda e, ns=ns, parn=parn, glane=glane: e.dma_start(
                    out=slabs[parn].ap().rearrange("p (c n) -> p c n", n=512),
                    in_=wo.rearrange("(c p) n -> p c n", p=128)[:, :, ns * 512:(ns + 1) * 512])
                    .then_inc(sems[glane], 16), dma=True)
                gld_wo[ns] = (glane, winc(glane, 16))
            for tb in range(8):
                it = ns * 8 + tb
                half = it % 2
                if tb == 0:
                    wait("tensor", *gld_wo[ns])
                if it >= 2:
                    wait("tensor", "act", s_evac[it - 2])
                for cd in range(32):
                    emit("tensor", lambda e, cd=cd, half=half, tb=tb, parn=parn: (
                        lambda i: i.then_inc(sems["pe"], 1) if cd == 31 else i)(
                        e.matmul(
                            St0[:, half * 512:(half + 1) * 512],
                            ctx_sb[:, cd * TPC + tb * 128:cd * TPC + tb * 128 + 128],
                            slabs[parn][:, cd * 512:(cd + 1) * 512],
                            start=(cd == 0), stop=(cd == 31))))
                wo_tile[it] = winc("pe")
                slane = "sa" if half == 0 else "sb"
                wait("scalar", "pe", wo_tile[it])
                if it >= 2:
                    wait("scalar", *s_store[it - 2])
                s_evac[it] = emit("scalar", lambda e, half=half: e.copy(
                    sbf_sb[:, half * 512:(half + 1) * 512], St0[:, half * 512:(half + 1) * 512]))
                wait("sync", "act", s_evac[it])
                emit("sync", lambda e, tb=tb, ns=ns, half=half, slane=slane: e.dma_start(
                    out=s_scr.ap()[tb * 128:(tb + 1) * 128, ns * 512:(ns + 1) * 512],
                    in_=sbf_sb[:, half * 512:(half + 1) * 512]).then_inc(sems[slane], 16))
                s_store[it] = (slane, winc(slane, 16))
        if PHASE_LIM >= 6:
            wait("gpsimd", "pe", wo_tile[6 * 8 + 7])
            emit("gpsimd", lambda e: e.dma_start(
                out=slab0_sb.ap().rearrange("p (c n) -> p c n", n=512),
                in_=qwt.rearrange("(c p) n -> p c n", p=128)[:, :, 0:512])
                .then_inc(sems["ga"], 16), dma=True)
            prefetch["ffn0"] = ("ga", winc("ga", 16))
        barrier()
        if DBG:
            emit("sync", lambda e: e.dma_start(out=dbg["s"][:, :], in_=s_scr.ap()[:, :])
                 .then_inc(sems["pre"], 16))
            winc("pre", 16)

    if PHASE_LIM >= 6:
        emit("sync", lambda e: e.dma_start(out=bias_sb[:, :], in_=bias_bc[:, :])
             .then_inc(sems["dc"], 16))
        bias_ld = winc("dc", 16)
        slabs = [slab0_sb, slab1_sb]
        gld_ffn = [None] * 8
        gld_ffn[0] = prefetch["ffn0"]
        emit("gpsimd", lambda e: e.dma_start(
            out=slab1_sb.ap().rearrange("p (c n) -> p c n", n=512),
            in_=qwt.rearrange("(c p) n -> p c n", p=128)[:, :, 512:1024])
            .then_inc(sems["gb"], 16), dma=True)
        gld_ffn[1] = ("gb", winc("gb", 16))
        y_tile = {}
        stt_done = {}
        silu_done = {}
        comb_done = {}
        o_store = {}
        tload = {}
        for ns in range(8):
            parn = ns % 2
            glane = "ga" if parn == 0 else "gb"
            if ns >= 2:
                wait("gpsimd", "pe", y_tile[(ns - 2) * 8 + 7])
                emit("gpsimd", lambda e, ns=ns, parn=parn, glane=glane: e.dma_start(
                    out=slabs[parn].ap().rearrange("p (c n) -> p c n", n=512),
                    in_=qwt.rearrange("(c p) n -> p c n", p=128)[:, :, ns * 512:(ns + 1) * 512])
                    .then_inc(sems[glane], 16), dma=True)
                gld_ffn[ns] = (glane, winc(glane, 16))
            for tb in range(8):
                it = ns * 8 + tb
                half = it % 2
                if it >= 2:
                    wait("sync", "dve", comb_done[it - 2])
                lane = "da" if half == 0 else "db"
                emit("sync", lambda e, tb=tb, ns=ns, half=half, lane=lane: e.dma_start(
                    out=stile_sb[:, half * 512:(half + 1) * 512],
                    in_=s_scr.ap()[tb * 128:(tb + 1) * 128, ns * 512:(ns + 1) * 512])
                    .then_inc(sems[lane], 16))
                winc(lane, 16)
                emit("sync", lambda e, tb=tb, ns=ns, half=half, lane=lane: e.dma_start(
                    out=xtile_sb[:, half * 512:(half + 1) * 512],
                    in_=x_in[tb * 128:(tb + 1) * 128, ns * 512:(ns + 1) * 512])
                    .then_inc(sems[lane], 16))
                tload[it] = (lane, winc(lane, 16))
                if tb == 0:
                    wait("tensor", *gld_ffn[ns])
                    if ns == 0:
                        wait("tensor", "dc", xn_reload)
                if it >= 2:
                    wait("tensor", "dve", stt_done[it - 2])
                for kc in range(32):
                    emit("tensor", lambda e, kc=kc, half=half, tb=tb, parn=parn: (
                        lambda i: i.then_inc(sems["pe"], 1) if kc == 31 else i)(
                        e.matmul(
                            St0[:, half * 512:(half + 1) * 512],
                            xn_sb[:, kc * TPC + tb * 128:kc * TPC + tb * 128 + 128],
                            slabs[parn][:, kc * 512:(kc + 1) * 512],
                            start=(kc == 0), stop=(kc == 31))))
                y_tile[it] = winc("pe")
                wait("vector", "pe", y_tile[it])
                if it == 0:
                    wait("vector", "dc", bias_ld)
                    
                if it >= 2:
                    wait("vector", "act", silu_done[it - 2])
                stt_done[it] = emit("vector", lambda e, half=half, ns=ns: e.scalar_tensor_tensor(
                    t1f_sb[:, half * 512:(half + 1) * 512], St0[:, half * 512:(half + 1) * 512],
                    swb_sb[:, 0:1], bias_sb[:, ns * 512:(ns + 1) * 512], ALU.mult, ALU.add))
                wait("scalar", "dve", stt_done[it])
                if it >= 2:
                    wait("scalar", "dve", comb_done[it - 2])
                silu_done[it] = emit("scalar", lambda e, half=half: e.activation(
                    gt_sb[:, half * 512:(half + 1) * 512], t1f_sb[:, half * 512:(half + 1) * 512],
                    AF.Silu, bias=cAP["zero"], scale=cAP["one"]))
                wait("vector", "act", silu_done[it])
                wait("vector", *tload[it])
                emit("vector", lambda e, half=half: e.tensor_mul(
                    t1f_sb[:, half * 512:(half + 1) * 512], gt_sb[:, half * 512:(half + 1) * 512],
                    stile_sb[:, half * 512:(half + 1) * 512]))
                if it >= 2:
                    wait("vector", *o_store[it - 2])
                comb_done[it] = emit("vector", lambda e, half=half: e.tensor_add(
                    otile_sb[:, half * 512:(half + 1) * 512],
                    t1f_sb[:, half * 512:(half + 1) * 512],
                    xtile_sb[:, half * 512:(half + 1) * 512]))
                slane = "sa" if half == 0 else "sb"
                wait("sync", "dve", comb_done[it])
                emit("sync", lambda e, tb=tb, ns=ns, half=half, slane=slane: e.dma_start(
                    out=out[tb * 128:(tb + 1) * 128, ns * 512:(ns + 1) * 512],
                    in_=otile_sb[:, half * 512:(half + 1) * 512]).then_inc(sems[slane], 16))
                o_store[it] = (slane, winc(slane, 16))
    for nm in ("sa", "sb", "sc", "sd", "pre", "dc"):
        wait("sync", nm, cnt[nm])

    with nc.Block() as block:
        @block.tensor
        def _(e):
            for fn in ops["tensor"]:
                fn(e)

        @block.scalar
        def _(e):
            for fn in ops["scalar"]:
                fn(e)

        @block.vector
        def _(e):
            for fn in ops["vector"]:
                fn(e)

        @block.gpsimd
        def _(e):
            for fn in ops["gpsimd"]:
                fn(e)

        @block.sync
        def _(e):
            for fn in ops["sync"]:
                fn(e)

    stack.close()
    return nc


_NC_CACHE = None


def get_nc():
    global _NC_CACHE
    if _NC_CACHE is None:
        _NC_CACHE = build_nc()
    return _NC_CACHE


def make_in_maps(inputs):
    x = np.asarray(inputs["L0_attn_resid"])
    wq = np.ascontiguousarray(np.asarray(inputs["L0_attn_Wq"], dtype=np.float32))
    wk = np.ascontiguousarray(np.asarray(inputs["L0_attn_Wk"], dtype=np.float32))
    wv = np.ascontiguousarray(np.asarray(inputs["L0_attn_Wv"], dtype=np.float32))
    wo = np.ascontiguousarray(np.asarray(inputs["L0_attn_Wo"], dtype=np.float32))
    qw = np.asarray(inputs["L0_ffn_qw"])
    sw = np.asarray(inputs["L0_ffn_sw"], dtype=np.float32)
    bias = np.asarray(inputs["L0_ffn_bias"], dtype=np.float32)

    qwt = np.ascontiguousarray(qw.T.astype(np.int32))
    bias_bc = np.ascontiguousarray(np.broadcast_to(bias[None, :], (128, DM)).astype(np.float32))
    sw_bc = np.full((128, 1), float(sw), np.float32)
    cst, rot, idm = host_consts()

    xf = np.asarray(x, dtype=np.float32).reshape(T, DM)
    in_maps = []
    orders = []
    for core in range(NCORES):
        tabs, pos = host_tables(core)
        orders.append(pos)
        shard = np.ascontiguousarray(xf[pos])
        in_maps.append(dict(
            x=shard, wq=wq, wk=wk, wv=wv, wo=wo, qwt=qwt,
            bias_bc=bias_bc, sw_bc=sw_bc,
            cosq=tabs["cosq"], sinq=tabs["sinq"], cosk=tabs["cosk"], sink=tabs["sink"],
            cst=cst, ctrlb=np.ascontiguousarray(tabs["ctrlb"]),
            dmask=np.ascontiguousarray(tabs["dmask"]), rot=rot, idm=idm,
        ))
    return in_maps, orders


def kernel(**inputs):
    nc = get_nc()
    in_maps, orders = make_in_maps(inputs)
    res = run_bass_kernel_spmd(nc, in_maps, core_ids=list(range(NCORES)))
    full = np.empty((T, DM), np.float32)
    for core in range(NCORES):
        full[orders[core]] = np.asarray(res.results[core]["out"], dtype=np.float32)
    return full.reshape(1, T, DM)


# version 16 (final)
# speedup vs baseline: 1.0232x; 1.0026x over previous
import math
from contextlib import ExitStack

import numpy as np
import ml_dtypes

import concourse.bass as bass
import concourse.mybir as mybir
from concourse.bass_utils import run_bass_kernel_spmd

F32 = mybir.dt.float32
BF16 = mybir.dt.bfloat16
I32 = mybir.dt.int32
AF = mybir.ActivationFunctionType
ALU = mybir.AluOpType
BF = ml_dtypes.bfloat16
AX = mybir.AxisListType.X

NCORES = 8
T = 8192
DM = 4096
NKV = 8
NQH = 32
DH = 128
BLK = 256
TPC = 1024
SG = [16, 32, 48, 64]
CSHIFT = -24.0
MINV = [0, 16, 32, 48]
CTRL_COLS = [(g, j) for g in range(4) for j in range(2, SG[g]) if j - 2 >= MINV[g]]
NCTRL = len(CTRL_COLS)
CTRL_IX = {gj: c for c, gj in enumerate(CTRL_COLS)}


def core_blocks(i):
    return [i, 15 - i, 16 + i, 31 - i]


def owner_of_block(b):
    if b <= 7:
        return b, 0
    if b <= 15:
        return 15 - b, 1
    if b <= 23:
        return b - 16, 2
    return 31 - b, 3


def slot_cols(g, j):
    if j < 2:
        return 8192 + g * 256 + j * 128
    m = j - 2
    b, sub = m >> 1, m & 1
    r, lb = owner_of_block(b)
    return r * 1024 + lb * 256 + sub * 128


def rope_cos_sin(positions):
    idx = np.arange(0, DH, 2, dtype=np.float64) / DH
    freq = 1.0 / (2.0 * math.pi) * (500000.0 ** (-idx))
    factor, lo, hi = 32.0, 1.0, 4.0
    L0 = 8192
    freq_low = lo / L0
    freq_high = hi / L0
    freq_scaled = np.where(freq < freq_low, freq / factor, freq)
    smooth = np.clip((L0 * freq - lo) / (hi - lo), 0.0, 1.0)
    freq_smooth = (1.0 - smooth) * (freq / factor) + smooth * freq
    is_mid = (freq >= freq_low) & (freq <= freq_high)
    freq = np.where(is_mid, freq_smooth, freq_scaled)
    phase = 2.0 * math.pi * positions.astype(np.float64)[:, None] * freq[None, :]
    emb = np.concatenate([phase, phase], axis=-1)
    return np.cos(emb).T.astype(np.float32), np.sin(emb).T.astype(np.float32)


DIAG_BATCHES = [(g, 0) for g in range(4)]
CTRL_BATCHES = []
for g in range(4):
    for bi in range(1, SG[g] // 4):
        if any((4 * bi + idx) - 2 >= MINV[g] for idx in range(4)):
            CTRL_BATCHES.append((g, bi))
CTRLB_IX = {gb: c for c, gb in enumerate(CTRL_BATCHES)}
NCTRLB = len(CTRL_BATCHES)


def slot_keep(blocks, g, j):
    if j < 2:
        return 1.0
    return 1.0 if (j - 2) < 2 * blocks[g] else 0.0


def host_tables(core):
    blocks = core_blocks(core)
    pos = np.concatenate([np.arange(b * BLK, (b + 1) * BLK) for b in blocks])
    cos, sin = rope_cos_sin(pos)
    s128 = np.float32(1.0 / math.sqrt(DH))
    ctrlb = np.ones((128, 4 * NCTRLB), np.float32)
    for (g, bi), c in CTRLB_IX.items():
        for idx in range(4):
            ctrlb[:, 4 * c + idx] = slot_keep(blocks, g, 4 * bi + idx)
    p = np.arange(128)[:, None]
    f = np.arange(256)[None, :]
    st0 = (f >= p).astype(np.float32)
    st1 = (f >= 128 + p).astype(np.float32)
    dmask = np.ones((128, 4 * 1024), np.float32)
    for g in range(4):
        dmask[:, g * 1024:g * 1024 + 256] = st0
        dmask[:, g * 1024 + 256:g * 1024 + 512] = st1
        dmask[:, g * 1024 + 512:g * 1024 + 768] = slot_keep(blocks, g, 2)
        dmask[:, g * 1024 + 768:g * 1024 + 1024] = slot_keep(blocks, g, 3)
    return dict(cosq=np.ascontiguousarray(cos * s128),
                sinq=np.ascontiguousarray(sin * s128),
                cosk=cos, sink=sin, ctrlb=ctrlb,
                dmask=dmask.astype(BF)), pos


def host_consts():
    cst = np.zeros((128, 8), np.float32)
    cst[:, 0] = CSHIFT
    cst[:, 1] = 1.0
    cst[:, 2] = -0.5
    cst[:, 3] = 1e-5
    cst[:, 4] = 1.0 / DM
    cst[:, 5] = 0.0
    cst[:, 6] = -1.0
    rot = np.zeros((128, 128), np.float32)
    rot[np.arange(64) + 64, np.arange(64)] = -1.0
    rot[np.arange(64), np.arange(64) + 64] = 1.0
    idm = np.eye(128, dtype=np.float32)
    return cst, rot.astype(BF), idm


DBG = False
DBG2 = False
PHASE_LIM = 99


def build_nc():
    nc = bass.Bass("TRN2", target_bir_lowering=False)

    x_in = nc.dram_tensor("x", [TPC, DM], F32, kind="ExternalInput").ap()
    wq = nc.dram_tensor("wq", [DM, DM], F32, kind="ExternalInput").ap()
    wk = nc.dram_tensor("wk", [DM, NKV * DH], F32, kind="ExternalInput").ap()
    wv = nc.dram_tensor("wv", [DM, NKV * DH], F32, kind="ExternalInput").ap()
    wo = nc.dram_tensor("wo", [DM, DM], F32, kind="ExternalInput").ap()
    qwt = nc.dram_tensor("qwt", [DM, DM], I32, kind="ExternalInput").ap()
    bias_bc = nc.dram_tensor("bias_bc", [128, DM], F32, kind="ExternalInput").ap()
    sw_bc = nc.dram_tensor("sw_bc", [128, 1], F32, kind="ExternalInput").ap()
    cosq = nc.dram_tensor("cosq", [128, TPC], F32, kind="ExternalInput").ap()
    sinq = nc.dram_tensor("sinq", [128, TPC], F32, kind="ExternalInput").ap()
    cosk = nc.dram_tensor("cosk", [128, TPC], F32, kind="ExternalInput").ap()
    sink = nc.dram_tensor("sink", [128, TPC], F32, kind="ExternalInput").ap()
    cst_in = nc.dram_tensor("cst", [128, 8], F32, kind="ExternalInput").ap()
    ctrlb_in = nc.dram_tensor("ctrlb", [128, 4 * NCTRLB], F32, kind="ExternalInput").ap()
    dmask_in = nc.dram_tensor("dmask", [128, 4 * 1024], BF16, kind="ExternalInput").ap()
    rot_in = nc.dram_tensor("rot", [128, 128], BF16, kind="ExternalInput").ap()
    idm_in = nc.dram_tensor("idm", [128, 128], F32, kind="ExternalInput").ap()
    out = nc.dram_tensor("out", [TPC, DM], F32, kind="ExternalOutput").ap()

    cc_in = nc.dram_tensor("cc_in", [2 * TPC, 1024], BF16)
    cc_outK = nc.dram_tensor("cc_outK", [NCORES * TPC, 1024], BF16)
    cc_outV = nc.dram_tensor("cc_outV", [NCORES * TPC, 1024], BF16)
    q_scr = nc.dram_tensor("q_scr", [NQH * DH, TPC], BF16)
    xn_scr = nc.dram_tensor("xn_scr", [128, 32 * TPC], BF16)
    s_scr = nc.dram_tensor("s_scr", [TPC, DM], BF16)
    dbg = {}
    if DBG:
        dbg["xn"] = nc.dram_tensor("dbg_xn", [128, 32 * TPC], BF16, kind="ExternalOutput").ap()
        dbg["cc"] = nc.dram_tensor("dbg_cc", [NCORES * 2 * TPC, 1024], BF16, kind="ExternalOutput").ap()
        dbg["q"] = nc.dram_tensor("dbg_q", [NQH * DH, TPC], BF16, kind="ExternalOutput").ap()
        dbg["ctx"] = nc.dram_tensor("dbg_ctx", [128, 32 * TPC], BF16, kind="ExternalOutput").ap()
        dbg["s"] = nc.dram_tensor("dbg_s", [TPC, DM], BF16, kind="ExternalOutput").ap()
        if DBG2:
            dbg["xnf"] = nc.dram_tensor("dbg_xnf", [128, DM], F32, kind="ExternalOutput").ap()
            dbg["stat"] = nc.dram_tensor("dbg_stat", [128, 8], F32, kind="ExternalOutput").ap()
            dbg["x2"] = nc.dram_tensor("dbg_x2", [128, DM], F32, kind="ExternalOutput").ap()

    BASE = 16384 + 1024
    A, Bb, C, D, E, F = (BASE, BASE + 65536, BASE + 131072, BASE + 167936,
                         BASE + 184320, BASE + 200704)

    def sb(name, shape, dtype, off):
        return nc.alloc_sbuf_tensor_at(name, shape, dtype, offset=off)

    xn_sb = sb("xn_sb", [128, 32 * TPC], BF16, A)
    K1_sb = sb("K1_sb", [128, 9216], BF16, A)
    V1_sb = sb("V1_sb", [128, 9216], BF16, A + 18432)
    l2_sb = sb("l2_sb", [128, TPC], F32, A + 36864)
    rl_sb = sb("rl_sb", [128, TPC], F32, A + 40960)

    ctx_sb = sb("ctx_sb", [128, 32 * TPC], BF16, Bb)
    cosq_sb = sb("cosq_sb", [128, TPC], F32, Bb)
    sinq_sb = sb("sinq_sb", [128, TPC], F32, Bb + 4096)
    cosk_sb = sb("cosk_sb", [128, TPC], F32, Bb + 8192)
    sink_sb = sb("sink_sb", [128, TPC], F32, Bb + 12288)
    kp_sb = sb("kp_sb", [128, 2 * 512], BF16, Bb + 16384)
    t1_sb = sb("t1_sb", [128, 512], F32, Bb + 18432)
    t2_sb = sb("t2_sb", [128, 512], F32, Bb + 20480)
    kp2_sb = sb("kp2_sb", [128, 2 * 512], BF16, Bb + 22528)
    bias_sb = sb("bias_sb", [128, DM], F32, Bb)
    t1f_sb = sb("t1f_sb", [128, 2 * 512], F32, Bb + 16384)
    gt_sb = sb("gt_sb", [128, 2 * 512], BF16, Bb + 20480)
    stile_sb = sb("stile_sb", [128, 2 * 512], BF16, Bb + 22528)
    xtile_sb = sb("xtile_sb", [128, 2 * 512], F32, Bb + 24576)
    otile_sb = sb("otile_sb", [128, 2 * 512], F32, Bb + 28672)

    xt_sb = sb("xt_sb", [128, 2 * DM], F32, C)
    vslab_sb = sb("vslab_sb", [128, 16384], BF16, C)
    K0_sb = sb("K0_sb", [128, 9216], BF16, C)
    V0_sb = sb("V0_sb", [128, 9216], BF16, C + 18432)
    slab0_sb = sb("slab0_sb", [128, 16384], BF16, C)

    x2_sb = sb("x2_sb", [128, DM], F32, D)
    qslabD_sb = sb("qslabD_sb", [128, 8192], BF16, D)
    Q0_sb = sb("Q0_sb", [128, 4096], BF16, D)
    Q1_sb = sb("Q1_sb", [128, 4096], BF16, D + 8192)
    slab1_sb = sb("slab1_sb", [128, 16384], BF16, D)

    xnf_sb = sb("xnf_sb", [128, DM], F32, E)
    wslab_sb = sb("wslab_sb", [128, 2 * 4096], BF16, E)

    cst_sb = sb("cst_sb", [128, 8], F32, F)
    ctrlb_sb = sb("ctrlb_sb", [128, 4 * NCTRLB], F32, F + 32)
    rot_sb = sb("rot_sb", [128, 128], BF16, F + 288)
    idm_sb = sb("idm_sb", [128, 128], F32, F + 544)
    ones_sb = sb("ones_sb", [128, 128], BF16, F + 1056)
    onesr_sb = sb("onesr_sb", [1, 128], F32, F + 1312)
    swb_sb = sb("swb_sb", [128, 1], F32, F + 1824)
    pt_sb = sb("pt_sb", [128, 2 * 1024], BF16, F + 1856)
    sbf_sb = sb("sbf_sb", [128, 2 * 512], BF16, F + 5952)
    stat_sb = sb("stat_sb", [128, 8], F32, F + 8000)
    dmask_sb = sb("dmask_sb", [128, 4 * 1024], BF16, A + 49152)

    St0 = nc.alloc_psum_tensor("St0", [128, 1024], F32)
    St1 = nc.alloc_psum_tensor("St1", [128, 1024], F32)
    ctxA = nc.alloc_psum_tensor("ctxA", [128, 512], F32)
    ctxB = nc.alloc_psum_tensor("ctxB", [128, 512], F32)
    lA = nc.alloc_psum_tensor("lA", [128, 512], F32)
    lB = nc.alloc_psum_tensor("lB", [128, 512], F32)
    StP = [St0, St1]
    KVs = [(K0_sb, V0_sb, Q0_sb), (K1_sb, V1_sb, Q1_sb)]

    cAP = {"shift": cst_sb[:, 0:1], "one": cst_sb[:, 1:2], "mhalf": cst_sb[:, 2:3],
           "eps": cst_sb[:, 3:4], "rdm": cst_sb[:, 4:5], "zero": cst_sb[:, 5:6],
           "mone": cst_sb[:, 6:7]}

    ENGS = ("tensor", "scalar", "vector", "gpsimd", "sync")
    ops = {e: [] for e in ENGS}
    stack = ExitStack()
    sems = {}
    cnt = {}
    for name in ("pre", "da", "db", "dc", "ga", "gb", "gc", "gd", "ge", "gf",
                 "gg", "gh", "sa", "sb", "sc", "sd",
                 "cc", "pe", "act", "dve", "gp", "bar"):
        sems[name] = stack.enter_context(nc.semaphore(name))
        cnt[name] = 0

    AUTO = {"scalar": "act", "vector": "dve", "gpsimd": "gp"}

    def emit(eng, fn, dma=False):
        if eng in AUTO and not dma:
            nm = AUTO[eng]
            idx = cnt[nm]
            sem = sems[nm]

            def wrapped(e, fn=fn, idx=idx, sem=sem):
                if idx > 0:
                    e.wait_ge(sem, idx)
                fn(e).then_inc(sem, 1)
            ops[eng].append(wrapped)
            cnt[nm] += 1
            return cnt[nm]
        ops[eng].append(fn)
        return None

    def winc(name, n=1):
        cnt[name] += n
        return cnt[name]

    def wait(eng, name, val):
        if val and val > 0:
            emit(eng, lambda e, s=sems[name], v=val: e.wait_ge(s, v), dma=True)

    def barrier(engines=ENGS):
        for nm in ("pre", "da", "db", "dc", "sa", "sb", "sc", "sd"):
            wait("sync", nm, cnt[nm])
        for nm in ("ga", "gb"):
            wait("gpsimd", nm, cnt[nm])
        wait("scalar", "act", cnt["act"])
        wait("vector", "dve", cnt["dve"])
        wait("gpsimd", "gp", cnt["gp"])
        wait("tensor", "pe", cnt["pe"])
        prev = cnt["bar"]
        for en in engines:
            emit(en, lambda e, s=sems["bar"], p=prev: e.wait_ge(s, p), dma=True)
            emit(en, lambda e, s=sems["bar"]: e.sem_inc(s, 1), dma=True)
        tgt = winc("bar", len(engines))
        for en in engines:
            wait(en, "bar", tgt)

    for dst, src in ((cst_sb, cst_in), (ctrlb_sb, ctrlb_in),
                     (rot_sb, rot_in), (idm_sb, idm_in), (swb_sb, sw_bc),
                     (cosq_sb, cosq), (sinq_sb, sinq), (cosk_sb, cosk), (sink_sb, sink)):
        emit("sync", lambda e, d=dst, s=src: e.dma_start(out=d[:, :], in_=s[:, :])
             .then_inc(sems["pre"], 16))
        winc("pre", 16)
    emit("gpsimd", lambda e: e.memset(ones_sb[:, :], 1.0))
    g_ones = emit("gpsimd", lambda e: e.memset(onesr_sb[:, :], 1.0))
    n_pre = cnt["pre"]
    for en in ENGS:
        wait(en, "pre", n_pre)
        wait(en, "gp", g_ones)


    if PHASE_LIM >= 1:
        d_xnf = [None] * 8
        d_evac = []
        pe_tr = {}

        def emit_stats(tb):
            par = tb % 2
            xt = xt_sb[:, par * DM:(par + 1) * DM]
            lane = "da" if par == 0 else "db"
            xl = xload[tb]
            wait("vector", lane, xl)
            wait("scalar", lane, xl)
            emit("vector", lambda e, xt=xt: e.reduce_sum(stat_sb[:, 0:1], xt, axis=AX))
            a_sq = emit("scalar", lambda e, xt=xt: e.square(x2_sb[:, :], xt))
            yield
            wait("vector", "act", a_sq)
            emit("vector", lambda e: e.reduce_sum(stat_sb[:, 1:2], x2_sb[:, :], axis=AX))
            yield
            emit("vector", lambda e: e.tensor_scalar_mul(stat_sb[:, 2:3], stat_sb[:, 0:1], cAP["rdm"]))
            emit("vector", lambda e: e.tensor_scalar_mul(stat_sb[:, 3:4], stat_sb[:, 1:2], cAP["rdm"]))
            emit("vector", lambda e: e.tensor_mul(stat_sb[:, 4:5], stat_sb[:, 2:3], stat_sb[:, 2:3]))
            d_var = emit("vector", lambda e: e.tensor_sub(stat_sb[:, 5:6], stat_sb[:, 3:4], stat_sb[:, 4:5]))
            wait("scalar", "dve", d_var)
            emit("scalar", lambda e: e.activation(stat_sb[:, 6:7], stat_sb[:, 5:6], AF.Ln,
                                                  bias=cAP["eps"], scale=cAP["one"]))
            a_rstd = emit("scalar", lambda e: e.activation(stat_sb[:, 7:8], stat_sb[:, 6:7], AF.Exp,
                                                           bias=cAP["zero"], scale=cAP["mhalf"]))
            yield
            wait("vector", "act", a_rstd)
            if tb >= 1:
                wait("vector", "pe", pe_tr[(tb - 1, 7)])
            d_xnf[tb] = emit("vector", lambda e, xt=xt: e.tensor_scalar(
                xnf_sb[:, :], xt, stat_sb[:, 2:3], stat_sb[:, 7:8], ALU.subtract, ALU.mult))

        def emit_tr_batch(tb, b4):
            Sp = StP[b4 % 2]
            if b4 == 0:
                wait("tensor", "dve", d_xnf[tb])
            gi = tb * 8 + b4
            if gi >= 2:
                wait("tensor", "dve", d_evac[gi - 2])
            for j in range(4):
                kc = b4 * 4 + j
                last = (j == 3)
                emit("tensor", lambda e, kc=kc, Sp=Sp, j=j, last=last: (
                    e.transpose(Sp[:, j * 128:(j + 1) * 128],
                                xnf_sb[:, kc * 128:(kc + 1) * 128], idm_sb[:, :])
                    .then_inc(sems["pe"], 1) if last else
                    e.transpose(Sp[:, j * 128:(j + 1) * 128],
                                xnf_sb[:, kc * 128:(kc + 1) * 128], idm_sb[:, :])))
            pe_tr[(tb, b4)] = winc("pe")
            wait("vector", "pe", pe_tr[(tb, b4)])
            d_evac.append(emit("vector", lambda e, tb=tb, b4=b4, Sp=Sp: e.tensor_copy(
                xn_sb.ap().rearrange("p (c n) -> p c n", n=TPC)
                     [:, 4 * b4:4 * b4 + 4, tb * 128:(tb + 1) * 128],
                Sp[:, 0:512].rearrange("p (c n) -> p c n", n=128))))

        xload = [None] * 8

        def load_x(tb):
            par = tb % 2
            lane = "da" if par == 0 else "db"
            if tb >= 2:
                wait("sync", "dve", d_xnf[tb - 2])
            emit("sync", lambda e, tb=tb, par=par, lane=lane: e.dma_start(
                out=xt_sb[:, par * DM:(par + 1) * DM],
                in_=x_in[tb * 128:(tb + 1) * 128, :]).then_inc(sems[lane], 16))
            xload[tb] = winc(lane, 16)

        load_x(0)
        load_x(1)
        gen = emit_stats(0)
        for _ in gen:
            pass
        for tb in range(8):
            if tb + 2 < 8:
                load_x(tb + 2)
            if tb + 1 < 8:
                gen = emit_stats(tb + 1)
            else:
                gen = iter(())
            emit_tr_batch(tb, 0)
            next(gen, None)
            emit_tr_batch(tb, 1)
            emit_tr_batch(tb, 2)
            next(gen, None)
            emit_tr_batch(tb, 3)
            emit_tr_batch(tb, 4)
            next(gen, None)
            emit_tr_batch(tb, 5)
            emit_tr_batch(tb, 6)
            emit_tr_batch(tb, 7)
            for _ in gen:
                pass
        barrier()
        if DBG:
            emit("sync", lambda e: e.dma_start(out=dbg["xn"][:, :], in_=xn_sb[:, :])
                 .then_inc(sems["pre"], 16))
            winc("pre", 16)
            if "xnf" in dbg and PHASE_LIM == 1:
                emit("sync", lambda e: e.dma_start(out=dbg["xnf"][:, :], in_=xnf_sb[:, :])
                     .then_inc(sems["pre"], 16))
                winc("pre", 16)
                emit("sync", lambda e: e.dma_start(out=dbg["stat"][:, :], in_=stat_sb[:, :])
                     .then_inc(sems["pre"], 16))
                winc("pre", 16)
                emit("sync", lambda e: e.dma_start(out=dbg["x2"][:, :], in_=x2_sb[:, :])
                     .then_inc(sems["pre"], 16))
                winc("pre", 16)

    kvload = [None] * 8
    av_last_kv = [None] * 8
    kv_issued = set()
    prefetch = {}

    if PHASE_LIM >= 2:
        gld_k = [None] * 8
        pe_mm = {}
        pe_rot = {}
        a_evac = {}
        d_t1 = {}
        d_t2 = {}
        d_kp2 = {}
        st_k = {}
        pe_last_of_dk = {}
        for dk in range(8):
            par = dk % 2
            if dk >= 2:
                wait("gpsimd", "pe", pe_last_of_dk[dk - 2])
            glane = "ga" if par == 0 else "gb"
            emit("gpsimd", lambda e, dk=dk, par=par, glane=glane: e.dma_start(
                out=wslab_sb[:, par * 4096:(par + 1) * 4096].rearrange("p (c n) -> p c n", n=128),
                in_=wk.rearrange("(c p) n -> p c n", p=128)[:, :, dk * 128:(dk + 1) * 128])
                .then_inc(sems[glane], 16), dma=True)
            gld_k[dk] = (glane, winc(glane, 16))
            for tb2 in range(2):
                it = dk * 2 + tb2
                half = it % 2
                wait("tensor", *gld_k[dk])
                if it >= 2:
                    wait("tensor", "act", a_evac[it - 2])
                for kc in range(32):
                    emit("tensor", lambda e, kc=kc, half=half, par=par, tb2=tb2: (
                        lambda i: i.then_inc(sems["pe"], 1) if kc == 31 else i)(
                        e.matmul(
                            St0[:, half * 512:(half + 1) * 512],
                            wslab_sb[:, par * 4096 + kc * 128:par * 4096 + (kc + 1) * 128],
                            xn_sb[:, kc * TPC + tb2 * 512:kc * TPC + (tb2 + 1) * 512],
                            start=(kc == 0), stop=(kc == 31))))
                pe_mm[it] = winc("pe")
                wait("scalar", "pe", pe_mm[it])
                if it >= 2:
                    wait("scalar", "pe", pe_rot[it - 2])
                    wait("scalar", "dve", d_t1[it - 2])
                a_evac[it] = emit("scalar", lambda e, half=half: e.copy(
                    kp_sb[:, half * 512:(half + 1) * 512], St0[:, half * 512:(half + 1) * 512]))
                wait("tensor", "act", a_evac[it])
                if it >= 2:
                    wait("tensor", "dve", d_t2[it - 2])
                emit("tensor", lambda e, half=half: e.matmul(
                    St1[:, half * 512:(half + 1) * 512], rot_sb[:, :],
                    kp_sb[:, half * 512:(half + 1) * 512], start=True, stop=True)
                    .then_inc(sems["pe"], 1))
                pe_rot[it] = winc("pe")
                pe_last_of_dk[dk] = pe_rot[it]
                wait("vector", "act", a_evac[it])
                d_t1[it] = emit("vector", lambda e, half=half, tb2=tb2: e.tensor_mul(
                    t1_sb[:, :], kp_sb[:, half * 512:(half + 1) * 512],
                    cosk_sb[:, tb2 * 512:(tb2 + 1) * 512]))
                wait("vector", "pe", pe_rot[it])
                d_t2[it] = emit("vector", lambda e, half=half, tb2=tb2: e.tensor_mul(
                    t2_sb[:, :], St1[:, half * 512:(half + 1) * 512],
                    sink_sb[:, tb2 * 512:(tb2 + 1) * 512]))
                slane = "sa" if half == 0 else "sb"
                if it >= 2:
                    wait("vector", *st_k[it - 2])
                d_kp2[it] = emit("vector", lambda e, half=half: e.tensor_add(
                    kp2_sb[:, half * 512:(half + 1) * 512], t1_sb[:, :], t2_sb[:, :]))
                wait("sync", "dve", d_kp2[it])
                emit("sync", lambda e, dk=dk, tb2=tb2, half=half, slane=slane: e.dma_start(
                    out=cc_in.ap()[dk * 128:(dk + 1) * 128, tb2 * 512:(tb2 + 1) * 512],
                    in_=kp2_sb[:, half * 512:(half + 1) * 512]).then_inc(sems[slane], 16))
                st_k[it] = (slane, winc(slane, 16))

        gld_v = [None] * 2
        pe_vmm = {}
        a_vevac = {}
        st_v = {}
        prefetch["ccK_gate"] = (cnt["sa"], cnt["sb"])
        for ns in range(2):
            if ns >= 1:
                wait("gpsimd", "pe", pe_vmm[7])
            else:
                wait("gpsimd", "dve", d_xnf[7])
            emit("gpsimd", lambda e, ns=ns: e.dma_start(
                out=vslab_sb.ap().rearrange("p (c n) -> p c n", n=512),
                in_=wv.rearrange("(c p) n -> p c n", p=128)[:, :, ns * 512:(ns + 1) * 512])
                .then_inc(sems["gc"], 16), dma=True)
            gld_v[ns] = winc("gc", 16)
            if ns == 1:
                wait("gpsimd", "sa", prefetch["ccK_gate"][0])
                wait("gpsimd", "sb", prefetch["ccK_gate"][1])
                emit("gpsimd", lambda e: e.collective_compute(
                    "AllGather", mybir.AluOpType.bypass,
                    replica_groups=[list(range(NCORES))],
                    ins=[cc_in.ap()[0:TPC, :]], outs=[cc_outK.ap()]).then_inc(sems["cc"]),
                    dma=True)
            for tb in range(8):
                itv = ns * 8 + tb
                half = itv % 2
                wait("tensor", "gc", gld_v[ns])
                if itv >= 2:
                    wait("tensor", "act", a_vevac[itv - 2])
                else:
                    wait("tensor", "act", a_evac[14 + itv])
                for kc in range(32):
                    emit("tensor", lambda e, kc=kc, half=half, tb=tb: (
                        lambda i: i.then_inc(sems["pe"], 1) if kc == 31 else i)(
                        e.matmul(
                            St0[:, half * 512:(half + 1) * 512],
                            xn_sb[:, kc * TPC + tb * 128:kc * TPC + tb * 128 + 128],
                            vslab_sb[:, kc * 512:(kc + 1) * 512],
                            start=(kc == 0), stop=(kc == 31))))
                pe_vmm[itv] = winc("pe")
                slane = "sa" if half == 0 else "sb"
                wait("scalar", "pe", pe_vmm[itv])
                wait("scalar", *(st_v[itv - 2] if itv >= 2 else st_k[14 + half]))
                a_vevac[itv] = emit("scalar", lambda e, half=half: e.copy(
                    kp_sb[:, half * 512:(half + 1) * 512], St0[:, half * 512:(half + 1) * 512]))
                wait("sync", "act", a_vevac[itv])
                emit("sync", lambda e, tb=tb, ns=ns, half=half, slane=slane: e.dma_start(
                    out=cc_in.ap()[1024 + tb * 128:1024 + (tb + 1) * 128, ns * 512:(ns + 1) * 512],
                    in_=kp_sb[:, half * 512:(half + 1) * 512]).then_inc(sems[slane], 16))
                st_v[itv] = (slane, winc(slane, 16))
        prefetch["ccV_gate"] = (cnt["sa"], cnt["sb"])

    def issue_kv_load(kvh):
        par = kvh % 2
        Ksb, Vsb, Qsb = KVs[par]
        lane = "da" if par == 0 else "db"
        if kvh < 2:
            wait("sync", "pre", cnt["pre"])
            wait("sync", "dc", cnt["dc"])
            wait("sync", "cc", 2)
            wait("sync", "pe", pe_last_of_qh[7])
            wait("sync", *st_q[2 * (4 * (kvh + 1)) - 1])
        else:
            wait("sync", "pe", av_last_kv[kvh - 2])
        for r in range(NCORES):
            emit("sync", lambda e, r=r, kvh=kvh, Ksb=Ksb, lane=lane: e.dma_start(
                out=Ksb[:, r * 1024:(r + 1) * 1024],
                in_=cc_outK.ap()[(1024 * r + kvh * 128):(1024 * r + (kvh + 1) * 128), :])
                .then_inc(sems[lane], 16))
            winc(lane, 16)
        emit("sync", lambda e, kvh=kvh, Ksb=Ksb, lane=lane: e.dma_start(
            out=Ksb[:, 8192:9216],
            in_=cc_in.ap()[kvh * 128:(kvh + 1) * 128, :]).then_inc(sems[lane], 16))
        winc(lane, 16)
        for r in range(NCORES):
            emit("sync", lambda e, r=r, kvh=kvh, Vsb=Vsb, lane=lane: e.dma_start(
                out=Vsb[:, r * 1024:(r + 1) * 1024].rearrange("p (c n) -> p c n", n=128),
                in_=cc_outV.ap().rearrange("(a p) n -> p a n", p=128)
                    [:, 8 * r:8 * r + 8, kvh * 128:(kvh + 1) * 128])
                .then_inc(sems[lane], 16))
            winc(lane, 16)
        emit("sync", lambda e, kvh=kvh, Vsb=Vsb, lane=lane: e.dma_start(
            out=Vsb[:, 8192:9216].rearrange("p (c n) -> p c n", n=128),
            in_=cc_in.ap().rearrange("(a p) n -> p a n", p=128)
                [:, 8:16, kvh * 128:(kvh + 1) * 128]).then_inc(sems[lane], 16))
        winc(lane, 16)
        emit("sync", lambda e, kvh=kvh, Qsb=Qsb, lane=lane: e.dma_start(
            out=Qsb.ap().rearrange("p (c n) -> p c n", n=TPC),
            in_=q_scr.ap().rearrange("(c p) n -> p c n", p=128)[:, 4 * kvh:4 * kvh + 4, :])
            .then_inc(sems[lane], 16))
        kvload[kvh] = ("da" if par == 0 else "db", winc(lane, 16))
        kv_issued.add(kvh)

    if PHASE_LIM >= 3:
        emit("sync", lambda e: e.dma_start(out=xn_scr.ap()[:, :], in_=xn_sb[:, :])
             .then_inc(sems["dc"], 16))
        winc("dc", 16)
        gld_q = [None] * 32
        pe_qmm = {}
        pe_qrot = {}
        a_qevac = {}
        dq_t1 = {}
        dq_t2 = {}
        dq_kp2 = {}
        st_q = {}
        pe_last_of_qh = {}
        QLANES = ["ga", "gb", "gc", "gd", "ge", "gf", "gg", "gh"]

        def qslab_ap(slot):
            if slot < 2:
                return wslab_sb[:, slot * 4096:(slot + 1) * 4096]
            if slot < 4:
                return qslabD_sb[:, (slot - 2) * 4096:(slot - 1) * 4096]
            return vslab_sb[:, (slot - 4) * 4096:(slot - 3) * 4096]

        for qh in range(32):
            slot = qh % 8
            glane = QLANES[slot]
            if qh >= 8:
                wait("gpsimd", "pe", pe_last_of_qh[qh - 8])
            elif slot < 2:
                wait("gpsimd", "pe", pe_last_of_dk[6 + slot])
            elif slot >= 4 or slot in (2, 3):
                wait("gpsimd", "pe", pe_vmm[15])
            emit("gpsimd", lambda e, qh=qh, slot=slot, glane=glane: e.dma_start(
                out=qslab_ap(slot).rearrange("p (c n) -> p c n", n=128),
                in_=wq.rearrange("(c p) n -> p c n", p=128)[:, :, qh * 128:(qh + 1) * 128])
                .then_inc(sems[glane], 16), dma=True)
            gld_q[qh] = (glane, winc(glane, 16))
            if qh == 10:
                wait("gpsimd", "sa", prefetch["ccV_gate"][0])
                wait("gpsimd", "sb", prefetch["ccV_gate"][1])
                emit("gpsimd", lambda e: e.collective_compute(
                    "AllGather", mybir.AluOpType.bypass,
                    replica_groups=[list(range(NCORES))],
                    ins=[cc_in.ap()[TPC:2 * TPC, :]], outs=[cc_outV.ap()]).then_inc(sems["cc"]),
                    dma=True)
                cnt["cc"] = 2
            for tb2 in range(2):
                it = qh * 2 + tb2
                half = it % 2
                wait("tensor", *gld_q[qh])
                if it >= 2:
                    wait("tensor", "act", a_qevac[it - 2])
                else:
                    wait("tensor", "act", a_vevac[14 + it])
                slot = qh % 8
                for kc in range(32):
                    emit("tensor", lambda e, kc=kc, half=half, slot=slot, tb2=tb2: (
                        lambda i: i.then_inc(sems["pe"], 1) if kc == 31 else i)(
                        e.matmul(
                            St0[:, half * 512:(half + 1) * 512],
                            qslab_ap(slot)[:, kc * 128:(kc + 1) * 128],
                            xn_sb[:, kc * TPC + tb2 * 512:kc * TPC + (tb2 + 1) * 512],
                            start=(kc == 0), stop=(kc == 31))))
                pe_qmm[it] = winc("pe")
                wait("scalar", "pe", pe_qmm[it])
                if it >= 2:
                    wait("scalar", "pe", pe_qrot[it - 2])
                    wait("scalar", "dve", dq_t1[it - 2])
                else:
                    wait("scalar", *st_v[14 + it])
                a_qevac[it] = emit("scalar", lambda e, half=half: e.copy(
                    kp_sb[:, half * 512:(half + 1) * 512], St0[:, half * 512:(half + 1) * 512]))
                wait("tensor", "act", a_qevac[it])
                if it >= 2:
                    wait("tensor", "dve", dq_t2[it - 2])
                else:
                    wait("tensor", "dve", d_t2[14 + it])
                emit("tensor", lambda e, half=half: e.matmul(
                    St1[:, half * 512:(half + 1) * 512], rot_sb[:, :],
                    kp_sb[:, half * 512:(half + 1) * 512], start=True, stop=True)
                    .then_inc(sems["pe"], 1))
                pe_qrot[it] = winc("pe")
                pe_last_of_qh[qh] = pe_qrot[it]
                wait("vector", "act", a_qevac[it])
                dq_t1[it] = emit("vector", lambda e, half=half, tb2=tb2: e.tensor_mul(
                    t1_sb[:, :], kp_sb[:, half * 512:(half + 1) * 512],
                    cosq_sb[:, tb2 * 512:(tb2 + 1) * 512]))
                wait("vector", "pe", pe_qrot[it])
                dq_t2[it] = emit("vector", lambda e, half=half, tb2=tb2: e.tensor_mul(
                    t2_sb[:, :], St1[:, half * 512:(half + 1) * 512],
                    sinq_sb[:, tb2 * 512:(tb2 + 1) * 512]))
                slane = "sc" if half == 0 else "sd"
                if it >= 2:
                    wait("vector", *st_q[it - 2])
                else:
                    wait("vector", *st_k[14 + it])
                dq_kp2[it] = emit("vector", lambda e, half=half: e.tensor_add(
                    kp2_sb[:, half * 512:(half + 1) * 512], t1_sb[:, :], t2_sb[:, :]))
                wait("sync", "dve", dq_kp2[it])
                emit("sync", lambda e, qh=qh, tb2=tb2, half=half, slane=slane: e.dma_start(
                    out=q_scr.ap()[qh * 128:(qh + 1) * 128, tb2 * 512:(tb2 + 1) * 512],
                    in_=kp2_sb[:, half * 512:(half + 1) * 512]).then_inc(sems[slane], 16))
                st_q[it] = (slane, winc(slane, 16))
        if PHASE_LIM >= 4:
            issue_kv_load(0)
            issue_kv_load(1)
            emit("sync", lambda e: e.dma_start(out=dmask_sb[:, :], in_=dmask_in[:, :])
                 .then_inc(sems["dc"], 16))
            prefetch["dmask"] = winc("dc", 16)
        barrier()
        if DBG:
            emit("sync", lambda e: e.dma_start(out=dbg["q"][:, :], in_=q_scr.ap()[:, :])
                 .then_inc(sems["pre"], 16))
            winc("pre", 16)

    if PHASE_LIM >= 4:
        batches = []
        for g in range(4):
            for bi in range(SG[g] // 4):
                batches.append((g, bi, [(g, 4 * bi + jj) for jj in range(4)]))
        assert len(batches) == 40

        sched = []
        for kvh in range(8):
            for qhl in range(4):
                for bid, (g, bi, slots) in enumerate(batches):
                    sched.append((kvh, qhl, bid, g, bi, slots))
        NB = len(sched)

        st_done = {}
        exp_done = {}
        gate_av = {}
        av_done = {}
        evacA_done = {}
        evacB_done = {}

        def emit_st(i):
            kvh, qhl, bid, g, bi, slots = sched[i]
            if kvh not in kv_issued:
                issue_kv_load(kvh)
            if bid == 0 and qhl == 0:
                wait("tensor", *kvload[kvh])
            par = kvh % 2
            Ksb, Vsb, Qsb = KVs[par]
            Sp = StP[i % 2]
            if i >= 2:
                wait("tensor", "act", exp_done[i - 2])
            for idx, (g_, j) in enumerate(slots):
                koff = slot_cols(g_, j)
                emit("tensor", lambda e, idx=idx, koff=koff, g_=g_, qhl=qhl, Sp=Sp,
                     Ksb=Ksb, Qsb=Qsb, last=(idx == 3): (
                         lambda x: x.then_inc(sems["pe"], 1) if last else x)(
                         e.matmul(
                             Sp[:, idx * 256:(idx + 1) * 256],
                             Ksb[:, koff:koff + 128],
                             Qsb[:, qhl * TPC + g_ * 256:qhl * TPC + (g_ + 1) * 256],
                             start=True, stop=True)))
            st_done[i] = winc("pe")

        wait("vector", "dc", prefetch["dmask"])
        emit_st(0)
        emit_st(1)
        for i in range(NB):
            kvh, qhl, bid, g, bi, slots = sched[i]
            qh = 4 * kvh + qhl
            par = kvh % 2
            Ksb, Vsb, Qsb = KVs[par]
            pb = i % 2
            Sp = StP[pb]
            wait("scalar", "pe", st_done[i])
            if i >= 2:
                wait("scalar", "pe", av_done[i - 2])
            exp_done[i] = emit("scalar", lambda e, pb=pb, Sp=Sp: e.activation(
                pt_sb[:, pb * 1024:(pb + 1) * 1024], Sp[:, :], AF.Exp,
                bias=cAP["shift"], scale=cAP["one"]))
            q0 = pb * 1024
            if bi == 0:
                wait("vector", "act", exp_done[i])
                ev = emit("vector", lambda e, q0=q0, g=g: e.tensor_mul(
                    pt_sb[:, q0:q0 + 1024], pt_sb[:, q0:q0 + 1024],
                    dmask_sb[:, g * 1024:(g + 1) * 1024]))
                gate_av[i] = [("dve", ev)]
            elif (g, bi) in CTRLB_IX:
                cb = CTRLB_IX[(g, bi)]
                base = ctrlb_sb.ap()
                in1 = bass.AP(tensor=base.tensor, offset=base.offset + 4 * cb,
                              ap=[[base.ap[0][0], 128], [1, 4], [0, 256]])
                wait("vector", "act", exp_done[i])
                ev = emit("vector", lambda e, q0=q0, in1=in1: e.tensor_tensor(
                    pt_sb.ap().rearrange("p (c n) -> p c n", n=256)[:, q0 // 256:q0 // 256 + 4, :],
                    pt_sb.ap().rearrange("p (c n) -> p c n", n=256)[:, q0 // 256:q0 // 256 + 4, :],
                    in1, ALU.mult))
                gate_av[i] = [("dve", ev)]
            else:
                gate_av[i] = [("act", exp_done[i])]
            if i + 2 < NB:
                emit_st(i + 2)
            for gt in gate_av[i]:
                wait("tensor", *gt)
            if bid == 0 and qh > 0:
                wait("tensor", "dve", evacA_done[qh - 1])
            if bid == 12 and qh > 0:
                wait("tensor", "dve", evacB_done[qh - 1])
            ctxp = ctxA if g < 2 else ctxB
            c0 = (g % 2) * 256
            lp = lA if g < 2 else lB
            l0 = (g % 2) * 256
            first = (bi == 0)
            last = (bi == SG[g] // 4 - 1)
            for idx, (g_, j) in enumerate(slots):
                voff = slot_cols(g_, j)
                emit("tensor", lambda e, idx=idx, voff=voff, ctxp=ctxp, pb=pb, c0=c0,
                     Vsb=Vsb, fl=(j == 0), ll=(j == SG[g] - 1): e.matmul(
                         ctxp[:, c0:c0 + 256], Vsb[:, voff:voff + 128],
                         pt_sb[:, pb * 1024 + idx * 256:pb * 1024 + (idx + 1) * 256],
                         start=fl, stop=ll))
            for idx in range(4):
                emit("tensor", lambda e, idx=idx, lp=lp, pb=pb, l0=l0,
                     fl=(first and idx == 0), ll=(last and idx == 3),
                     lastb=(idx == 3): (
                         lambda x: x.then_inc(sems["pe"], 1) if lastb else x)(
                         e.matmul(
                             lp[:, l0:l0 + 256], ones_sb[:, :],
                             pt_sb[:, pb * 1024 + idx * 256:pb * 1024 + (idx + 1) * 256],
                             start=fl, stop=ll)))
            av_done[i] = winc("pe")
            if bid == 39:
                av_last_kv[kvh] = av_done[i]
            if bid == 11:
                wait("scalar", "pe", av_done[i])
                if qh > 0:
                    wait("scalar", "dve", evacA_done[qh - 1])
                emit("scalar", lambda e: e.activation(
                    l2_sb[:, 0:512], lA[:, :], AF.Ln, bias=cAP["zero"], scale=cAP["one"]))
                a_rl = emit("scalar", lambda e: e.activation(
                    rl_sb[:, 0:512], l2_sb[:, 0:512], AF.Exp, bias=cAP["zero"], scale=cAP["mone"]))
                wait("vector", "act", a_rl)
                evacA_done[qh] = emit("vector", lambda e, qh=qh: e.tensor_mul(
                    ctx_sb[:, qh * TPC:qh * TPC + 512], ctxA[:, :], rl_sb[:, 0:512]))
            if bid == 39:
                wait("scalar", "pe", av_done[i])
                if qh > 0:
                    wait("scalar", "dve", evacB_done[qh - 1])
                emit("scalar", lambda e: e.activation(
                    l2_sb[:, 512:1024], lB[:, :], AF.Ln, bias=cAP["zero"], scale=cAP["one"]))
                b_rl = emit("scalar", lambda e: e.activation(
                    rl_sb[:, 512:1024], l2_sb[:, 512:1024], AF.Exp, bias=cAP["zero"], scale=cAP["mone"]))
                wait("vector", "act", b_rl)
                evacB_done[qh] = emit("vector", lambda e, qh=qh: e.tensor_mul(
                    ctx_sb[:, qh * TPC + 512:(qh + 1) * TPC], ctxB[:, :], rl_sb[:, 512:1024]))
        if PHASE_LIM >= 5:
            wait("gpsimd", "pe", av_last_kv[6])
            emit("gpsimd", lambda e: e.dma_start(
                out=slab0_sb.ap().rearrange("p (c n) -> p c n", n=512),
                in_=wo.rearrange("(c p) n -> p c n", p=128)[:, :, 0:512])
                .then_inc(sems["ga"], 16), dma=True)
            prefetch["wo0"] = ("ga", winc("ga", 16))
        barrier()
        if DBG:
            emit("sync", lambda e: e.dma_start(out=dbg["ctx"][:, :], in_=ctx_sb[:, :])
                 .then_inc(sems["pre"], 16))
            winc("pre", 16)
            emit("sync", lambda e: e.dma_start(out=dbg["cc"][:, :], in_=cc_out.ap()[:, :])
                 .then_inc(sems["pre"], 16))
            winc("pre", 16)

    if PHASE_LIM >= 5:
        emit("sync", lambda e: e.dma_start(out=xn_sb[:, :], in_=xn_scr.ap()[:, :])
             .then_inc(sems["dc"], 16))
        xn_reload = winc("dc", 16)
        slabs = [slab0_sb, slab1_sb]
        gld_wo = [None] * 8
        gld_wo[0] = prefetch["wo0"]
        emit("gpsimd", lambda e: e.dma_start(
            out=slab1_sb.ap().rearrange("p (c n) -> p c n", n=512),
            in_=wo.rearrange("(c p) n -> p c n", p=128)[:, :, 512:1024])
            .then_inc(sems["gb"], 16), dma=True)
        gld_wo[1] = ("gb", winc("gb", 16))
        wo_tile = {}
        s_evac = {}
        s_store = {}
        for ns in range(8):
            parn = ns % 2
            glane = "ga" if parn == 0 else "gb"
            if ns >= 2:
                wait("gpsimd", "pe", wo_tile[(ns - 2) * 8 + 7])
                emit("gpsimd", lambda e, ns=ns, parn=parn, glane=glane: e.dma_start(
                    out=slabs[parn].ap().rearrange("p (c n) -> p c n", n=512),
                    in_=wo.rearrange("(c p) n -> p c n", p=128)[:, :, ns * 512:(ns + 1) * 512])
                    .then_inc(sems[glane], 16), dma=True)
                gld_wo[ns] = (glane, winc(glane, 16))
            for tb in range(8):
                it = ns * 8 + tb
                half = it % 2
                if tb == 0:
                    wait("tensor", *gld_wo[ns])
                if it >= 2:
                    wait("tensor", "act", s_evac[it - 2])
                for cd in range(32):
                    emit("tensor", lambda e, cd=cd, half=half, tb=tb, parn=parn: (
                        lambda i: i.then_inc(sems["pe"], 1) if cd == 31 else i)(
                        e.matmul(
                            St0[:, half * 512:(half + 1) * 512],
                            ctx_sb[:, cd * TPC + tb * 128:cd * TPC + tb * 128 + 128],
                            slabs[parn][:, cd * 512:(cd + 1) * 512],
                            start=(cd == 0), stop=(cd == 31))))
                wo_tile[it] = winc("pe")
                slane = "sa" if half == 0 else "sb"
                wait("scalar", "pe", wo_tile[it])
                if it >= 2:
                    wait("scalar", *s_store[it - 2])
                s_evac[it] = emit("scalar", lambda e, half=half: e.copy(
                    sbf_sb[:, half * 512:(half + 1) * 512], St0[:, half * 512:(half + 1) * 512]))
                wait("sync", "act", s_evac[it])
                emit("sync", lambda e, tb=tb, ns=ns, half=half, slane=slane: e.dma_start(
                    out=s_scr.ap()[tb * 128:(tb + 1) * 128, ns * 512:(ns + 1) * 512],
                    in_=sbf_sb[:, half * 512:(half + 1) * 512]).then_inc(sems[slane], 16))
                s_store[it] = (slane, winc(slane, 16))
        if PHASE_LIM >= 6:
            wait("gpsimd", "pe", wo_tile[6 * 8 + 7])
            emit("gpsimd", lambda e: e.dma_start(
                out=slab0_sb.ap().rearrange("p (c n) -> p c n", n=512),
                in_=qwt.rearrange("(c p) n -> p c n", p=128)[:, :, 0:512])
                .then_inc(sems["ga"], 16), dma=True)
            prefetch["ffn0"] = ("ga", winc("ga", 16))
        barrier()
        if DBG:
            emit("sync", lambda e: e.dma_start(out=dbg["s"][:, :], in_=s_scr.ap()[:, :])
                 .then_inc(sems["pre"], 16))
            winc("pre", 16)

    if PHASE_LIM >= 6:
        emit("sync", lambda e: e.dma_start(out=bias_sb[:, :], in_=bias_bc[:, :])
             .then_inc(sems["dc"], 16))
        bias_ld = winc("dc", 16)
        slabs = [slab0_sb, slab1_sb]
        gld_ffn = [None] * 8
        gld_ffn[0] = prefetch["ffn0"]
        emit("gpsimd", lambda e: e.dma_start(
            out=slab1_sb.ap().rearrange("p (c n) -> p c n", n=512),
            in_=qwt.rearrange("(c p) n -> p c n", p=128)[:, :, 512:1024])
            .then_inc(sems["gb"], 16), dma=True)
        gld_ffn[1] = ("gb", winc("gb", 16))
        y_tile = {}
        stt_done = {}
        silu_done = {}
        comb_done = {}
        o_store = {}
        tload = {}
        for ns in range(8):
            parn = ns % 2
            glane = "ga" if parn == 0 else "gb"
            if ns >= 2:
                wait("gpsimd", "pe", y_tile[(ns - 2) * 8 + 7])
                emit("gpsimd", lambda e, ns=ns, parn=parn, glane=glane: e.dma_start(
                    out=slabs[parn].ap().rearrange("p (c n) -> p c n", n=512),
                    in_=qwt.rearrange("(c p) n -> p c n", p=128)[:, :, ns * 512:(ns + 1) * 512])
                    .then_inc(sems[glane], 16), dma=True)
                gld_ffn[ns] = (glane, winc(glane, 16))
            for tb in range(8):
                it = ns * 8 + tb
                half = it % 2
                if it >= 2:
                    wait("sync", "dve", comb_done[it - 2])
                lane = "da" if half == 0 else "db"
                emit("sync", lambda e, tb=tb, ns=ns, half=half, lane=lane: e.dma_start(
                    out=stile_sb[:, half * 512:(half + 1) * 512],
                    in_=s_scr.ap()[tb * 128:(tb + 1) * 128, ns * 512:(ns + 1) * 512])
                    .then_inc(sems[lane], 16))
                winc(lane, 16)
                emit("sync", lambda e, tb=tb, ns=ns, half=half, lane=lane: e.dma_start(
                    out=xtile_sb[:, half * 512:(half + 1) * 512],
                    in_=x_in[tb * 128:(tb + 1) * 128, ns * 512:(ns + 1) * 512])
                    .then_inc(sems[lane], 16))
                tload[it] = (lane, winc(lane, 16))
                if tb == 0:
                    wait("tensor", *gld_ffn[ns])
                    if ns == 0:
                        wait("tensor", "dc", xn_reload)
                if it >= 2:
                    wait("tensor", "dve", stt_done[it - 2])
                for kc in range(32):
                    emit("tensor", lambda e, kc=kc, half=half, tb=tb, parn=parn: (
                        lambda i: i.then_inc(sems["pe"], 1) if kc == 31 else i)(
                        e.matmul(
                            St0[:, half * 512:(half + 1) * 512],
                            xn_sb[:, kc * TPC + tb * 128:kc * TPC + tb * 128 + 128],
                            slabs[parn][:, kc * 512:(kc + 1) * 512],
                            start=(kc == 0), stop=(kc == 31))))
                y_tile[it] = winc("pe")
                wait("vector", "pe", y_tile[it])
                if it == 0:
                    wait("vector", "dc", bias_ld)
                    
                if it >= 2:
                    wait("vector", "act", silu_done[it - 2])
                stt_done[it] = emit("vector", lambda e, half=half, ns=ns: e.scalar_tensor_tensor(
                    t1f_sb[:, half * 512:(half + 1) * 512], St0[:, half * 512:(half + 1) * 512],
                    swb_sb[:, 0:1], bias_sb[:, ns * 512:(ns + 1) * 512], ALU.mult, ALU.add))
                wait("scalar", "dve", stt_done[it])
                if it >= 2:
                    wait("scalar", "dve", comb_done[it - 2])
                silu_done[it] = emit("scalar", lambda e, half=half: e.activation(
                    gt_sb[:, half * 512:(half + 1) * 512], t1f_sb[:, half * 512:(half + 1) * 512],
                    AF.Silu, bias=cAP["zero"], scale=cAP["one"]))
                wait("vector", "act", silu_done[it])
                wait("vector", *tload[it])
                emit("vector", lambda e, half=half: e.tensor_mul(
                    t1f_sb[:, half * 512:(half + 1) * 512], gt_sb[:, half * 512:(half + 1) * 512],
                    stile_sb[:, half * 512:(half + 1) * 512]))
                if it >= 2:
                    wait("vector", *o_store[it - 2])
                comb_done[it] = emit("vector", lambda e, half=half: e.tensor_add(
                    otile_sb[:, half * 512:(half + 1) * 512],
                    t1f_sb[:, half * 512:(half + 1) * 512],
                    xtile_sb[:, half * 512:(half + 1) * 512]))
                slane = "sa" if half == 0 else "sb"
                wait("sync", "dve", comb_done[it])
                emit("sync", lambda e, tb=tb, ns=ns, half=half, slane=slane: e.dma_start(
                    out=out[tb * 128:(tb + 1) * 128, ns * 512:(ns + 1) * 512],
                    in_=otile_sb[:, half * 512:(half + 1) * 512]).then_inc(sems[slane], 16))
                o_store[it] = (slane, winc(slane, 16))
    for nm in ("sa", "sb", "sc", "sd", "pre", "dc"):
        wait("sync", nm, cnt[nm])

    with nc.Block() as block:
        @block.tensor
        def _(e):
            for fn in ops["tensor"]:
                fn(e)

        @block.scalar
        def _(e):
            for fn in ops["scalar"]:
                fn(e)

        @block.vector
        def _(e):
            for fn in ops["vector"]:
                fn(e)

        @block.gpsimd
        def _(e):
            for fn in ops["gpsimd"]:
                fn(e)

        @block.sync
        def _(e):
            for fn in ops["sync"]:
                fn(e)

    stack.close()
    return nc


_NC_CACHE = None


def get_nc():
    global _NC_CACHE
    if _NC_CACHE is None:
        _NC_CACHE = build_nc()
    return _NC_CACHE


def make_in_maps(inputs):
    x = np.asarray(inputs["L0_attn_resid"])
    wq = np.ascontiguousarray(np.asarray(inputs["L0_attn_Wq"], dtype=np.float32))
    wk = np.ascontiguousarray(np.asarray(inputs["L0_attn_Wk"], dtype=np.float32))
    wv = np.ascontiguousarray(np.asarray(inputs["L0_attn_Wv"], dtype=np.float32))
    wo = np.ascontiguousarray(np.asarray(inputs["L0_attn_Wo"], dtype=np.float32))
    qw = np.asarray(inputs["L0_ffn_qw"])
    sw = np.asarray(inputs["L0_ffn_sw"], dtype=np.float32)
    bias = np.asarray(inputs["L0_ffn_bias"], dtype=np.float32)

    qwt = np.ascontiguousarray(qw.T.astype(np.int32))
    bias_bc = np.ascontiguousarray(np.broadcast_to(bias[None, :], (128, DM)).astype(np.float32))
    sw_bc = np.full((128, 1), float(sw), np.float32)
    cst, rot, idm = host_consts()

    xf = np.asarray(x, dtype=np.float32).reshape(T, DM)
    in_maps = []
    orders = []
    for core in range(NCORES):
        tabs, pos = host_tables(core)
        orders.append(pos)
        shard = np.ascontiguousarray(xf[pos])
        in_maps.append(dict(
            x=shard, wq=wq, wk=wk, wv=wv, wo=wo, qwt=qwt,
            bias_bc=bias_bc, sw_bc=sw_bc,
            cosq=tabs["cosq"], sinq=tabs["sinq"], cosk=tabs["cosk"], sink=tabs["sink"],
            cst=cst, ctrlb=np.ascontiguousarray(tabs["ctrlb"]),
            dmask=np.ascontiguousarray(tabs["dmask"]), rot=rot, idm=idm,
        ))
    return in_maps, orders


def kernel(**inputs):
    nc = get_nc()
    in_maps, orders = make_in_maps(inputs)
    res = run_bass_kernel_spmd(nc, in_maps, core_ids=list(range(NCORES)))
    full = np.empty((T, DM), np.float32)
    for core in range(NCORES):
        full[orders[core]] = np.asarray(res.results[core]["out"], dtype=np.float32)
    return full.reshape(1, T, DM)
